# version 02 (final)
# baseline (speedup 1.0000x reference)
import numpy as np
import concourse.bass as bass
import concourse.mybir as mybir
from dataclasses import dataclass, field

F32 = mybir.dt.float32
BF16 = mybir.dt.bfloat16
I32 = mybir.dt.int32
U32 = mybir.dt.uint32
AL = mybir.AluOpType
AF = mybir.ActivationFunctionType
AX = mybir.AxisListType


@dataclass
class Cfg:
    M: int = 1024
    D: int = 2048
    S: int = 16
    R: int = 64
    T: int = 8192
    NB: int = 2
    NL: int = 4
    DCONV: int = 4
    EPS: float = 1e-5
    TMAC: int = 256
    TCH: int = 512
    SOCT: int = 4
    E: int = 8
    F: int = 4096
    MCH: int = 128
    MOE_SLACK: int = 7
    dbg: bool = False

    @property
    def Dh(self):
        return self.D // 2

    @property
    def GH(self):
        return self.Dh // 128

    @property
    def MT(self):
        return self.M // 128

    @property
    def CAP(self):
        return max((self.NB * self.T) // self.E, 4)

    @property
    def NMAC(self):
        return self.T // self.TMAC

    @property
    def NTC(self):
        return self.T // self.TCH

    @property
    def NOCT(self):
        return self.S // self.SOCT

    @property
    def TOK_OWN(self):
        return self.T // 2

    @property
    def NCHUNK(self):
        return self.TOK_OWN // self.MCH + self.MOE_SLACK


ENGS = ("PE", "ACT", "DVE", "POOL", "SP")
QUEUES = {"QSP": "SP", "QACT": "ACT", "QPOOL": "POOL", "QIND": "POOL"}
NQSEM = 8


class Buf:
    __slots__ = ("name", "writer", "readers")

    def __init__(self, name):
        self.name = name
        self.writer = None
        self.readers = []


class K:
    def __init__(self, nc: bass.Bass, cfg: Cfg):
        self.nc = nc
        self.cfg = cfg
        self.ops = {e: [] for e in ENGS}
        self.sem_n = {e: 0 for e in ENGS}
        self.quse = {q: [0] * NQSEM for q in QUEUES}
        self.qrr = {q: 0 for q in QUEUES}
        self.bufs = {}
        self._sems = {}
        self.wm = {}
        self._anon = 0

    def buf(self, name):
        if name not in self.bufs:
            self.bufs[name] = Buf(name)
        return self.bufs[name]

    def _dep_waits(self, waiter_key, reads, writes):
        need = {}
        def add(dep):
            if dep is None:
                return
            sk, val = dep
            if need.get(sk, 0) < val:
                need[sk] = val
        for r in reads:
            b = self.buf(r)
            if b.writer:
                add(b.writer)
        for w in writes:
            b = self.buf(w)
            if b.writer:
                add(b.writer)
            for rd in b.readers:
                add(rd)
        out = []
        for sk, val in need.items():
            key = (waiter_key, sk)
            if self.wm.get(key, -1) >= val:
                continue
            out.append((sk, val))
            self.wm[key] = val
        return out

    def _record(self, stream, fn, reads, writes, inc, tag):
        if stream in QUEUES:
            eng_stream = QUEUES[stream]
            idx = self.qrr[stream]
            self.qrr[stream] = (idx + 1) % NQSEM
            sem_key = (stream, idx)
            prev_uses = self.quse[stream][idx]
            waits = self._dep_waits(eng_stream + "#dma", reads, writes)
            if prev_uses > 0:
                key = (eng_stream + "#dma", sem_key)
                if self.wm.get(key, -1) < 16 * prev_uses:
                    waits.append((sem_key, 16 * prev_uses))
                    self.wm[key] = 16 * prev_uses
            self.quse[stream][idx] = prev_uses + 1
            my = (sem_key, 16 * (prev_uses + 1))
            rec = dict(stream=stream, sem_key=sem_key, fn=fn, waits=waits,
                       inc=True, tag=tag)
            self.ops[eng_stream].append(rec)
        else:
            waits = self._dep_waits(stream, reads, writes)
            if inc:
                self.sem_n[stream] += 1
            seq = self.sem_n[stream]
            my = (stream, seq)
            rec = dict(stream=stream, sem_key=stream, fn=fn, waits=waits,
                       inc=inc, tag=tag)
            self.ops[stream].append(rec)
        for w in writes:
            b = self.buf(w)
            b.writer = my
            b.readers = []
        for r in reads:
            self.buf(r).readers.append(my)
        return rec

    def op(self, eng, fn, reads=(), writes=(), inc=True, tag=""):
        assert eng in ENGS
        return self._record(eng, fn, list(reads), list(writes), inc, tag)

    def dma(self, queue, out_ap_fn, reads=(), writes=(), tag=""):
        assert queue in QUEUES
        return self._record(queue, out_ap_fn, list(reads), list(writes), True, tag)

    def barrier_read(self, eng, reads, tag="bar"):
        self._record(eng, None, list(reads), [], False, tag)

    def build(self, block):
        nc = self.nc
        sems = {}
        for s in ENGS:
            sems[s] = nc.semaphore(f"sem_{s}").__enter__()
        for q in QUEUES:
            for i in range(NQSEM):
                sems[(q, i)] = nc.semaphore(f"sem_{q}{i}").__enter__()
        self._sems = sems
        eng_map = {
            "PE": ("tensor", nc.tensor),
            "ACT": ("scalar", nc.scalar),
            "DVE": ("vector", nc.vector),
            "POOL": ("gpsimd", nc.gpsimd),
            "SP": ("sync", nc.sync),
        }
        for ename, (attr, handle) in eng_map.items():
            oplist = self.ops[ename]
            if not oplist:
                continue
            def body(e, oplist=oplist, ename=ename):
                for rec in oplist:
                    for sk, val in rec["waits"]:
                        e.wait_ge(sems[sk], val)
                    if rec["fn"] is None:
                        continue
                    try:
                        ins = rec["fn"](e)
                    except Exception:
                        import traceback
                        print("FAILING OP tag:", rec.get("tag"))
                        traceback.print_exc()
                        raise
                    if rec["inc"]:
                        amt = 16 if rec["stream"] in QUEUES else 1
                        if ins is not None:
                            ins.then_inc(sems[rec["sem_key"]], amt)
                        else:
                            e.nop().then_inc(sems[rec["sem_key"]], amt)
            getattr(block, attr)(body)
        return sems


def join(self):
    targets = [(e, self.sem_n[e]) for e in ENGS if self.sem_n[e] > 0]
    for q in QUEUES:
        for i in range(NQSEM):
            u = self.quse[q][i]
            if u > 0:
                targets.append(((q, i), 16 * u))
    for e in ENGS:
        waits = []
        for sk, val in targets:
            if sk == e:
                continue
            key = (e, sk)
            if self.wm.get(key, -1) >= val:
                continue
            self.wm[key] = val
            waits.append((sk, val))
        key2 = e + "#dma"
        for sk, val in targets:
            if self.wm.get((key2, sk), -1) < val:
                self.wm[(key2, sk)] = val
        if waits:
            self.ops[e].append(dict(stream=e, sem_key=e, fn=None,
                                    waits=waits, inc=False, tag="join"))
    for b in self.bufs.values():
        b.writer = None
        b.readers = []

K.join = join


from concourse.bass_core_rust import IndirectOffsetOnAxis
import ml_dtypes
"""Full per-core SPMD program for mk_recurrent_d1 (island design).

One bass program runs on all 8 cores; per-core behavior differs only through
input values (weight slices, meta scalars, shared-slot id, nonce).
Cores {0,1}=batch0 island, {2,3}=batch1 island, {4..7} duplicates (ignored).
Within an island: core j owns d_inner half j for mamba (full sequence), and
token half j of the batch for MoE + final combine.
"""

BIG = float(1 << 22)


def build_core(c: Cfg, debug=False, detect_races=True):
    nc = bass.Bass("TRN2", target_bir_lowering=False, debug=debug,
                   detect_race_conditions=detect_races)
    NTOK = c.NB * c.T
    Dh, GH, MT, TM, NM = c.Dh, c.GH, c.MT, c.TMAC, c.NMAC
    TCH, SOCT, NOCT = c.TCH, c.SOCT, c.NOCT
    NTC = c.T // c.TCH
    XW = c.R + 2 * c.S
    S = c.S
    NTIL = NTOK // 128
    NMSG0 = c.NL * (c.T // c.TMAC) * 2 + 8
    NOWN = c.TOK_OWN // 128
    NCH = c.NCHUNK
    FB = c.F // 512
    FC = c.F // 128
    CAP = c.CAP
    RCH = min(2048, NTOK // 4)

    dram = lambda n, sh, dt, **kw: nc.dram_tensor(n, sh, dt, **kw).ap()
    x_full = dram("x_full", [NTOK, c.M], F32, kind="ExternalInput")
    Wi_sl = dram("Wi_sl", [c.NL, 2 * Dh, c.M], F32, kind="ExternalInput")
    convw = dram("convw", [c.NL, Dh, c.DCONV], F32, kind="ExternalInput")
    convb = dram("convb", [c.NL, Dh], F32, kind="ExternalInput")
    Wx_sl = dram("Wx_sl", [c.NL, XW, Dh], F32, kind="ExternalInput")
    Wdt_sl = dram("Wdt_sl", [c.NL, Dh, c.R], F32, kind="ExternalInput")
    bdt_sl = dram("bdt_sl", [c.NL, Dh], F32, kind="ExternalInput")
    Alog_sl = dram("Alog_sl", [c.NL, Dh, S], F32, kind="ExternalInput")
    Dp_sl = dram("Dp_sl", [c.NL, Dh], F32, kind="ExternalInput")
    Wo_sl = dram("Wo_sl", [c.NL, c.M, Dh], F32, kind="ExternalInput")
    lnw_i = dram("lnw_i", [c.NL, c.M], F32, kind="ExternalInput")
    lnb_i = dram("lnb_i", [c.NL, c.M], F32, kind="ExternalInput")
    wg_i = dram("wg_i", [c.M, c.E], F32, kind="ExternalInput")
    w1_i = dram("w1_i", [c.E, c.M, c.F], F32, kind="ExternalInput")
    b1_i = dram("b1_i", [c.E, c.F], F32, kind="ExternalInput")
    w2_i = dram("w2_i", [c.E, c.F, c.M], F32, kind="ExternalInput")
    b2_i = dram("b2_i", [c.E, c.M], F32, kind="ExternalInput")
    ident_f = dram("ident_f", [128, 128], F32, kind="ExternalInput")
    ident_b = dram("ident_b", [128, 128], BF16, kind="ExternalInput")
    iotaNCH_i = dram("iotaNCH_i", [NCH, 1], F32, kind="ExternalInput")
    selmat = dram("selmat", [16, c.S * 128], F32, kind="ExternalInput")
    iota_r = dram("iota_r", [1, NTOK], F32, kind="ExternalInput")
    iota_c = dram("iota_c", [128, 1], F32, kind="ExternalInput")
    iota8_r = dram("iota8_r", [1, 8], F32, kind="ExternalInput")
    k128_r = dram("k128_r", [1, NCH + 1], F32, kind="ExternalInput")
    trie_i = dram("trie_i", [8, 8], F32, kind="ExternalInput")
    ones8 = dram("ones8", [1, 8], F32, kind="ExternalInput")
    metaf_i = dram("metaf_i", [1, 16], F32, kind="ExternalInput")
    metai_i = dram("metai_i", [1, 16], I32, kind="ExternalInput")
    nonce_i = dram("nonce_i", [1, 16], U32, kind="ExternalInput")
    xsend_i = dram("xsend_i", [128, c.NL * NM], I32, kind="ExternalInput")
    wsend_i = dram("wsend_i", [128, c.NL * NM * MT], I32, kind="ExternalInput")
    fsend_i = dram("fsend_i", [2, NMSG0], I32, kind="ExternalInput")
    fidx_i = dram("fidx_i", [128, c.TOK_OWN // 128], I32, kind="ExternalInput")
    selmsk_i = dram("selmsk_i", [1, 16], F32, kind="ExternalInput")

    out_own = dram("out_own", [c.TOK_OWN, c.M], F32, kind="ExternalOutput")

    xT_d = dram("xT_d", [c.M, NTOK], F32, kind="Internal")
    xs_d = dram("xs_d", [c.M, c.T], F32, kind="Internal")
    u_d = dram("u_d", [Dh, c.T], F32, kind="Internal")
    bc_d = dram("bc_d", [2 * S, c.T], F32, kind="Internal")
    dt_d = dram("dt_d", [Dh, c.T], F32, kind="Internal")
    sz_d = dram("sz_d", [Dh, c.T], F32, kind="Internal")
    y_d = dram("y_d", [Dh, c.T], F32, kind="Internal")
    mu_d = dram("mu_d", [NTIL, 128], F32, kind="Internal")
    rstd_d = dram("rstd_d", [NTIL, 128], F32, kind="Internal")
    g1c_d = dram("g1c_d", [NTOK, 1], F32, kind="Internal")
    keptc_d = dram("keptc_d", [NTOK, 1], F32, kind="Internal")
    idxr_d = dram("idxr_d", [NTIL, 128], F32, kind="Internal")
    rankc_d = dram("rankc_d", [NTOK, 1], F32, kind="Internal")
    idxc_d = dram("idxc_d", [NTOK, 1], F32, kind="Internal")
    cnts4_d = dram("cnts4_d", [8, 4], F32, kind="Internal")
    toklist_d = dram("toklist_d", [c.E * c.TOK_OWN, 1], I32, kind="Internal")
    chtab_d = dram("chtab_d", [NCH, 2], F32, kind="Internal")
    gbuf_d = dram("gbuf_d", [c.TOK_OWN, c.M], F32, kind="Internal")
    w1bf_d = dram("w1bf_d", [c.E, c.M, c.F], BF16, kind="Internal")
    h1_d = dram("h1_d", [NCH * 128, c.F], BF16, kind="Internal")
    w2bf_d = dram("w2bf_d", [c.E, c.F, c.M], BF16, kind="Internal")
    b1col_d = dram("b1col_d", [c.E, 128, FC], F32, kind="Internal")
    b2col_d = dram("b2col_d", [c.E, 128, MT], F32, kind="Internal")

    dbg = {}
    if c.dbg:
        for n, sh in [("dbg_u", [Dh, c.T]), ("dbg_dt", [Dh, c.T]),
                      ("dbg_sz", [Dh, c.T]), ("dbg_y", [Dh, c.T]),
                      ("dbg_x1", [c.M, c.T]), ("dbg_BC", [2 * S, c.T]),
                      ("dbg_route", [8, NTOK]), ("dbg_g", [c.TOK_OWN, c.M]),
                      ("dbg_cht", [NCH, 2])]:
            dbg[n] = dram(n, sh, F32, kind="ExternalOutput")

    shp_x = dram("shp_x", [2 * c.NL * NM * 128, TM], F32, kind="Internal",
                 addr_space="Shared")
    shp_w = dram("shp_w", [2 * c.NL * NM * c.M, TM], F32, kind="Internal",
                 addr_space="Shared")
    NMSG = c.NL * NM * 2 + 8
    shf = dram("shf", [2 * NMSG, 16], U32, kind="Internal", addr_space="Shared")

    _backer = nc.alloc_sbuf_tensor("arena_backer", [128, 1], F32)
    SB_BASE = nc.lookup_mloc(_backer).addr
    SB_BUDGET = 192 * 1024 - SB_BASE - 4096
    nc.alloc_sbuf_tensor("arena_backer2", [128, (SB_BUDGET - 32) // 4], F32)
    sb_off = [0]

    def sbuf(name, sh, dt, at=None):
        per = int(np.prod(sh[1:])) * (2 if dt in (BF16,) else 4)
        if dt in (I32, U32, F32):
            per = int(np.prod(sh[1:])) * 4
        if at is None:
            at = sb_off[0]
            sb_off[0] = at + ((per + 31) // 32) * 32
        assert at + per <= SB_BUDGET, f"SBUF overflow at {name}"
        h = nc.alloc_sbuf_tensor_at(name, sh, dt, offset=SB_BASE + at)
        return h.ap()

    ident_fs = sbuf("ident_fs", [128, 128], F32)
    metaf_s = sbuf("metaf_s", [1, 16], F32)
    metai_s = sbuf("metai_s", [1, 16], I32)
    nonce_s = sbuf("nonce_s", [1, 16], U32)
    iota_cs = sbuf("iota_cs", [128, 1], F32)
    iota8_s = sbuf("iota8_s", [1, 8], F32)
    k128_s = sbuf("k128_s", [1, NCH + 1], F32)
    trie_s = sbuf("trie_s", [8, 8], F32)
    ones8_s = sbuf("ones8_s", [1, 8], F32)
    onescol = sbuf("onescol", [128, 1], F32)
    carry = sbuf("carry", [128, GH, S], F32)
    flagt = sbuf("flagt", [1, 16], U32)
    tok0b = sbuf("tok0b", [128, 1], F32)
    A_s = sbuf("A_s", [128, GH, S], F32)
    cb_s = sbuf("cb_s", [128, GH], F32)
    bdt_s = sbuf("bdt_s", [128, GH], F32)
    Dp_s = sbuf("Dp_s", [128, GH], F32)
    lnw_s = sbuf("lnw_s", [128, MT], F32)
    lnb_s = sbuf("lnb_s", [128, MT], F32)
    WxT = sbuf("WxT", [128, GH, XW], BF16)
    WdtT = sbuf("WdtT", [c.R, Dh], BF16)
    wgs = sbuf("wgs", [128, MT, 8], F32)
    csr = sbuf("csr", [8, 1], F32)
    ident_bs = sbuf("ident_bs", [128, 128], BF16)
    iotaNCH_s = sbuf("iotaNCH_s", [NCH, 1], F32)
    cntc = sbuf("cntc", [8, 1], F32)
    ccumc = sbuf("ccumc", [8, 1], F32)
    ccrow = sbuf("ccrow", [1, 8], F32)
    ccrep = sbuf("ccrep", [NCH, 8], F32)
    cmpc = sbuf("cmpc", [NCH, 8], F32)
    cci = sbuf("cci", [8, 1], I32)
    wld_f = sbuf("wld_f", [128, max(c.M, Dh, 512)], F32)
    wld_b = sbuf("wld_b", [128, max(c.M, Dh, 512)], BF16)
    xsend_s = sbuf("xsend_s", [128, c.NL * NM], I32)
    wsend_s = sbuf("wsend_s", [128, c.NL * NM * MT], I32)
    fsend_s = sbuf("fsend_s", [2, NMSG0], I32)
    nonce2 = sbuf("nonce2", [2, 16], U32)
    fidx_s = sbuf("fidx_s", [128, c.TOK_OWN // 128], I32)
    selmsk_s = sbuf("selmsk_s", [1, 16], F32)
    srtab = sbuf("srtab", [128, NCH], I32)
    cnt8 = sbuf("cnt8", [8, NCH + 1], F32)
    colsc = sbuf("colsc", [128, 16], F32)
    ch_sb = sbuf("ch_sb", [NCH, 4], F32)
    chi_sb = sbuf("chi_sb", [NCH, 4], I32)
    cci2 = sbuf("cci2", [8, 1], I32)
    PERSIST_END = sb_off[0] + 2048

    a32 = lambda x: (int(x) + 31) // 32 * 32
    f4 = lambda *d: a32(int(np.prod(d)) * 4)

    W_BASE = PERSIST_END
    o = 0
    WiT = sbuf("WiT", [128, MT, 2 * Dh], BF16, at=W_BASE + o)
    o += a32(MT * 2 * Dh * 2)
    WoT = sbuf("WoT", [128, GH, c.M], BF16, at=W_BASE + o)
    o += a32(GH * c.M * 2)
    sel_s = sbuf("sel_s", [16, c.S * 128], F32, at=W_BASE + o)
    o += f4(c.S * 128)
    dconv_s = sbuf("dconv_s", [128, GH * c.DCONV, 128], BF16, at=W_BASE + o)
    o += a32(GH * c.DCONV * 128 * 2)
    WM_END = o
    w1e = sbuf("w1e", [128, MT, c.F], BF16, at=W_BASE)
    w2e = sbuf("w2e", [128, FC, c.M], BF16, at=W_BASE)
    WMOE_END = max(a32(MT * c.F * 2), a32(FC * c.M * 2))

    o = 0
    P0 = W_BASE
    ptile = sbuf("ptile", [128, c.M], F32, at=P0 + o); o += f4(c.M)
    psq = sbuf("psq", [128, c.M], F32, at=P0 + o); o += f4(c.M)
    pxT = sbuf("pxT", [128, c.M], F32, at=P0 + o); o += f4(c.M)
    mu_all = sbuf("mu_all", [128, NTIL], F32, at=P0 + o); o += f4(NTIL)
    sx2_all = sbuf("sx2_all", [128, NTIL], F32, at=P0 + o); o += f4(NTIL)
    rstd_all = sbuf("rstd_all", [128, NTIL], F32, at=P0 + o); o += f4(NTIL)
    lgw = sbuf("lgw", [128, NTIL, 8], F32, at=P0 + o); o += f4(NTIL, 8)
    lg8 = sbuf("lg8", [8, 128], F32, at=P0 + o); o += f4(128)
    scr1 = sbuf("scr1", [128, NTIL, 8], F32, at=P0 + o); o += f4(NTIL, 8)
    scr2 = sbuf("scr2", [128, NTIL], F32, at=P0 + o); o += f4(NTIL)
    scr3 = sbuf("scr3", [128, NTIL], F32, at=P0 + o); o += f4(NTIL)
    tr_s = sbuf("tr_s", [128, 128], F32, at=P0 + o); o += f4(128)
    mask1T = sbuf("mask1T", [8, RCH], F32, at=P0 + o); o += f4(RCH)
    locT = sbuf("locT", [8, RCH], F32, at=P0 + o); o += f4(RCH)
    rowA = sbuf("rowA", [1, RCH], F32, at=P0 + o); o += f4(RCH)
    rowB = sbuf("rowB", [1, RCH], F32, at=P0 + o); o += f4(RCH)
    carry8 = sbuf("carry8", [8, 1], F32, at=P0 + o); o += 32
    rank8 = sbuf("rank8", [8, 1], F32, at=P0 + o); o += 32
    P_END = o

    if debug or True:
        print(f"[layout] SB_BUDGET={SB_BUDGET} PERSIST={PERSIST_END} "
              f"WM={WM_END} WMOE={WMOE_END} P={P_END}")
    AR = W_BASE + max(WM_END, P_END)

    def arena(name, sh, dt, off):
        return sbuf(name, sh, dt, at=AR + off)

    o = 0
    xmac = arena("xmac", [128, MT, TM], F32, o); o += f4(MT, TM)
    sq_a = arena("sq_a", [128, MT, TM], F32, o)
    sp1 = sq_a
    o += f4(MT, TM)
    xnmac = arena("xnmac", [128, MT, TM], BF16, o); o += a32(MT * TM * 2)
    xs_sb = []
    for p in range(2):
        xs_sb.append(arena(f"xs_sb{p}", [128, GH, c.DCONV - 1 + TM], BF16, o))
        o += a32(GH * (c.DCONV - 1 + TM) * 2)
    u_sb = arena("u_sb", [128, GH, TM], F32, o); o += f4(GH, TM)
    ub_sb = arena("ub_sb", [128, GH, TM], BF16, o); o += a32(GH * TM * 2)
    sz_sb = arena("sz_sb", [128, GH, TM], F32, o); o += f4(GH, TM)
    dtm_sb = arena("dtm_sb", [128, GH, TM], F32, o); o += f4(GH, TM)
    sp2 = arena("sp2", [128, GH, TM], F32, o); o += f4(GH, TM)
    xd_mine = arena("xd_mine", [128, 2, TM], F32, o); o += f4(2, TM)
    xd_sum = arena("xd_sum", [128, TM], F32, o); o += f4(TM)
    xd_two = arena("xd_two", [128, 2, TM], F32, o); o += f4(2, TM)
    mrow = arena("mrow", [1, TM], F32, o); o += f4(TM)
    rrow = arena("rrow", [1, TM], F32, o); o += f4(TM)
    vrow = arena("vrow", [1, TM], F32, o); o += f4(TM)
    A_END = o
    o = 0
    brep = arena("brep", [128, SOCT * TCH], F32, o); o += f4(SOCT * TCH)
    crep = arena("crep", [128, SOCT * TCH], F32, o); o += f4(SOCT * TCH)
    dAc = arena("dAc", [128, SOCT * TCH], F32, o); o += f4(SOCT * TCH)
    Xc = arena("Xc", [128, SOCT * TCH], F32, o); o += f4(SOCT * TCH)
    dtg = arena("dtg", [128, GH, TCH], F32, o); o += f4(GH, TCH)
    ug = arena("ug", [128, GH, TCH], F32, o); o += f4(GH, TCH)
    yacc = arena("yacc", [128, GH, TCH], F32, o); o += f4(GH, TCH)
    ytm = arena("ytm", [128, TCH], F32, o); o += f4(TCH)
    cfix = arena("cfix", [128, SOCT], F32, o); o += f4(SOCT)
    b_sb = arena("b_sb", [S, TCH], F32, o); o += f4(TCH)
    c_sb = arena("c_sb", [S, TCH], F32, o); o += f4(TCH)
    B_END = o
    o = 0
    ymac = arena("ymac", [128, GH, TM], F32, o); o += f4(GH, TM)
    uc_sb = arena("uc_sb", [128, GH, TM], F32, o); o += f4(GH, TM)
    szc_sb = arena("szc_sb", [128, GH, TM], F32, o); o += f4(GH, TM)
    ybf = arena("ybf", [128, GH, TM], BF16, o); o += a32(GH * TM * 2)
    womine = arena("womine", [128, MT, 2, TM], F32, o); o += f4(MT, 2, TM)
    wopart = arena("wopart", [128, MT, TM], F32, o); o += f4(MT, TM)
    xcur = arena("xcur", [128, MT, 2, TM], F32, o); o += f4(MT, 2, TM)
    wop2 = arena("wop2", [128, MT, TM], F32, o); o += f4(MT, TM)
    C_END = o

    AR2 = W_BASE + WMOE_END

    def arena2(name, sh, dt, off):
        return sbuf(name, sh, dt, at=AR2 + off)

    o = 0
    xg = arena2("xg", [128, c.M], F32, o); o += f4(c.M)
    xgsq = arena2("xgsq", [128, c.M], F32, o); o += f4(c.M)
    xnb = arena2("xnb", [128, c.M], BF16, o); o += a32(c.M * 2)
    xnT = arena2("xnT", [128, MT, 128], BF16, o); o += a32(MT * 128 * 2)
    h1b = arena2("h1b", [128, FC, 128], BF16, o); o += a32(FC * 128 * 2)
    g1col = arena2("g1col", [128, 1], F32, o); o += 32
    keptcol = arena2("keptcol", [128, 1], F32, o); o += 32
    tloc = arena2("tloc", [128, 1], F32, o); o += 32
    scat_i = arena2("scat_i", [128, 1], I32, o); o += 32
    tok_i = arena2("tok_i", [128, 1], I32, o); o += 32
    tokf = arena2("tokf", [128, 1], F32, o); o += 32
    eo_sb = arena2("eo_sb", [128, MT, 128], F32, o); o += f4(MT, 128)
    eot = arena2("eot", [128, c.M], F32, o); o += f4(c.M)
    gsc1 = arena2("gsc1", [128, 512], F32, o); o += f4(512)
    gsc2 = arena2("gsc2", [128, 512], F32, o); o += f4(512)
    gsc3 = arena2("gsc3", [128, 512], F32, o); o += f4(512)
    b1c = arena2("b1c", [128, FC], F32, o); o += f4(FC)
    b2c = arena2("b2c", [128, MT], F32, o); o += f4(MT)
    zero_t = arena2("zero_t", [128, 512], F32, o); o += f4(512)
    bigi = arena2("bigi", [128, 512], I32, o); o += f4(512)
    M_END = o
    o = 0
    fs_sb = arena2("fs_sb", [128, MT, 128], F32, o); o += f4(MT, 128)
    fsT = arena2("fsT", [128, c.M], F32, o); o += f4(c.M)
    fx = arena2("fx", [128, c.M], F32, o); o += f4(c.M)
    fg = arena2("fg", [128, c.M], F32, o); o += f4(c.M)
    fsl2 = arena2("fsl2", [128, MT, 128], F32, o); o += f4(MT, 128)
    F_END = o

    need = max(AR + max(A_END, B_END, C_END), AR2 + max(M_END, F_END),
               W_BASE + max(WM_END, WMOE_END, P_END))
    assert need < SB_BUDGET, (
        f"SBUF overflow: {need}B/part (persist {PERSIST_END}, WM {WM_END}, "
        f"WMOE {WMOE_END}, P {P_END}, A {A_END}, B {B_END}, C {C_END}, "
        f"M {M_END}, F {F_END})")
    sb_off[0] = PERSIST_END - 2048

    k = K(nc, c)
    PS = [nc.alloc_psum_tensor(f"ps{i}", [128, 512], F32).ap() for i in range(6)]
    PBF = [nc.alloc_psum_tensor(f"pbf{i}", [128, 1024], BF16).ap() for i in range(2)]
    psn = [f"ps{i}" for i in range(6)]
    pbn = [f"pbf{i}" for i in range(2)]

    sem_poll = nc.semaphore("sem_poll").__enter__()
    sem_w = nc.semaphore("sem_w").__enter__()
    sem_ch = nc.semaphore("sem_ch").__enter__()

    dyn = {}
    bndreg = {}

    def bnd(v):
        if v not in bndreg:
            bndreg[v] = nc.gpsimd.alloc_register(f"bnd{v}")
        return bndreg[v]

    def init_bnds(e):
        for v, r in bndreg.items():
            e.reg_mov(r, v)
        return None

    for _v in (NTOK, c.TOK_OWN, c.E * c.TOK_OWN, 2 * NMSG0,
               2 * c.NL * (c.T // c.TMAC) * 128,
               2 * c.NL * (c.T // c.TMAC) * c.M):
        bnd(_v)

    def stash(engine_name, eng_enum, names_aps):
        def fn(e):
            for nm, ap, lo, hi in names_aps:
                dyn[(engine_name, nm)] = nc.values_load(
                    ap, engines=[eng_enum],
                    min_val=lo, max_val=hi, skip_runtime_bounds_check=True)
            return None
        return fn

    ET = mybir.EngineType

    k.dma("QSP", lambda e: e.dma_start(out=ident_fs, in_=ident_f), writes=["ident"])
    k.dma("QSP", lambda e: e.dma_start(out=ident_bs, in_=ident_b), writes=["identb"])
    k.dma("QSP", lambda e: e.dma_start(out=metaf_s, in_=metaf_i), writes=["metaf"])
    k.dma("QSP", lambda e: e.dma_start(out=metai_s, in_=metai_i), writes=["metai"])
    k.dma("QSP", lambda e: e.dma_start(out=nonce_s, in_=nonce_i), writes=["nonce"])
    k.dma("QSP", lambda e: e.dma_start(out=g0 if False else nonce2[0:1, :],
                                       in_=nonce_i), writes=["nonce2"])
    k.dma("QSP", lambda e: e.dma_start(out=nonce2[1:2, :], in_=nonce_i),
          writes=["nonce2"])
    k.dma("QSP", lambda e: e.dma_start(out=iota_cs, in_=iota_c), writes=["iotac"])
    k.dma("QSP", lambda e: e.dma_start(out=iota8_s, in_=iota8_r), writes=["iota8"])
    k.dma("QSP", lambda e: e.dma_start(out=k128_s, in_=k128_r), writes=["k128"])
    k.dma("QSP", lambda e: e.dma_start(out=trie_s, in_=trie_i), writes=["trie"])
    k.dma("QSP", lambda e: e.dma_start(out=ones8_s, in_=ones8), writes=["ones8"])
    k.dma("QSP", lambda e: e.dma_start(
        out=wgs, in_=wg_i.rearrange("(mt p) e -> p mt e", p=128)), writes=["wgs"])
    k.dma("QSP", lambda e: e.dma_start(out=xsend_s, in_=xsend_i), writes=["xsend"])
    k.dma("QSP", lambda e: e.dma_start(out=wsend_s, in_=wsend_i), writes=["wsend"])
    k.dma("QSP", lambda e: e.dma_start(out=fsend_s, in_=fsend_i), writes=["fsend"])
    k.dma("QSP", lambda e: e.dma_start(out=fidx_s, in_=fidx_i), writes=["fidx"])
    k.dma("QSP", lambda e: e.dma_start(out=selmsk_s, in_=selmsk_i), writes=["selmsk"])
    k.op("POOL", init_bnds, writes=["bnds"], tag="initbnd")
    k.op("DVE", lambda e: e.memset(onescol, 1.0), writes=["onescol"])
    cstv = [float(c.EPS), 0.0, 1.0, -1.0, 0.5, -0.5, 0.044715,
            0.7978845608028654, -0.7978845608028654]
    csttile = sbuf("csttile", [128, len(cstv)], F32)
    for ci, cv in enumerate(cstv):
        k.op("DVE", lambda e, ci=ci, cv=cv: e.memset(
            csttile[:, ci:ci + 1], cv), writes=["cstt"])
        nc.const_aps.register(F32, cv, csttile[:, ci:ci + 1])
    k.join()
    k.op("DVE", lambda e: e.memset(zero_t, 0.0), writes=["zero_t"])
    k.op("DVE", lambda e: e.memset(carry, 0.0), writes=["carry"])

    k.op("SP", stash("SP", ET.SP, [
        ("quart", metai_s[0:1, 2:3], 0, 3),
    ]), reads=["metai"], tag="stash_sp")
    onesrow = sbuf("onesrow", [1, 128], F32)
    k.op("DVE", lambda e: e.memset(onesrow, 1.0), writes=["onesrow"])
    k.op("PE", lambda e: e.matmul(PS[0][:, 0:1], onesrow, metaf_s[0:1, 0:1],
                                  start=True, stop=True),
         reads=["onesrow", "metaf"], writes=["ps0"], tag="tok0b")
    k.op("ACT", lambda e: e.copy(tok0b, PS[0][:, 0:1]), reads=["ps0"],
         writes=["tok0b"])
    bfcol = sbuf("bfcol", [128, 1], F32)
    jfcol = sbuf("jfcol", [128, 1], F32)
    k.op("PE", lambda e: e.matmul(PS[0][:, 0:1], onesrow, selmsk_s[0:1, 0:1],
                                  start=True, stop=True),
         reads=["onesrow", "selmsk"], writes=["ps0"], tag="bfb")
    k.op("ACT", lambda e: e.copy(bfcol, PS[0][:, 0:1]), reads=["ps0"],
         writes=["bfcol"])
    k.op("PE", lambda e: e.matmul(PS[0][:, 0:1], onesrow, selmsk_s[0:1, 1:2],
                                  start=True, stop=True),
         reads=["onesrow", "selmsk"], writes=["ps0"], tag="jfb")
    k.op("ACT", lambda e: e.copy(jfcol, PS[0][:, 0:1]), reads=["ps0"],
         writes=["jfcol"])

    for tt in range(NTIL):
        pb = ["ps0", "ps1"][tt % 2]
        pst = PS[0] if tt % 2 == 0 else PS[1]
        k.dma("QSP", lambda e, tt=tt: e.dma_start(
            out=ptile, in_=x_full[tt * 128:(tt + 1) * 128, :]),
            writes=["ptile"], tag=f"xt{tt}")
        k.op("ACT", lambda e: e.activation(
            psq, ptile, AF.Square), reads=["ptile"], writes=["psq"])
        k.op("DVE", lambda e, tt=tt: e.tensor_reduce(
            sx2_all[:, tt:tt + 1].unsqueeze(2), psq.unsqueeze(1), AX.X, AL.add),
            reads=["psq"], writes=["sx2c"])
        k.op("DVE", lambda e, tt=tt: e.tensor_reduce(
            mu_all[:, tt:tt + 1].unsqueeze(2), ptile.unsqueeze(1), AX.X, AL.add),
            reads=["ptile"], writes=["muc"])
        for mt in range(MT):
            k.op("PE", lambda e, mt=mt, pst=pst: e.transpose(
                pst[:, 0:128], ptile[:, mt * 128:(mt + 1) * 128], ident_fs),
                reads=["ptile", "ident"], writes=[pb], tag=f"tr{tt}_{mt}")
            k.op("ACT", lambda e, mt=mt, pst=pst: e.copy(
                pxT[:, mt * 128:(mt + 1) * 128], pst[:, 0:128]),
                reads=[pb], writes=["pxT"])
        k.dma("QSP", lambda e, tt=tt: e.dma_start(
            out=xT_d[:, tt * 128:(tt + 1) * 128].rearrange(
                "(mt p) t -> p mt t", p=128),
            in_=pxT.rearrange("p (mt t) -> p mt t", t=128)),
            reads=["pxT"], writes=["xT_d"], tag=f"xts{tt}")
        def lgmm(e, tt=tt):
            last = None
            for mt in range(MT):
                last = e.matmul(PS[2][0:8, 0:128], wgs[:, mt, :],
                                pxT[:, mt * 128:(mt + 1) * 128],
                                start=(mt == 0), stop=(mt == MT - 1))
            return last
        k.op("PE", lgmm, reads=["pxT", "wgs"], writes=["ps2"], tag=f"lg{tt}")
        k.op("ACT", lambda e: e.copy(lg8, PS[2][0:8, 0:128]), reads=["ps2"],
             writes=["lg8"])
        k.op("PE", lambda e: e.transpose(PS[3][:, 0:8], lg8[0:8, :],
            ident_fs[0:8, 0:8]), reads=["lg8", "ident"],
            writes=["ps3"], tag=f"lgt{tt}")
        k.op("ACT", lambda e, tt=tt: e.copy(
            lgw[:, tt, :], PS[3][:, 0:8]), reads=["ps3"], writes=["lgw"])

    k.op("DVE", lambda e: e.tensor_scalar(mu_all, mu_all, 1.0 / c.M, 0.0,
                                          AL.mult, AL.add),
         reads=["muc"], writes=["mu_all"])
    k.op("DVE", lambda e: e.tensor_scalar(sx2_all, sx2_all, 1.0 / c.M, 0.0,
                                          AL.mult, AL.add),
         reads=["sx2c"], writes=["sx2_all"])
    k.op("DVE", lambda e: e.tensor_mul(scr2, mu_all, mu_all),
         reads=["mu_all"], writes=["scr2"])
    k.op("DVE", lambda e: e.tensor_sub(scr3, sx2_all, scr2),
         reads=["sx2_all", "scr2"], writes=["scr3"])
    k.op("ACT", lambda e: e.activation(scr2, scr3, AF.Ln, bias=float(c.EPS)),
         reads=["scr3"], writes=["scr2"])
    k.op("ACT", lambda e: e.activation(rstd_all, scr2, AF.Exp, scale=-0.5),
         reads=["scr2"], writes=["rstd_all"])
    for nm, src, dst in [("muT", mu_all, mu_d), ("rsT", rstd_all, rstd_d)]:
        k.op("PE", lambda e, src=src: e.transpose(
            PS[0][0:NTIL, 0:128], src, ident_fs),
             reads=["mu_all" if src is mu_all else "rstd_all", "ident"],
             writes=["ps0"], tag=nm)
        k.op("ACT", lambda e: e.copy(tr_s[0:NTIL, :], PS[0][0:NTIL, 0:128]),
             reads=["ps0"], writes=["tr_s"])
        k.dma("QSP", lambda e, dst=dst: e.dma_start(out=dst, in_=tr_s[0:NTIL, :]),
              reads=["tr_s"], writes=["mu_d" if dst is mu_d else "rstd_d"],
              tag=nm + "s")

    def csmm(e):
        last = None
        for mt in range(MT):
            last = e.matmul(PS[2][0:8, 0:1], wgs[:, mt, :], onescol,
                            start=(mt == 0), stop=(mt == MT - 1))
        return last
    k.op("PE", csmm, reads=["wgs", "onescol"], writes=["ps2"], tag="cs")
    k.op("ACT", lambda e: e.copy(csr, PS[2][0:8, 0:1]),
         reads=["ps2"], writes=["csr"])
    csrep = sbuf("csrep", [128, 8], F32)
    k.op("PE", lambda e: e.transpose(PS[3][0:1, 0:8], csr[0:8, 0:1], ident_fs[0:8, 0:8]),
         reads=["csr", "ident"], writes=["ps3"], tag="csT")
    csrow = sbuf("csrow", [1, 8], F32)
    k.op("ACT", lambda e: e.copy(csrow, PS[3][0:1, 0:8]), reads=["ps3"],
         writes=["csrow"])
    k.op("PE", lambda e: e.matmul(PS[2][:, 0:8], onesrow, csrow,
                                  start=True, stop=True),
         reads=["onesrow", "csrow"], writes=["ps2"], tag="csrep2")
    k.op("ACT", lambda e: e.copy(csrep, PS[2][:, 0:8]), reads=["ps2"],
         writes=["csrep"])

    mu_bc = mu_all.unsqueeze(2).broadcast_to([128, NTIL, 8])
    rs_bc = rstd_all.unsqueeze(2).broadcast_to([128, NTIL, 8])
    cs_bc = csrep.unsqueeze(1).broadcast_to([128, NTIL, 8])
    k.op("DVE", lambda e: e.tensor_mul(scr1, mu_bc, cs_bc),
         reads=["mu_all", "csrep"], writes=["scr1"])
    k.op("DVE", lambda e: e.tensor_sub(lgw, lgw, scr1),
         reads=["scr1"], writes=["lgw"])
    k.op("DVE", lambda e: e.tensor_mul(lgw, lgw, rs_bc),
         reads=["rstd_all"], writes=["lgw"])
    k.op("DVE", lambda e: e.tensor_reduce(scr2.unsqueeze(2),
                                          lgw.rearrange("p n e -> p n e"),
                                          AX.X, AL.max),
         reads=["lgw"], writes=["scr2"])
    mx_bc = scr2.unsqueeze(2).broadcast_to([128, NTIL, 8])
    k.op("DVE", lambda e: e.tensor_sub(scr1, lgw, mx_bc),
         reads=["lgw", "scr2"], writes=["scr1"])
    k.op("ACT", lambda e: e.activation(scr1, scr1, AF.Exp), writes=["scr1e"],
         reads=["scr1"])
    k.op("DVE", lambda e: e.tensor_reduce(scr3.unsqueeze(2), scr1, AX.X, AL.add),
         reads=["scr1e"], writes=["scr3"])
    k.op("DVE", lambda e: e.reciprocal(scr3, scr3), reads=["scr3"],
         writes=["g1a"])
    io8_bc = iota8_s
    io8rep = sbuf("io8rep", [128, 8], F32)
    k.op("PE", lambda e: e.matmul(PS[2][:, 0:8], onesrow, iota8_s,
                                  start=True, stop=True),
         reads=["onesrow", "iota8"], writes=["ps2"], tag="io8rep")
    k.op("ACT", lambda e: e.copy(io8rep, PS[2][:, 0:8]), reads=["ps2"],
         writes=["io8rep"])
    io8_bc3 = io8rep.unsqueeze(1).broadcast_to([128, NTIL, 8])
    k.op("DVE", lambda e: e.tensor_tensor(scr1, lgw, mx_bc, AL.is_ge),
         reads=["lgw", "scr2"], writes=["scr1"])
    k.op("DVE", lambda e: e.scalar_tensor_tensor(scr1, scr1, -BIG, io8_bc3,
                                                 AL.mult, AL.add),
         reads=["scr1", "io8rep"], writes=["scr1"])
    k.op("DVE", lambda e: e.tensor_reduce(scr2.unsqueeze(2), scr1, AX.X, AL.min),
         reads=["scr1"], writes=["scr2"])
    k.op("DVE", lambda e: e.tensor_scalar(scr2, scr2, 1.0, BIG, AL.mult, AL.add),
         reads=["scr2"], writes=["idxa"])

    for nm, srcbuf, src, dst, dshape in [
            ("idxT", "idxa", scr2, idxr_d, None),
            ("g1T", "g1a", scr3, g1c_d, None)]:
        k.op("PE", lambda e, src=src: e.transpose(
            PS[0][0:NTIL, 0:128], src, ident_fs),
             reads=[srcbuf, "ident"], writes=["ps0"], tag=nm)
        k.op("ACT", lambda e: e.copy(tr_s[0:NTIL, :], PS[0][0:NTIL, 0:128]),
             reads=["ps0"], writes=["tr_s"])
        if dst is idxr_d:
            k.dma("QSP", lambda e: e.dma_start(out=idxr_d, in_=tr_s[0:NTIL, :]),
                  reads=["tr_s"], writes=["idxr_d"], tag=nm + "s")
        else:
            k.dma("QSP", lambda e: e.dma_start(
                out=g1c_d.rearrange("(a b) o -> a (b o)", b=128),
                in_=tr_s[0:NTIL, :]),
                reads=["tr_s"], writes=["g1c_d"], tag=nm + "s")

    iotacol8 = sbuf("iotacol8", [8, 1], F32)
    k.dma("QSP", lambda e: e.dma_start(
        out=iotacol8, in_=iota8_r.rearrange("o e -> e o")),
        writes=["iotacol8"], tag="io8c")
    Q4 = NTOK // 4
    CW = min(512, RCH)
    idxflat = idxr_d.rearrange("a b -> (a b)").unsqueeze(0)
    for rc in range(NTOK // RCH):
        r0 = rc * RCH
        qstart = (r0 % Q4) == 0
        qend = ((r0 + RCH) % Q4) == 0
        k.dma("QSP", lambda e, r0=r0: e.dma_start(
            out=rowA, in_=idxflat[0:1, r0:r0 + RCH]),
            reads=["idxr_d"], writes=["rowA"], tag=f"idxrow{rc}")
        for ch in range(RCH // CW):
            pb = ["ps0", "ps1"][ch % 2]
            pst = PS[0] if ch % 2 == 0 else PS[1]
            k.op("PE", lambda e, ch=ch, pst=pst: e.matmul(
                pst[0:8, 0:CW], ones8_s.rearrange("o e -> o e"),
                rowA[0:1, ch * CW:(ch + 1) * CW], start=True, stop=True),
                reads=["rowA", "ones8"], writes=[pb], tag=f"mk{rc}_{ch}")
            k.op("DVE", lambda e, ch=ch, pst=pst: e.tensor_scalar(
                mask1T[:, ch * CW:(ch + 1) * CW], pst[0:8, 0:CW],
                iotacol8[0:8, 0:1], 0.0, AL.is_equal, AL.add),
                reads=[pb, "iotacol8"], writes=["mask1T"], tag=f"mke{rc}_{ch}")
        ones_bc = onescol[0:8, 0:1].broadcast_to([8, RCH])
        k.op("DVE", lambda e, rc=rc: e.tensor_tensor_scan(
            locT, ones_bc, mask1T,
            0.0 if rc == 0 else carry8, AL.mult, AL.add),
            reads=["mask1T", "onescol", "carry8"], writes=["locT"],
            tag=f"locscan{rc}")
        k.op("DVE", lambda e: e.tensor_copy(carry8, locT[:, RCH - 1:RCH]),
             reads=["locT"], writes=["carry8"], tag=f"cs{rc}")
        k.op("DVE", lambda e: e.tensor_mul(locT, mask1T, locT),
             reads=["mask1T"], writes=["locT"])
        for ch in range(RCH // CW):
            pb = ["ps0", "ps1"][ch % 2]
            pst = PS[0] if ch % 2 == 0 else PS[1]
            k.op("PE", lambda e, ch=ch, pst=pst: e.matmul(
                pst[0:1, 0:CW], onescol[0:8, 0:1],
                locT[:, ch * CW:(ch + 1) * CW], start=True, stop=True),
                reads=["locT", "onescol"], writes=[pb], tag=f"lr{rc}_{ch}")
            k.op("ACT", lambda e, ch=ch, pst=pst: e.copy(
                rowB[0:1, ch * CW:(ch + 1) * CW], pst[0:1, 0:CW]),
                reads=[pb], writes=["rowB"])
        k.op("DVE", lambda e: e.tensor_scalar(rowB, rowB, CAP + 0.5, 0.0,
                                              AL.is_lt, AL.add),
             reads=["rowB"], writes=["rowB"])
        k.dma("QSP", lambda e, r0=r0: e.dma_start(
            out=keptc_d[r0:r0 + RCH].rearrange("a b -> b a"), in_=rowB),
            reads=["rowB"], writes=["keptc_d"], tag=f"keptst{rc}")
        k.op("DVE", lambda e, qstart=qstart: e.tensor_tensor_scan(
            locT, ones_bc, mask1T,
            0.0 if qstart else rank8, AL.mult, AL.add),
            reads=["mask1T", "onescol", "rank8"], writes=["locT"],
            tag=f"rk{rc}")
        k.op("DVE", lambda e: e.tensor_copy(rank8, locT[:, RCH - 1:RCH]),
             reads=["locT"], writes=["rank8"], tag=f"rks{rc}")
        if qend:
            q = r0 // Q4
            k.op("DVE", lambda e, q=q: e.tensor_copy(cnt8[:, q:q + 1], rank8),
                 reads=["rank8"], writes=["cnt8q"], tag=f"cq{rc}")
        k.op("DVE", lambda e: e.tensor_mul(locT, mask1T, locT),
             reads=["mask1T"], writes=["locT"])
        for ch in range(RCH // CW):
            pb = ["ps0", "ps1"][ch % 2]
            pst = PS[0] if ch % 2 == 0 else PS[1]
            k.op("PE", lambda e, ch=ch, pst=pst: e.matmul(
                pst[0:1, 0:CW], onescol[0:8, 0:1],
                locT[:, ch * CW:(ch + 1) * CW], start=True, stop=True),
                reads=["locT", "onescol"], writes=[pb], tag=f"rr{rc}_{ch}")
            k.op("ACT", lambda e, ch=ch, pst=pst: e.copy(
                rowB[0:1, ch * CW:(ch + 1) * CW], pst[0:1, 0:CW]),
                reads=[pb], writes=["rowB"])
        k.dma("QSP", lambda e, r0=r0: e.dma_start(
            out=rankc_d[r0:r0 + RCH].rearrange("a b -> b a"), in_=rowB),
            reads=["rowB"], writes=["rankc_d"], tag=f"rankst{rc}")
        if c.dbg:
            k.dma("QSP", lambda e, r0=r0: e.dma_start(
                out=dbg["dbg_route"][0:8, r0:r0 + RCH], in_=mask1T),
                reads=["mask1T"], writes=["dbgroute"], tag=f"dbgr{rc}")
    k.dma("QSP", lambda e: e.dma_start(
        out=cnts4_d, in_=cnt8[0:8, 0:4]),
        reads=["cnt8q"], writes=["cnts4_d"], tag="cntst")
    k.dma("QSP", lambda e: e.dma_start(
        out=idxc_d.rearrange("(a b) o -> a (b o)", b=128),
        in_=idxr_d), reads=["idxr_d"], writes=["idxc_d"], tag="idxcst")

    k.join()

    k.op("SP", stash("SP", ET.SP, [
        ("brow", metai_s[0:1, 3:4], 0, NTOK - c.T),
    ]), reads=["metai"], tag="stash_brow")

    for mm in range(NM):
        k.dma("QSP", lambda e, mm=mm: e.dma_start(
            out=xmac,
            in_=xT_d[:, mm * TM:(mm + 1) * TM].rearrange(
                "(mt p) t -> p mt t", p=128)),
            reads=["xT_d"], writes=["xmac"], tag=f"ox{mm}")
        k.dma("QSP", lambda e, mm=mm: e.dma_start(
            out=sq_a,
            in_=xT_d[:, c.T + mm * TM:c.T + (mm + 1) * TM].rearrange(
                "(mt p) t -> p mt t", p=128)),
            reads=["xT_d"], writes=["sq_a"], tag=f"ox2{mm}")
        k.op("DVE", lambda e: e.tensor_sub(
            sq_a.rearrange("p a b -> p (a b)"),
            sq_a.rearrange("p a b -> p (a b)"),
            xmac.rearrange("p a b -> p (a b)")),
            reads=["xmac", "sq_a"], writes=["sq_a"], tag=f"osel{mm}")
        k.op("DVE", lambda e: e.scalar_tensor_tensor(
            xmac.rearrange("p a b -> p (a b)"),
            sq_a.rearrange("p a b -> p (a b)"), bfcol[:, 0:1],
            xmac.rearrange("p a b -> p (a b)"), AL.mult, AL.add),
            reads=["sq_a", "bfcol", "xmac"], writes=["xmac"], tag=f"osel2{mm}")
        for nmw, dst, srcd in [("mrow", mrow, mu_d), ("rrow", rrow, rstd_d)]:
            k.dma("QSP", lambda e, mm=mm, dst=dst, srcd=srcd: e.dma_start(
                out=dst, in_=srcd.rearrange("a b -> (a b)").unsqueeze(0)[
                    0:1, mm * TM:(mm + 1) * TM]),
                reads=["mu_d" if srcd is mu_d else "rstd_d"], writes=[nmw],
                tag=f"o{nmw}{mm}")
            k.dma("QSP", lambda e, mm=mm, srcd=srcd: e.dma_start(
                out=vrow, in_=srcd.rearrange("a b -> (a b)").unsqueeze(0)[
                    0:1, c.T + mm * TM:c.T + (mm + 1) * TM]),
                reads=["mu_d" if srcd is mu_d else "rstd_d"], writes=["vrow"],
                tag=f"o2{nmw}{mm}")
            k.op("DVE", lambda e, dst=dst: e.tensor_sub(vrow, vrow, dst),
                 reads=[nmw, "vrow"], writes=["vrow"], tag=f"o3{nmw}{mm}")
            k.op("DVE", lambda e, dst=dst: e.scalar_tensor_tensor(
                dst, vrow, selmsk_s[0:1, 0:1], dst, AL.mult, AL.add),
                reads=["vrow", "selmsk"], writes=[nmw], tag=f"o4{nmw}{mm}")
        k.op("PE", lambda e: e.matmul(PS[4][:, 0:TM], onesrow, mrow,
                                      start=True, stop=True),
             reads=["onesrow", "mrow"], writes=["ps4"], tag=f"obm{mm}")
        k.op("PE", lambda e: e.matmul(PS[5][:, 0:TM], onesrow, rrow,
                                      start=True, stop=True),
             reads=["onesrow", "rrow"], writes=["ps5"], tag=f"obr{mm}")
        for mt in range(MT):
            k.op("DVE", lambda e, mt=mt: e.tensor_sub(
                xmac[:, mt, :], xmac[:, mt, :], PS[4][:, 0:TM]),
                reads=["ps4", "xmac"], writes=["xmac"])
            k.op("DVE", lambda e, mt=mt: e.tensor_mul(
                xmac[:, mt, :], xmac[:, mt, :], PS[5][:, 0:TM]),
                reads=["ps5"], writes=["xmac"])
        k.dma("QSP", lambda e, mm=mm: e.dma_start(
            out=xs_d[:, mm * TM:(mm + 1) * TM].rearrange(
                "(mt p) t -> p mt t", p=128), in_=xmac),
            reads=["xmac"], writes=["xs_d"], tag=f"oxs{mm}")

    env = dict(locals())
    emit_rest(env)
    assert env["sb_off"][0] <= PERSIST_END, (
        f"late autos overflow pad: {env['sb_off'][0]} > {PERSIST_END}")
    with nc.Block() as block:
        k.build(block)
    return env


def emit_rest(env):
    g = env
    c = g["c"]; nc = g["nc"]; k = g["k"]; dyn = g["dyn"]
    PS = g["PS"]; PBF = g["PBF"]
    Dh, GH, MT, TM, NM = c.Dh, c.GH, c.MT, c.TMAC, c.NMAC
    TCH, SOCT, NOCT = c.TCH, c.SOCT, c.NOCT
    NTC = c.T // c.TCH
    S = c.S; XW = c.R + 2 * c.S
    NTOK = c.NB * c.T
    NCH = c.NCHUNK
    FC = c.F // 128
    SUBS = min(4, FC)
    FB = FC // SUBS
    NOWN = c.TOK_OWN // 128
    ET = mybir.EngineType
    ds = bass.ds
    stash = g["stash"]

    L = lambda n: g[n]
    (ident_fs, sel_s, metaf_s, metai_s, nonce_s, onesrow, onescol, flagt,
     carry) = (g["ident_fs"], g["sel_s"], g["metaf_s"],
        g["metai_s"], g["nonce_s"], g["onesrow"], g["onescol"], g["flagt"],
        g["carry"])
    nonce2 = g["nonce2"]
    bnd = g["bnd"]
    (xmac, sq_a, sp1, xnmac, xs_sb, u_sb, ub_sb, sz_sb, dtm_sb, sp2) = (
        g["xmac"], g["sq_a"], g["sp1"], g["xnmac"], g["xs_sb"], g["u_sb"],
        g["ub_sb"], g["sz_sb"], g["dtm_sb"], g["sp2"])
    (mrow, rrow, vrow, xd_mine, xd_sum) = (g["mrow"], g["rrow"], g["vrow"],
        g["xd_mine"], g["xd_sum"])
    xd_two, wop2, srtab = g["xd_two"], g["wop2"], g["srtab"]
    xsend_s, wsend_s, fsend_s, fidx_s, selmsk_s = (g["xsend_s"], g["wsend_s"],
        g["fsend_s"], g["fidx_s"], g["selmsk_s"])
    bfcol = g["bfcol"]
    (WiT, WoT, WxT, WdtT, A_s, cb_s, bdt_s, Dp_s, lnw_s, lnb_s, dconv_s) = (
        g["WiT"], g["WoT"], g["WxT"], g["WdtT"], g["A_s"], g["cb_s"],
        g["bdt_s"], g["Dp_s"], g["lnw_s"], g["lnb_s"], g["dconv_s"])
    (brep, crep, dAc, Xc, dtg, ug, yacc, ytm, cfix, b_sb, c_sb) = (
        g["brep"], g["crep"], g["dAc"], g["Xc"], g["dtg"], g["ug"],
        g["yacc"], g["ytm"], g["cfix"], g["b_sb"], g["c_sb"])
    (ymac, uc_sb, szc_sb, ybf, womine, wopart, xcur) = (g["ymac"], g["uc_sb"],
        g["szc_sb"], g["ybf"], g["womine"], g["wopart"], g["xcur"])
    (xs_d, u_d, dt_d, sz_d, y_d, shp_x, shp_w, shf, bc_d) = (g["xs_d"],
        g["u_d"], g["dt_d"], g["sz_d"], g["y_d"], g["shp_x"], g["shp_w"],
        g["shf"], g["bc_d"])
    (Wi_sl, convw, convb, Wx_sl, Wdt_sl, bdt_sl, Alog_sl, Dp_sl, Wo_sl,
     lnw_i, lnb_i) = (g["Wi_sl"], g["convw"], g["convb"], g["Wx_sl"],
        g["Wdt_sl"], g["bdt_sl"], g["Alog_sl"], g["Dp_sl"], g["Wo_sl"],
        g["lnw_i"], g["lnb_i"])
    ident_bs = g["ident_bs"]
    sem_poll, sem_w, sem_ch = g["sem_poll"], g["sem_w"], g["sem_ch"]
    NMSG = c.NL * NM * 2 + 8
    dbg = g["dbg"]

    flag2 = g["sbuf"]("flag2", [2, 16], U32)
    shf3 = shf.rearrange("(s m) w -> s m w", s=2)

    def poll_fn(msg):
        def fn(e):
            with e.register(f"kp{msg}") as kp, e.register(f"f0{msg}") as f0, \
                 e.register(f"f1{msg}") as f1, e.register(f"nv{msg}") as nv, \
                 e.register(f"c0{msg}") as c0:
                e.reg_load(nv, nonce_s[0:1, 1:2])
                e.reg_mov(kp, 1)
                with e.While(kp):
                    e.sem_clear(sem_poll)
                    e.dma_start(out=flag2,
                                in_=shf3[:, msg, :]).then_inc(sem_poll, 16)
                    e.wait_ge(sem_poll, 16)
                    e.reg_load(f0, flag2[0:1, 0:1])
                    e.reg_load(f1, flag2[1:2, 0:1])
                    e.reg_alu(c0, f0, nv, AL.not_equal)
                    e.reg_alu(kp, f1, nv, AL.not_equal)
                    e.reg_alu(kp, kp, c0, AL.mult)
                e.sem_clear(sem_poll)
            return e.nop()
        return fn

    def send_flag(msg):
        def fn(e):
            return e.indirect_dma_start(
                shf, IndirectOffsetOnAxis(ap=fsend_s[:, msg:msg + 1], axis=0),
                nonce2, None, bounds_check=bnd(2 * NMSG), oob_is_err=False)
        return fn

    def load_col(dst, src_ap, kk, tagp, bufname):
        k.dma("QSP", lambda e: e.dma_start(
            out=g["wld_f"][0:kk, 0:128],
            in_=src_ap.rearrange("(a p) -> a p", p=128)),
            writes=["stgf"], tag=tagp + "L")
        k.op("PE", lambda e: e.transpose(
            PS[0][:, 0:kk], g["wld_f"][0:kk, 0:128],
            ident_fs[0:kk, 0:kk]),
            reads=["stgf", "ident"], writes=["ps0"], tag=tagp + "T")
        k.op("ACT", lambda e: e.copy(dst, PS[0][:, 0:kk]),
             reads=["ps0"], writes=[bufname], tag=tagp + "C")

    def load_layer_weights(l):
        R2 = 2 * Dh // 128
        stg_f = g["wld_f"][:, 0:c.M]
        stg_b = g["wld_b"][:, 0:c.M]
        for rr in range(R2):
            k.dma("QSP", lambda e, l=l, rr=rr: e.dma_start(
                out=stg_f, in_=Wi_sl[l, rr * 128:(rr + 1) * 128, :]),
                writes=["stgf"], tag=f"wl{l}_{rr}")
            k.op("DVE", lambda e: e.tensor_copy(stg_b, stg_f),
                 reads=["stgf"], writes=["stgb"])
            for mt in range(MT):
                pb = ["pbf0", "pbf1"][mt % 2]
                pp = PBF[mt % 2]
                k.op("PE", lambda e, mt=mt, pp=pp: e.transpose(
                    pp[:, 0:128], stg_b[:, mt * 128:(mt + 1) * 128], ident_bs),
                    reads=["stgb", "identb"], writes=[pb], tag=f"wt{l}_{rr}_{mt}")
                k.op("DVE", lambda e, mt=mt, rr=rr, pp=pp: e.tensor_copy(
                    WiT[:, mt, rr * 128:(rr + 1) * 128], pp[:, 0:128]),
                    reads=[pb], writes=["WiT"])
        for mr in range(MT):
            k.dma("QSP", lambda e, l=l, mr=mr: e.dma_start(
                out=g["wld_f"][:, 0:Dh], in_=Wo_sl[l, mr * 128:(mr + 1) * 128, :]),
                writes=["stgf"], tag=f"wo{l}_{mr}")
            k.op("DVE", lambda e: e.tensor_copy(g["wld_b"][:, 0:Dh], g["wld_f"][:, 0:Dh]),
                 reads=["stgf"], writes=["stgb"])
            for gg in range(GH):
                pb = ["pbf0", "pbf1"][gg % 2]
                pp = PBF[gg % 2]
                k.op("PE", lambda e, gg=gg, pp=pp: e.transpose(
                    pp[:, 0:128], g["wld_b"][:, gg * 128:(gg + 1) * 128], ident_bs),
                    reads=["stgb", "identb"], writes=[pb], tag=f"wot{l}{mr}{gg}")
                k.op("DVE", lambda e, gg=gg, mr=mr, pp=pp: e.tensor_copy(
                    WoT[:, gg, mr * 128:(mr + 1) * 128], pp[:, 0:128]),
                    reads=[pb], writes=["WoT"])
        k.dma("QSP", lambda e, l=l: e.dma_start(
            out=g["wld_f"][0:XW, 0:Dh], in_=Wx_sl[l]), writes=["stgf"], tag=f"wx{l}")
        k.op("DVE", lambda e: e.tensor_copy(g["wld_b"][0:XW, 0:Dh], g["wld_f"][0:XW, 0:Dh]),
             reads=["stgf"], writes=["stgb"])
        for gg in range(GH):
            pb = ["pbf0", "pbf1"][gg % 2]
            pp = PBF[gg % 2]
            k.op("PE", lambda e, gg=gg, pp=pp: e.transpose(
                pp[:, 0:XW], g["wld_b"][0:XW, gg * 128:(gg + 1) * 128], ident_bs[0:XW, 0:XW]),
                reads=["stgb", "identb"], writes=[pb], tag=f"wxt{l}{gg}")
            k.op("DVE", lambda e, gg=gg, pp=pp: e.tensor_copy(
                WxT[:, gg, :], pp[:, 0:XW]), reads=[pb], writes=["WxT"])
        for gg in range(GH):
            k.dma("QSP", lambda e, l=l, gg=gg: e.dma_start(
                out=g["wld_f"][:, 0:c.R], in_=Wdt_sl[l, gg * 128:(gg + 1) * 128, :]),
                writes=["stgf"], tag=f"wd{l}{gg}")
            k.op("DVE", lambda e: e.tensor_copy(g["wld_b"][:, 0:c.R], g["wld_f"][:, 0:c.R]),
                 reads=["stgf"], writes=["stgb"])
            pb = ["pbf0", "pbf1"][gg % 2]
            pp = PBF[gg % 2]
            k.op("PE", lambda e, pp=pp: e.transpose(
                pp[0:c.R, 0:128], g["wld_b"][:, 0:c.R], ident_bs),
                reads=["stgb", "identb"], writes=[pb], tag=f"wdt{l}{gg}")
            k.op("DVE", lambda e, gg=gg, pp=pp: e.tensor_copy(
                WdtT[0:c.R, gg * 128:(gg + 1) * 128], pp[0:c.R, 0:128]),
                reads=[pb], writes=["WdtT"])
        k.dma("QSP", lambda e, l=l: e.dma_start(
            out=A_s, in_=Alog_sl[l].rearrange("(g p) s -> p g s", p=128)),
            writes=["Araw"], tag=f"va{l}")
        k.op("ACT", lambda e: e.activation(A_s, A_s, AF.Exp),
             reads=["Araw"], writes=["Aexp"])
        k.op("DVE", lambda e: e.tensor_scalar(A_s, A_s, -1.0, 0.0, AL.mult, AL.add),
             reads=["Aexp"], writes=["A_s"])
        for nm, dst, src in [("cb", cb_s, convb), ("bd", bdt_s, bdt_sl),
                             ("dp", Dp_s, Dp_sl)]:
            load_col(dst, src[l], GH, f"v{nm}{l}", nm)
        for nm, dst, src in [("lw", lnw_s, lnw_i), ("lb", lnb_s, lnb_i)]:
            load_col(dst, src[l], MT, f"v{nm}{l}", nm)
        k.dma("QSP", lambda e, l=l: e.dma_start(
            out=g["wld_f"][:, 0:GH * c.DCONV].rearrange(
                "p (g k) -> p g k", k=c.DCONV),
            in_=convw[l].rearrange("(g p) k -> p g k", p=128)),
            writes=["stgf"], tag=f"cw{l}")
        for gg in range(GH):
            for kk in range(c.DCONV):
                k.op("DVE", lambda e, gg=gg, kk=kk: e.tensor_scalar(
                    dconv_s[:, gg * c.DCONV + kk, :], ident_bs,
                    g["wld_f"][:, gg * c.DCONV + kk:gg * c.DCONV + kk + 1],
                    0.0, AL.mult, AL.add),
                    reads=["stgf", "identb"], writes=["dconv"],
                    tag=f"dc{l}{gg}{kk}")

    def phase_a(l):
        k.op("DVE", lambda e: e.memset(xd_mine, 0.0), writes=["xdm"],
             tag=f"a_z{l}")
        for mm in range(NM + 1):
            cur = mm % 2
            prv = 1 - cur
            if mm < NM:
                k.dma("QSP", lambda e, mm=mm: e.dma_start(
                    out=xmac, in_=xs_d[:, mm * TM:(mm + 1) * TM].rearrange(
                        "(mt p) t -> p mt t", p=128)),
                    writes=["xmac"], tag=f"a_x{mm}")
                def mu_mm(e, mm=mm):
                    last = None
                    for mt in range(MT):
                        last = e.matmul(PS[0][0:1, 0:TM], onescol,
                                        xmac[:, mt, :], start=(mt == 0),
                                        stop=(mt == MT - 1))
                    return last
                k.op("PE", mu_mm, reads=["xmac", "onescol"], writes=["ps0"],
                     tag=f"a_mu{mm}")
                k.op("ACT", lambda e: e.activation(sq_a, xmac, AF.Square),
                     reads=["xmac"], writes=["sq_a"])
                def sq_mm(e):
                    last = None
                    for mt in range(MT):
                        last = e.matmul(PS[1][0:1, 0:TM], onescol,
                                        sq_a[:, mt, :], start=(mt == 0),
                                        stop=(mt == MT - 1))
                    return last
                k.op("PE", sq_mm, reads=["sq_a", "onescol"], writes=["ps1"],
                     tag=f"a_sq{mm}")
                k.op("ACT", lambda e: e.copy(mrow, PS[0][0:1, 0:TM]),
                     reads=["ps0"], writes=["mrow"])
                k.op("ACT", lambda e: e.copy(vrow, PS[1][0:1, 0:TM]),
                     reads=["ps1"], writes=["vrow"])
                k.op("DVE", lambda e: e.tensor_scalar(mrow, mrow, 1.0 / c.M, 0.0,
                                                      AL.mult, AL.add),
                     reads=["mrow"], writes=["mrow"])
                k.op("DVE", lambda e: e.tensor_scalar(vrow, vrow, 1.0 / c.M, 0.0,
                                                      AL.mult, AL.add),
                     reads=["vrow"], writes=["vrow"])
                k.op("DVE", lambda e: e.scalar_tensor_tensor(
                    rrow, mrow, 1.0, mrow, AL.mult, AL.mult),
                    reads=["mrow"], writes=["rrow"])
                k.op("DVE", lambda e: e.tensor_sub(vrow, vrow, rrow),
                     reads=["rrow"], writes=["vrow"])
                k.op("ACT", lambda e: e.activation(vrow, vrow, AF.Ln,
                                                   bias=float(c.EPS)),
                     reads=["vrow"], writes=["vrow"])
                k.op("ACT", lambda e: e.activation(rrow, vrow, AF.Exp, scale=-0.5),
                     reads=["vrow"], writes=["rrow"])
                k.op("PE", lambda e: e.matmul(PS[4][:, 0:TM], onesrow, mrow,
                                              start=True, stop=True),
                     reads=["onesrow", "mrow"], writes=["ps4"], tag=f"a_bm{mm}")
                k.op("PE", lambda e: e.matmul(PS[5][:, 0:TM], onesrow, rrow,
                                              start=True, stop=True),
                     reads=["onesrow", "rrow"], writes=["ps5"], tag=f"a_br{mm}")
                for mt in range(MT):
                    k.op("DVE", lambda e, mt=mt: e.tensor_sub(
                        xmac[:, mt, :], xmac[:, mt, :], PS[4][:, 0:TM]),
                        reads=["ps4"], writes=["xmac"])
                    k.op("DVE", lambda e, mt=mt: e.tensor_mul(
                        xmac[:, mt, :], xmac[:, mt, :], PS[5][:, 0:TM]),
                        reads=["ps5"], writes=["xmac"])
                    k.op("ACT", lambda e, mt=mt: e.activation(
                        xnmac[:, mt, :], xmac[:, mt, :], AF.Identity,
                        bias=lnb_s[:, mt:mt + 1], scale=lnw_s[:, mt:mt + 1]),
                        reads=["xmac", "lw", "lb"], writes=["xnmac"])
                for io in range(2 * GH):
                    pb = ["ps2", "ps3"][io % 2]
                    pp = PS[2 + io % 2]
                    def xz_mm(e, io=io, pp=pp):
                        last = None
                        for mt in range(MT):
                            last = e.matmul(
                                pp[:, 0:TM], WiT[:, mt, io * 128:(io + 1) * 128],
                                xnmac[:, mt, :], start=(mt == 0),
                                stop=(mt == MT - 1))
                        return last
                    k.op("PE", xz_mm, reads=["WiT", "xnmac"], writes=[pb],
                         tag=f"a_xz{mm}_{io}")
                    if io < GH:
                        k.op("ACT", lambda e, io=io, cur=cur, pp=pp: e.activation(
                            xs_sb[cur][:, io, c.DCONV - 1:], pp[:, 0:TM], AF.Copy),
                            reads=[pb], writes=[f"xs{cur}"])
                    else:
                        k.op("ACT", lambda e, io=io, pp=pp: e.activation(
                            sz_sb[:, io - GH, :], pp[:, 0:TM], AF.Silu),
                            reads=[pb], writes=["sz_sb"])
                if mm == 0:
                    k.op("DVE", lambda e, cur=cur: e.memset(
                        xs_sb[cur][:, :, 0:c.DCONV - 1], 0.0),
                        writes=[f"xs{cur}"], tag=f"a_h0{mm}")
                else:
                    for gg in range(GH):
                        k.op("DVE", lambda e, gg=gg, cur=cur, prv=prv:
                             e.tensor_copy(
                                 xs_sb[cur][:, gg, 0:c.DCONV - 1],
                                 xs_sb[prv][:, gg, TM:TM + c.DCONV - 1]),
                             reads=[f"xs{prv}"], writes=[f"xs{cur}"],
                             tag=f"a_h{mm}_{gg}")
                for gg in range(GH):
                    pb = ["ps2", "ps3"][gg % 2]
                    pp = PS[2 + gg % 2]
                    def conv_mm(e, gg=gg, cur=cur, pp=pp):
                        last = None
                        for kk in range(c.DCONV):
                            last = e.matmul(
                                pp[:, 0:TM],
                                dconv_s[:, gg * c.DCONV + kk, :],
                                xs_sb[cur][:, gg, kk:kk + TM],
                                start=(kk == 0), stop=(kk == c.DCONV - 1))
                        return last
                    k.op("PE", conv_mm, reads=[f"xs{cur}", "dconv"], writes=[pb],
                         tag=f"a_cv{mm}_{gg}")
                    k.op("ACT", lambda e, gg=gg, pp=pp: e.activation(
                        u_sb[:, gg, :], pp[:, 0:TM], AF.Silu,
                        bias=cb_s[:, gg:gg + 1]),
                        reads=[pb, "cb"], writes=["u_sb"])
                k.op("DVE", lambda e: e.tensor_copy(ub_sb, u_sb),
                     reads=["u_sb"], writes=["ub_sb"])
                k.dma("QSP", lambda e, mm=mm: e.dma_start(
                    out=u_d[:, mm * TM:(mm + 1) * TM].rearrange(
                        "(g p) t -> p g t", p=128), in_=u_sb),
                    reads=["u_sb"], writes=["u_d"], tag=f"a_us{mm}")
                k.dma("QSP", lambda e, mm=mm: e.dma_start(
                    out=sz_d[:, mm * TM:(mm + 1) * TM].rearrange(
                        "(g p) t -> p g t", p=128), in_=sz_sb),
                    reads=["sz_sb"], writes=["sz_d"], tag=f"a_ss{mm}")
                for (nm, pidx, lo, n) in [("B", 0, c.R, S),
                                          ("C", 1, c.R + S, S),
                                          ("dt", 5, 0, c.R)]:
                    pb = f"ps{pidx}"
                    pp = PS[pidx]
                    def xd_mm(e, lo=lo, n=n, pp=pp):
                        last = None
                        for gg in range(GH):
                            last = e.matmul(
                                pp[0:n, 0:TM], WxT[:, gg, lo:lo + n],
                                ub_sb[:, gg, :], start=(gg == 0),
                                stop=(gg == GH - 1))
                        return last
                    k.op("PE", xd_mm, reads=["WxT", "ub_sb"], writes=[pb],
                         tag=f"a_xd{mm}_{nm}")
                base = {"B": 0, "C": 32, "dt": 64}
                k.op("ACT", lambda e, cur=cur: e.copy(
                    xd_mine[0:S, cur, :], PS[0][0:S, 0:TM]),
                    reads=["ps0"], writes=["xdm"])
                k.op("ACT", lambda e, cur=cur: e.copy(
                    xd_mine[32:32 + S, cur, :], PS[1][0:S, 0:TM]),
                    reads=["ps1"], writes=["xdm"])
                k.op("ACT", lambda e, cur=cur: e.copy(
                    xd_mine[64:64 + c.R, cur, :], PS[5][0:c.R, 0:TM]),
                    reads=["ps5"], writes=["xdm"])
                msg = l * NM + mm
                k.dma("QIND", lambda e, msg=msg, cur=cur: e.indirect_dma_start(
                    shp_x,
                    IndirectOffsetOnAxis(ap=xsend_s[:, msg:msg + 1], axis=0),
                    xd_mine[:, cur, :], None,
                    bounds_check=bnd(2 * c.NL * NM * 128), oob_is_err=False),
                    reads=["xdm", "xsend"], writes=["shsent"], tag=f"a_sx{mm}")
                k.dma("QIND", send_flag(msg), reads=["shsent", "nonce2", "fsend"],
                      writes=["flag_sent"], tag=f"a_sf{mm}")
            if mm >= 1:
                pm = mm - 1
                pcur = pm % 2
                msg = l * NM + pm
                k.op("SP", poll_fn(msg), reads=["flag_sent"], writes=["polled"],
                     tag=f"a_pl{pm}")
                k.dma("QSP", lambda e, msg=msg: e.dma_start(
                    out=xd_two,
                    in_=shp_x.rearrange("(s r) t -> s r t", s=2)[
                        :, msg * 128:(msg + 1) * 128, :].rearrange(
                        "s p t -> p s t")),
                    reads=["polled", "shsent"], writes=["xdp"], tag=f"a_lp{pm}")
                k.op("DVE", lambda e: e.tensor_add(
                    xd_sum, xd_two[:, 0, :], xd_two[:, 1, :]),
                    reads=["xdp"], writes=["xds"])
                k.dma("QACT", lambda e, pm=pm: e.dma_start(
                    out=bc_d[0:S, pm * TM:(pm + 1) * TM], in_=xd_sum[0:S, :]),
                    reads=["xds"], writes=["bc_d"], tag=f"a_bs{pm}")
                k.dma("QACT", lambda e, pm=pm: e.dma_start(
                    out=bc_d[S:2 * S, pm * TM:(pm + 1) * TM],
                    in_=xd_sum[32:32 + S, :]),
                    reads=["xds"], writes=["bc_d"], tag=f"a_cs{pm}")
                k.op("DVE", lambda e: e.tensor_copy(
                    xnmac.rearrange("p a b -> p (a b)")[0:c.R, 0:TM],
                    xd_sum[64:64 + c.R, :]),
                    reads=["xds"], writes=["xnmac"], tag=f"a_dc{pm}")
                for gg in range(GH):
                    pb = ["ps2", "ps3"][gg % 2]
                    pp = PS[2 + gg % 2]
                    k.op("PE", lambda e, gg=gg, pp=pp: e.matmul(
                        pp[:, 0:TM], WdtT[0:c.R, gg * 128:(gg + 1) * 128],
                        xnmac.rearrange("p a b -> p (a b)")[0:c.R, 0:TM],
                        start=True, stop=True),
                        reads=["WdtT", "xnmac"], writes=[pb], tag=f"a_dm{pm}_{gg}")
                    k.op("DVE", lambda e, gg=gg, pp=pp: e.tensor_scalar(
                        sp1[:, gg, :], pp[:, 0:TM], bdt_s[:, gg:gg + 1], 0.0,
                        AL.add, AL.add),
                        reads=[pb, "bd"], writes=["sq_a"], tag=f"a_sp{pm}_{gg}")
                for gg in range(GH):
                    k.op("DVE", lambda e, gg=gg: e.scalar_tensor_tensor(
                        sp2[:, gg, :], sp1[:, gg, :], -1.0, sp1[:, gg, :],
                        AL.mult, AL.max),
                        reads=["sq_a"], writes=["sp2"])
                k.op("ACT", lambda e: e.activation(dtm_sb, sp2, AF.Exp, scale=-1.0),
                     reads=["sp2"], writes=["dtm"])
                k.op("ACT", lambda e: e.activation(sp2, dtm_sb, AF.Ln, bias=1.0),
                     reads=["dtm"], writes=["sp2"])
                for gg in range(GH):
                    k.op("DVE", lambda e, gg=gg: e.scalar_tensor_tensor(
                        dtm_sb[:, gg, :], sp1[:, gg, :], 0.0, sp2[:, gg, :],
                        AL.max, AL.add),
                        reads=["sq_a", "sp2"], writes=["dtm"])
                k.dma("QSP", lambda e, pm=pm: e.dma_start(
                    out=dt_d[:, pm * TM:(pm + 1) * TM].rearrange(
                        "(g p) t -> p g t", p=128), in_=dtm_sb),
                    reads=["dtm"], writes=["dt_d"], tag=f"a_ds{pm}")

    def phase_b(l):
        k.op("DVE", lambda e: e.memset(carry, 0.0), writes=["carry"],
             tag=f"b_c0{l}")
        for tc in range(NTC):
            t0 = tc * TCH
            k.dma("QSP", lambda e, t0=t0: e.dma_start(
                out=dtg, in_=dt_d[:, t0:t0 + TCH].rearrange(
                    "(g p) t -> p g t", p=128)),
                reads=["dt_d"], writes=["dtg"], tag=f"b_ld{tc}")
            k.dma("QSP", lambda e, t0=t0: e.dma_start(
                out=ug, in_=u_d[:, t0:t0 + TCH].rearrange(
                    "(g p) t -> p g t", p=128)),
                reads=["u_d"], writes=["ug"], tag=f"b_lu{tc}")
            k.dma("QSP", lambda e, t0=t0: e.dma_start(
                out=b_sb, in_=bc_d[0:S, t0:t0 + TCH]),
                reads=["bc_d"], writes=["b_sb"], tag=f"b_lb{tc}")
            k.dma("QSP", lambda e, t0=t0: e.dma_start(
                out=c_sb, in_=bc_d[S:2 * S, t0:t0 + TCH]),
                reads=["bc_d"], writes=["c_sb"], tag=f"b_lc{tc}")
            for gg in range(GH):
                k.op("DVE", lambda e, gg=gg: e.tensor_mul(
                    ug[:, gg, :], dtg[:, gg, :], ug[:, gg, :]),
                    reads=["dtg", "ug"], writes=["ug"], tag=f"b_v{tc}_{gg}")
            CCW = min(512, TCH)
            for oc in range(NOCT):
                for sl in range(SOCT):
                    s_g = oc * SOCT + sl
                    for cc in range(TCH // CCW):
                        pb = ["ps0", "ps1"][cc % 2]
                        pp = PS[cc % 2]
                        k.op("PE", lambda e, s_g=s_g, cc=cc, pp=pp:
                             e.matmul(pp[0:128, 0:CCW],
                                      sel_s[:, s_g * 128:(s_g + 1) * 128],
                                      b_sb[:, cc * CCW:(cc + 1) * CCW],
                                      start=True, stop=True),
                             reads=["b_sb", "sel"], writes=[pb],
                             tag=f"b_rb{tc}{oc}{sl}{cc}")
                        k.op("ACT", lambda e, sl=sl, cc=cc, pp=pp: e.copy(
                            brep[:, sl * TCH + cc * CCW: sl * TCH + (cc + 1) * CCW],
                            pp[:, 0:CCW]), reads=[pb], writes=["brep"])
                        k.op("PE", lambda e, s_g=s_g, cc=cc, pp=pp:
                             e.matmul(pp[0:128, 0:CCW],
                                      sel_s[:, s_g * 128:(s_g + 1) * 128],
                                      c_sb[:, cc * CCW:(cc + 1) * CCW],
                                      start=True, stop=True),
                             reads=["c_sb", "sel"], writes=[pb],
                             tag=f"b_rc{tc}{oc}{sl}{cc}")
                        k.op("ACT", lambda e, sl=sl, cc=cc, pp=pp: e.copy(
                            crep[:, sl * TCH + cc * CCW: sl * TCH + (cc + 1) * CCW],
                            pp[:, 0:CCW]), reads=[pb], writes=["crep"])
                for gg in range(GH):
                    for sl in range(SOCT):
                        s_g = oc * SOCT + sl
                        k.op("ACT", lambda e, gg=gg, sl=sl, s_g=s_g: e.activation(
                            dAc[:, sl * TCH:(sl + 1) * TCH], dtg[:, gg, :],
                            AF.Exp, scale=A_s[:, gg, s_g:s_g + 1]),
                            reads=["dtg", "A_s"], writes=["dAc"],
                            tag=f"b_da{tc}{oc}{gg}{sl}")
                    vb = ug[:, gg, :].unsqueeze(1).broadcast_to([128, SOCT, TCH])
                    k.op("DVE", lambda e, vb=vb: e.tensor_mul(
                        Xc.rearrange("p (s t) -> p s t", s=SOCT), vb,
                        brep.rearrange("p (s t) -> p s t", s=SOCT)),
                        reads=["ug", "brep"], writes=["Xc"], tag=f"b_X{tc}{oc}{gg}")
                    dv = dAc.rearrange("p (s t) -> p s t", s=SOCT)[:, :, 0]
                    xv = Xc.rearrange("p (s t) -> p s t", s=SOCT)[:, :, 0]
                    cs_ = carry[:, gg, oc * SOCT:(oc + 1) * SOCT]
                    k.op("DVE", lambda e, dv=dv, cs_=cs_: e.tensor_mul(
                        cfix, dv, cs_), reads=["dAc", "carry"], writes=["cfix"],
                        tag=f"b_cf{tc}{oc}{gg}")
                    k.op("DVE", lambda e, xv=xv: e.tensor_add(xv, xv, cfix),
                         reads=["cfix"], writes=["Xc"])
                    k.op("DVE", lambda e, dv=dv: e.memset(dv, 0.0),
                         reads=["cfix"], writes=["dAc"], tag=f"b_z{tc}{oc}{gg}")
                    k.op("DVE", lambda e: e.tensor_tensor_scan(
                        Xc, dAc, Xc, 0.0, AL.mult, AL.add),
                        reads=["dAc"], writes=["Xc"], tag=f"b_sc{tc}{oc}{gg}")
                    hv = Xc.rearrange("p (s t) -> p s t", s=SOCT)[:, :, TCH - 1]
                    k.op("DVE", lambda e, hv=hv, cs_=cs_: e.tensor_copy(cs_, hv),
                         reads=["Xc"], writes=["carry"], tag=f"b_cs{tc}{oc}{gg}")
                    k.op("DVE", lambda e: e.tensor_mul(dAc, Xc, crep),
                         reads=["Xc", "crep"], writes=["dAc"],
                         tag=f"b_cm{tc}{oc}{gg}")
                    red_in = dAc.rearrange("p (s t) -> p t s", s=SOCT)
                    if oc == 0:
                        k.op("DVE", lambda e, gg=gg, red_in=red_in: e.tensor_reduce(
                            yacc[:, gg, :].unsqueeze(2), red_in, AX.X, AL.add),
                            reads=["dAc"], writes=["yacc"], tag=f"b_rd{tc}{oc}{gg}")
                    else:
                        k.op("DVE", lambda e, red_in=red_in: e.tensor_reduce(
                            ytm.unsqueeze(2), red_in, AX.X, AL.add),
                            reads=["dAc"], writes=["ytm"], tag=f"b_rd{tc}{oc}{gg}")
                        k.op("DVE", lambda e, gg=gg: e.tensor_add(
                            yacc[:, gg, :], yacc[:, gg, :], ytm),
                            reads=["ytm"], writes=["yacc"], tag=f"b_ra{tc}{oc}{gg}")
            k.dma("QSP", lambda e, t0=t0: e.dma_start(
                out=y_d[:, t0:t0 + TCH].rearrange("(g p) t -> p g t", p=128),
                in_=yacc), reads=["yacc"], writes=["y_d"], tag=f"b_ys{tc}")

    def phase_c(l, last):
        for mm in range(NM + 1):
            cur = mm % 2
            prv = 1 - cur
            if mm < NM:
                k.dma("QSP", lambda e, mm=mm: e.dma_start(
                    out=ymac, in_=y_d[:, mm * TM:(mm + 1) * TM].rearrange(
                        "(g p) t -> p g t", p=128)),
                    reads=["y_d"], writes=["ymac"], tag=f"c_y{mm}")
                k.dma("QSP", lambda e, mm=mm: e.dma_start(
                    out=uc_sb, in_=u_d[:, mm * TM:(mm + 1) * TM].rearrange(
                        "(g p) t -> p g t", p=128)),
                    reads=["u_d"], writes=["uc_sb"], tag=f"c_u{mm}")
                k.dma("QSP", lambda e, mm=mm: e.dma_start(
                    out=szc_sb, in_=sz_d[:, mm * TM:(mm + 1) * TM].rearrange(
                        "(g p) t -> p g t", p=128)),
                    reads=["sz_d"], writes=["szc_sb"], tag=f"c_s{mm}")
                k.dma("QSP", lambda e, mm=mm, cur=cur: e.dma_start(
                    out=xcur[:, :, cur, :],
                    in_=xs_d[:, mm * TM:(mm + 1) * TM].rearrange(
                        "(mt p) t -> p mt t", p=128)),
                    reads=["xs_d"], writes=["xcur"], tag=f"c_x{mm}")
                for gg in range(GH):
                    k.op("DVE", lambda e, gg=gg: e.scalar_tensor_tensor(
                        ymac[:, gg, :], uc_sb[:, gg, :], Dp_s[:, gg:gg + 1],
                        ymac[:, gg, :], AL.mult, AL.add),
                        reads=["uc_sb", "dp", "ymac"], writes=["ymac"],
                        tag=f"c_g{mm}_{gg}")
                k.op("DVE", lambda e: e.tensor_mul(ymac, ymac, szc_sb),
                     reads=["szc_sb"], writes=["ymac"])
                k.op("DVE", lambda e: e.tensor_copy(ybf, ymac),
                     reads=["ymac"], writes=["ybf"])
                for mt in range(MT):
                    pb = ["ps2", "ps3"][mt % 2]
                    pp = PS[2 + mt % 2]
                    def wo_mm(e, mt=mt, pp=pp):
                        last_i = None
                        for gg in range(GH):
                            last_i = e.matmul(
                                pp[:, 0:TM], WoT[:, gg, mt * 128:(mt + 1) * 128],
                                ybf[:, gg, :], start=(gg == 0),
                                stop=(gg == GH - 1))
                        return last_i
                    k.op("PE", wo_mm, reads=["WoT", "ybf"], writes=[pb],
                         tag=f"c_wo{mm}_{mt}")
                    k.op("ACT", lambda e, mt=mt, cur=cur, pp=pp: e.copy(
                        womine[:, mt, cur, :], pp[:, 0:TM]),
                        reads=[pb], writes=["wom"])
                msg = c.NL * NM + l * NM + mm
                for mt in range(MT):
                    k.dma("QIND", lambda e, mm=mm, cur=cur, l=l, mt=mt:
                          e.indirect_dma_start(
                              shp_w,
                              IndirectOffsetOnAxis(
                                  ap=wsend_s[:, (l * NM + mm) * MT + mt:
                                             (l * NM + mm) * MT + mt + 1],
                                  axis=0),
                              womine[:, mt, cur, :], None,
                              bounds_check=bnd(2 * c.NL * NM * c.M),
                              oob_is_err=False),
                          reads=["wom", "wsend"], writes=["wsent"],
                          tag=f"c_sw{mm}_{mt}")
                k.dma("QIND", send_flag(msg), reads=["wsent", "nonce2", "fsend"],
                      writes=["wflag"], tag=f"c_sf{mm}")
            if mm >= 1:
                pm = mm - 1
                pcur = pm % 2
                msg = c.NL * NM + l * NM + pm
                k.op("SP", poll_fn(msg), reads=["wflag"], writes=["wpolled"],
                     tag=f"c_pl{pm}")
                k.dma("QSP", lambda e, pm=pm, l=l: e.dma_start(
                    out=wopart,
                    in_=shp_w[(l * NM + pm) * c.M:(l * NM + pm + 1) * c.M,
                              :].rearrange("(mt p) t -> p mt t", p=128)),
                    reads=["wpolled", "wsent"], writes=["wop"], tag=f"c_lp{pm}")
                k.dma("QSP", lambda e, pm=pm, l=l: e.dma_start(
                    out=wop2,
                    in_=shp_w[c.NL * NM * c.M + (l * NM + pm) * c.M:
                              c.NL * NM * c.M + (l * NM + pm + 1) * c.M,
                              :].rearrange("(mt p) t -> p mt t", p=128)),
                    reads=["wpolled", "wsent"], writes=["wop2"], tag=f"c_l2{pm}")
                k.op("DVE", lambda e, pcur=pcur: e.tensor_add(
                    xcur[:, :, pcur, :], xcur[:, :, pcur, :], wopart),
                    reads=["wop", "xcur"], writes=["xcur"], tag=f"c_a1{pm}")
                k.op("DVE", lambda e, pcur=pcur: e.tensor_add(
                    xcur[:, :, pcur, :], xcur[:, :, pcur, :], wop2),
                    reads=["wop2"], writes=["xcur"], tag=f"c_a2{pm}")
                k.dma("QSP", lambda e, pm=pm, pcur=pcur: e.dma_start(
                    out=xs_d[:, pm * TM:(pm + 1) * TM].rearrange(
                        "(mt p) t -> p mt t", p=128),
                    in_=xcur[:, :, pcur, :]),
                    reads=["xcur"], writes=["xs_d"], tag=f"c_xs{pm}")

    w1bf_d, w2bf_d, b1col_d, b2col_d = (g["w1bf_d"], g["w2bf_d"],
                                        g["b1col_d"], g["b2col_d"])
    pw1_i, pb1_i, pw2_i, pb2_i = g["w1_i"], g["b1_i"], g["w2_i"], g["b2_i"]
    b1c_p, b2c_p = g["b1c"], g["b2c"]
    wld_f, wld_b = g["wld_f"], g["wld_b"]
    CW1 = min(512, c.F)
    CW2 = min(512, c.M)
    pst = []

    def _slab(src, dst, w, outname):
        def fn():
            k.dma("QACT", lambda e: e.dma_start(out=wld_f[:, 0:w], in_=src),
                  writes=["stgf"], tag="psI")
            k.op("DVE", lambda e: e.tensor_copy(wld_b[:, 0:w], wld_f[:, 0:w]),
                 reads=["stgf"], writes=["stgb"], tag="psC")
            k.dma("QPOOL", lambda e: e.dma_start(out=dst, in_=wld_b[:, 0:w]),
                  reads=["stgb"], writes=[outname], tag="psO")
        return fn

    for ee in range(c.E):
        for mt in range(MT):
            for fo in range(0, c.F, CW1):
                pst.append(_slab(pw1_i[ee, mt * 128:(mt + 1) * 128,
                                       fo:fo + CW1],
                                 w1bf_d[ee, mt * 128:(mt + 1) * 128,
                                        fo:fo + CW1], CW1, "w1bf"))
        for fc in range(FC):
            for mo in range(0, c.M, CW2):
                pst.append(_slab(pw2_i[ee, fc * 128:(fc + 1) * 128,
                                       mo:mo + CW2],
                                 w2bf_d[ee, fc * 128:(fc + 1) * 128,
                                        mo:mo + CW2], CW2, "w2bf"))

        def _bias(ee=ee):
            load_col(b1c_p, pb1_i[ee], FC, f"psb1{ee}", "b1c")
            k.dma("QPOOL", lambda e, ee=ee: e.dma_start(
                out=b1col_d[ee], in_=b1c_p),
                reads=["b1c"], writes=["b1col"], tag=f"psb1o{ee}")
            load_col(b2c_p, pb2_i[ee], MT, f"psb2{ee}", "b2c")
            k.dma("QPOOL", lambda e, ee=ee: e.dma_start(
                out=b2col_d[ee], in_=b2c_p),
                reads=["b2c"], writes=["b2col"], tag=f"psb2o{ee}")
        pst.append(_bias)
    for fn in pst:
        fn()

    k.dma("QSP", lambda e: e.dma_start(out=sel_s, in_=g["selmat"]),
          writes=["sel"], tag="selload")
    for l in range(c.NL):
        k.join()
        load_layer_weights(l)
        k.join()
        phase_a(l)
        k.join()
        phase_b(l)
        k.join()
        phase_c(l, last=(l == c.NL - 1))
        if g["c"].dbg and l == 0:
            k.join()
            for nm, src in [("dbg_u", u_d), ("dbg_dt", dt_d), ("dbg_sz", sz_d),
                            ("dbg_y", y_d)]:
                if nm in dbg:
                    for rr in range(Dh // 128):
                        k.dma("QACT", lambda e, nm=nm, src=src, rr=rr:
                              e.dma_start(out=dbg[nm][rr * 128:(rr + 1) * 128, :],
                                          in_=src[rr * 128:(rr + 1) * 128, :]),
                              writes=[f"d{nm}"], tag=f"dbg{nm}{rr}")
            bcv = dbg.get("dbg_BC")
            if bcv is not None:
                k.dma("QACT", lambda e: e.dma_start(out=bcv, in_=bc_d),
                      reads=["bc_d"], writes=["dbc1"], tag="dbgBC")
        if g["c"].dbg and l == 0 and "dbg_x1" in dbg:
            k.join()
            k.dma("QACT", lambda e: e.dma_start(out=dbg["dbg_x1"], in_=xs_d),
                  reads=["xs_d"], writes=["dx1"], tag="dbgx1")

    (w1e, w2e, xg, xgsq, xnb, xnT, h1b, g1col, keptcol, tloc, scat_i,
     tok_i, tokf, eo_sb, eot, gsc1, gsc2, gsc3, b1c, b2c, zero_t) = (
        g["w1e"], g["w2e"], g["xg"], g["xgsq"], g["xnb"],
        g["xnT"], g["h1b"], g["g1col"], g["keptcol"], g["tloc"], g["scat_i"],
        g["tok_i"], g["tokf"], g["eo_sb"], g["eot"], g["gsc1"], g["gsc2"],
        g["gsc3"], g["b1c"], g["b2c"], g["zero_t"])
    (x_full, w1_i, b1_i, w2_i, b2_i, g1c_d, keptc_d, rankc_d, idxc_d,
     cnts4_d, toklist_d, gbuf_d, iota_r, out_own, tok0b) = (
        g["x_full"], g["w1_i"], g["b1_i"], g["w2_i"], g["b2_i"], g["g1c_d"],
        g["keptc_d"], g["rankc_d"], g["idxc_d"], g["cnts4_d"], g["toklist_d"],
        g["gbuf_d"], g["iota_r"], g["out_own"], g["tok0b"])
    (k128_s, trie_s, ones8_s, iota_cs, chi_sb, ch_sb, cnt8, colsc) = (
        g["k128_s"], g["trie_s"], g["ones8_s"], g["iota_cs"], g["chi_sb"],
        g["ch_sb"], g["cnt8"], g["colsc"])
    iotaNCH_c = g["iotaNCH_i"]; iotaNCH_s = g["iotaNCH_s"]
    fs_sb, fsT, fx, fg = g["fs_sb"], g["fsT"], g["fx"], g["fg"]
    fsl2 = g["fsl2"]
    jfcol = g["jfcol"]
    E = c.E
    TOK_OWN = c.TOK_OWN

    k.join()
    k.op("DVE", lambda e: e.memset(zero_t, 0.0), writes=["zero_t"],
         tag="m_z0")
    k.dma("QSP", lambda e: e.dma_start(out=iotaNCH_s, in_=iotaNCH_c),
          writes=["ioN"], tag="m_ion")
    crow = g["csrow"]
    k.dma("QSP", lambda e: e.dma_start(
        out=crow, in_=cnts4_d[:, ds(dyn[("SP", "quart")], 1)].rearrange(
            "e o -> o e"), allow_slow_non_contiguous=True),
        reads=["cnts4_d"], writes=["csrow"], tag="m_cnt")
    k.op("PE", lambda e: e.matmul(PS[0][0:8, 0:1], crow, onesrow[0:1, 0:1],
                                  start=True, stop=True),
         reads=["csrow", "onesrow"], writes=["ps0"], tag="m_cc")
    cntcol = g["iotacol8"]
    cntc = g["cntc"]
    k.op("ACT", lambda e: e.copy(cntc, PS[0][0:8, 0:1]), reads=["ps0"],
         writes=["cntc"])
    k.op("PE", lambda e: e.matmul(PS[1][0:8, 0:NCH + 1], onesrow[0:1, 0:8],
                                  k128_s, start=True, stop=True),
         reads=["onesrow", "k128"], writes=["ps1"], tag="m_k1")
    k.op("DVE", lambda e: e.tensor_scalar(
        cnt8[0:8, :], PS[1][0:8, 0:NCH + 1], cntc[0:8, 0:1], 0.0,
        AL.is_lt, AL.add), reads=["ps1", "cntc"], writes=["cnt8"], tag="m_cmp")
    k.op("DVE", lambda e: e.tensor_reduce(
        cntc.unsqueeze(2), cnt8[0:8, :].unsqueeze(1), AX.X, AL.add),
        reads=["cnt8"], writes=["cntc"], tag="m_nch")
    k.op("PE", lambda e: e.matmul(PS[0][0:8, 0:1], trie_s, cntc[0:8, 0:1],
                                  start=True, stop=True),
         reads=["trie", "cntc"], writes=["ps0"], tag="m_ccum")
    ccumc = g["ccumc"]
    k.op("ACT", lambda e: e.copy(ccumc, PS[0][0:8, 0:1]), reads=["ps0"],
         writes=["ccumc"])
    k.op("PE", lambda e: e.transpose(PS[1][0:1, 0:8], ccumc, ident_fs[0:8, 0:8]),
         reads=["ccumc", "ident"], writes=["ps1"], tag="m_ccT")
    ccrow = g["ccrow"]
    k.op("ACT", lambda e: e.copy(ccrow, PS[1][0:1, 0:8]), reads=["ps1"],
         writes=["ccrow"])
    k.op("PE", lambda e: e.matmul(PS[0][0:NCH, 0:8], onesrow[0:1, 0:NCH], ccrow,
                                  start=True, stop=True),
         reads=["onesrow", "ccrow"], writes=["ps0"], tag="m_ccb")
    ccrep = g["ccrep"]
    k.op("ACT", lambda e: e.copy(ccrep, PS[0][0:NCH, 0:8]), reads=["ps0"],
         writes=["ccrep"])
    cmpc = g["cmpc"]
    k.op("DVE", lambda e: e.tensor_scalar(
        cmpc, ccrep, iotaNCH_s, 0.0, AL.is_le, AL.add),
        reads=["ccrep", "ioN"], writes=["cmpc"], tag="m_cmp2")
    k.op("DVE", lambda e: e.tensor_reduce(
        ch_sb[:, 0:1].unsqueeze(2), cmpc.unsqueeze(1), AX.X, AL.add),
        reads=["cmpc"], writes=["chsb"], tag="m_ej")
    k.op("DVE", lambda e: e.tensor_scalar(
        ch_sb[:, 0:1], ch_sb[:, 0:1], 1.0, -1.0, AL.mult, AL.add),
        reads=["chsb"], writes=["chsb"])
    k.op("DVE", lambda e: e.tensor_mul(cmpc, cmpc, ccrep),
         reads=["ccrep"], writes=["cmpc"])
    k.op("DVE", lambda e: e.tensor_reduce(
        ch_sb[:, 1:2].unsqueeze(2), cmpc.unsqueeze(1), AX.X, AL.max),
        reads=["cmpc"], writes=["chsb"], tag="m_cca")
    k.op("DVE", lambda e: e.tensor_sub(ch_sb[:, 1:2], iotaNCH_s,
        ch_sb[:, 1:2]), reads=["ioN"], writes=["chsb"])
    k.op("DVE", lambda e: e.tensor_scalar(
        ch_sb[:, 2:3], ch_sb[:, 1:2], 128.0, 0.0, AL.mult, AL.add),
        reads=["chsb"], writes=["chsb"], tag="m_sr1")
    k.op("DVE", lambda e: e.scalar_tensor_tensor(
        ch_sb[:, 1:2], ch_sb[:, 0:1], float(TOK_OWN), ch_sb[:, 2:3],
        AL.mult, AL.add), reads=["chsb"], writes=["chsb"], tag="m_sr2")
    k.op("DVE", lambda e: e.tensor_scalar(
        ch_sb[:, 3:4], ch_sb[:, 0:1], 1.0, 1.0, AL.mult, AL.add),
        reads=["chsb"], writes=["chsb"], tag="m_ej1")
    k.op("DVE", lambda e: e.tensor_copy(chi_sb[:, 0:1], ch_sb[:, 1:2]),
         reads=["chsb"], writes=["chi"], tag="m_chi0")
    k.op("DVE", lambda e: e.tensor_copy(chi_sb[:, 1:2], ch_sb[:, 3:4]),
         reads=["chsb"], writes=["chi"], tag="m_chi1")
    k.op("DVE", lambda e: e.tensor_scalar(
        ch_sb[:, 2:3], ch_sb[:, 3:4], 1.0, float(E), AL.mult, AL.add),
        reads=["chsb"], writes=["chsb"], tag="m_ej9")
    k.op("DVE", lambda e: e.tensor_copy(chi_sb[:, 2:3], ch_sb[:, 2:3]),
         reads=["chsb"], writes=["chi"], tag="m_chi2")
    k.op("PE", lambda e: e.transpose(PS[1][0:1, 0:NCH], ch_sb[:, 1:2],
                                     ident_fs[0:NCH, 0:NCH]),
         reads=["chsb", "ident"], writes=["ps1"], tag="m_srT")
    srrow = g["sbuf"]("srrow", [1, NCH], F32)
    k.op("ACT", lambda e: e.copy(srrow, PS[1][0:1, 0:NCH]),
         reads=["ps1"], writes=["srrow"], tag="m_srr")
    k.op("PE", lambda e: e.matmul(PS[0][:, 0:NCH], onesrow, srrow,
                                  start=True, stop=True),
         reads=["onesrow", "srrow"], writes=["ps0"], tag="m_srb")
    srf = g["sbuf"]("srf", [128, NCH], F32)
    k.op("DVE", lambda e: e.tensor_scalar(
        srf, PS[0][:, 0:NCH], iota_cs[:, 0:1], 0.0, AL.add, AL.add),
        reads=["ps0", "iotac"], writes=["srf"], tag="m_sradd")
    k.op("DVE", lambda e: e.tensor_copy(srtab, srf),
         reads=["srf"], writes=["srtab"], tag="m_srci")
    if c.dbg and "dbg_cht" in dbg:
        k.dma("QACT", lambda e: e.dma_start(out=dbg["dbg_cht"], in_=ch_sb[:, 0:2]),
              reads=["chsb"], writes=["dcht"], tag="m_dcht")
    cci = g["cci"]
    k.op("DVE", lambda e: e.tensor_copy(cci, ccumc), reads=["ccumc"],
         writes=["cci"], tag="m_cci")
    k.op("DVE", lambda e: e.tensor_scalar(
        colsc[0:8, 0:1], ccumc, 1.0, float(NCH), AL.mult, AL.add),
        reads=["ccumc"], writes=["cc2f"], tag="m_cc2f")
    k.op("DVE", lambda e: e.tensor_copy(g["cci2"], colsc[0:8, 0:1]),
         reads=["cc2f"], writes=["cci2"], tag="m_cci2")

    bigi = g["bigi"]
    k.op("DVE", lambda e: e.memset(bigi, 1 << 22), writes=["bigi"], tag="m_bi")
    nrow_tl = E * TOK_OWN
    assert nrow_tl % 512 == 0
    tl_prow = nrow_tl // 512
    for bb in range(0, tl_prow, 64):
        rows = min(64, tl_prow - bb)
        k.dma("QSP", lambda e, bb=bb, rows=rows: e.dma_start(
            out=toklist_d.rearrange("(a b) o -> a (b o)", b=512)[
                bb:bb + rows, :],
            in_=bigi[0:rows, :]), reads=["bigi"], writes=["tokl"],
            tag=f"m_ti{bb}")
    ZW = min(512, c.M)
    for bb in range(c.TOK_OWN // 128):
        for ms in range(c.M // ZW):
            k.dma("QSP", lambda e, bb=bb, ms=ms: e.dma_start(
                out=gbuf_d[bb * 128:(bb + 1) * 128, ms * ZW:(ms + 1) * ZW],
                in_=zero_t[:, 0:ZW]),
                reads=["zero_t"], writes=["gbuf"], tag=f"m_gz{bb}_{ms}")
    for i in range(NOWN):
        k.dma("QIND", lambda e, i=i: e.indirect_dma_start(
            g1col, None, g1c_d,
            IndirectOffsetOnAxis(ap=fidx_s[:, i:i + 1], axis=0),
            bounds_check=bnd(NTOK), oob_is_err=False),
            reads=["g1c_d", "fidx"], writes=["g1col"], tag=f"m_p1{i}")
        k.dma("QIND", lambda e, i=i: e.indirect_dma_start(
            keptcol, None, keptc_d,
            IndirectOffsetOnAxis(ap=fidx_s[:, i:i + 1], axis=0),
            bounds_check=bnd(NTOK), oob_is_err=False),
            reads=["keptc_d", "fidx"], writes=["keptcol"], tag=f"m_p2{i}")
        k.dma("QIND", lambda e, i=i: e.indirect_dma_start(
            tloc, None, rankc_d,
            IndirectOffsetOnAxis(ap=fidx_s[:, i:i + 1], axis=0),
            bounds_check=bnd(NTOK), oob_is_err=False),
            reads=["rankc_d", "fidx"], writes=["tloc"], tag=f"m_p3{i}")
        k.dma("QIND", lambda e, i=i: e.indirect_dma_start(
            tokf, None, idxc_d,
            IndirectOffsetOnAxis(ap=fidx_s[:, i:i + 1], axis=0),
            bounds_check=bnd(NTOK), oob_is_err=False),
            reads=["idxc_d", "fidx"], writes=["tokf"], tag=f"m_p4{i}")
        k.op("DVE", lambda e: e.scalar_tensor_tensor(
            g1col if False else tloc, tokf, float(TOK_OWN), tloc,
            AL.mult, AL.add), reads=["tokf", "tloc"], writes=["tloc"],
            tag=f"m_lp{i}")
        k.op("DVE", lambda e: e.tensor_scalar(
            tloc, tloc, 1.0, -1.0, AL.mult, AL.add),
            reads=["tloc"], writes=["tloc"])
        k.op("DVE", lambda e: e.scalar_tensor_tensor(
            gsc1[:, 0:1], keptcol, -BIG, tloc, AL.mult, AL.add),
            reads=["keptcol", "tloc"], writes=["gsc1c"], tag=f"m_ko{i}")
        k.op("DVE", lambda e: e.tensor_scalar(
            gsc1[:, 0:1], gsc1[:, 0:1], 1.0, BIG, AL.mult, AL.add),
            reads=["gsc1c"], writes=["gsc1c"])
        k.op("DVE", lambda e: e.tensor_copy(scat_i, gsc1[:, 0:1]),
             reads=["gsc1c"], writes=["scat_i"])
        k.op("DVE", lambda e, i=i: e.tensor_copy(tok_i, fidx_s[:, i:i + 1]),
             reads=["fidx"], writes=["tok_i"])
        k.dma("QIND", lambda e, i=i: e.indirect_dma_start(
            toklist_d, IndirectOffsetOnAxis(ap=scat_i, axis=0),
            tok_i, None, bounds_check=bnd(nrow_tl), oob_is_err=False),
            reads=["scat_i", "tok_i", "tokl"], writes=["tokl"],
            tag=f"m_sc{i}")

    h1_d = g["h1_d"]
    cci2 = g["cci2"]

    def stash_pe(j, nm, col, lo, hi):
        def fn(e):
            dyn[("PE", f"{nm}{j}")] = nc.values_load(
                chi_sb[j:j + 1, col:col + 1], engines=[ET.PE], min_val=lo,
                max_val=hi, skip_runtime_bounds_check=True)
            return None
        return fn

    def free_val(e, key):
        v = dyn.pop(key, None)
        if v is not None:
            try:
                for h in v.val.handles:
                    e.free_register(h)
            except Exception:
                pass

    def stash_ccum_sp(nm, src, lo, hi):
        def fn(e):
            for ee in range(E):
                if nm == "cc2":
                    free_val(e, ("SP", f"ccum{ee}"))
                dyn[("SP", f"{nm}{ee}")] = nc.values_load(
                    src[ee:ee + 1, 0:1], engines=[ET.SP], min_val=lo,
                    max_val=hi, skip_runtime_bounds_check=True)
            return None
        return fn

    def chunk_pre(j, sweep):
        k.op("DVE", lambda e: e.memset(tok_i, 1 << 22), writes=["tok_i"],
             tag=f"m{sweep}_tz{j}")
        k.dma("QIND", lambda e, j=j: e.indirect_dma_start(
            tok_i, None, toklist_d,
            IndirectOffsetOnAxis(ap=srtab[:, j:j + 1], axis=0),
            bounds_check=bnd(nrow_tl), oob_is_err=False),
            reads=["tokl", "srtab"], writes=["tok_i"], tag=f"m{sweep}_ix{j}")

    k.op("SP", stash_ccum_sp("ccum", cci, 0, NCH), reads=["cci"],
         tag="m_stcc")
    for ee in range(E):
        def wl1(e, ee=ee):
            e.wait_ge(sem_ch, dyn[("SP", f"ccum{ee}")])
            return e.dma_start(
                out=w1e, in_=g["w1bf_d"][ee].rearrange(
                    "(mt p) f -> p mt f", p=128))
        k.dma("QSP", wl1, reads=["w1bf"], writes=["w1e"], tag=f"m_w1{ee}")
        k.dma("QSP", lambda e, ee=ee: e.dma_start(
            out=b1c, in_=g["b1col_d"][ee]),
            reads=["b1col"], writes=["b1c"], tag=f"m_b1{ee}")
        k.op("SP", lambda e: (e.nop().then_inc(sem_w, 1), None)[1],
             reads=["w1e", "b1c"], tag=f"m_wd{ee}")

    for j in range(NCH):
        k.op("PE", stash_pe(j, "ej1", 1, 1, E), reads=["chi"],
             tag=f"m_stp{j}")
        chunk_pre(j, 1)
        k.dma("QIND", lambda e: e.indirect_dma_start(
            xg, None, x_full, IndirectOffsetOnAxis(ap=tok_i, axis=0),
            bounds_check=bnd(NTOK), oob_is_err=False),
            reads=["tok_i", "x_full"], writes=["xg"], tag=f"m_gx{j}")
        k.op("DVE", lambda e: e.tensor_reduce(
            tloc.unsqueeze(2), xg.unsqueeze(1), AX.X, AL.add),
            reads=["xg"], writes=["tloc"], tag=f"m_ln1{j}")
        k.op("ACT", lambda e: e.activation(xgsq, xg, AF.Square),
             reads=["xg"], writes=["xgsq"])
        k.op("DVE", lambda e: e.tensor_reduce(
            gsc2[:, 1:2].unsqueeze(2), xgsq.unsqueeze(1), AX.X, AL.add),
            reads=["xgsq"], writes=["sx"], tag=f"m_ln2{j}")
        k.op("DVE", lambda e: e.tensor_scalar(
            tloc, tloc, 1.0 / c.M, 0.0, AL.mult, AL.add),
            reads=["tloc"], writes=["tloc"])
        k.op("DVE", lambda e: e.tensor_scalar(
            gsc2[:, 1:2], gsc2[:, 1:2], 1.0 / c.M, 0.0, AL.mult, AL.add),
            reads=["sx"], writes=["sx"])
        k.op("DVE", lambda e: e.scalar_tensor_tensor(
            gsc3[:, 0:1], tloc, 1.0, tloc, AL.mult, AL.mult),
            reads=["tloc"], writes=["sc3"])
        k.op("DVE", lambda e: e.tensor_sub(gsc2[:, 1:2], gsc2[:, 1:2],
                                           gsc3[:, 0:1]),
             reads=["sc3"], writes=["sx"])
        k.op("ACT", lambda e: e.activation(gsc2[:, 1:2], gsc2[:, 1:2], AF.Ln,
                                           bias=float(c.EPS)),
             reads=["sx"], writes=["sx"])
        k.op("ACT", lambda e: e.activation(gsc2[:, 1:2], gsc2[:, 1:2], AF.Exp,
                                           scale=-0.5),
             reads=["sx"], writes=["sx"])
        k.op("DVE", lambda e: e.tensor_scalar(
            xg, xg, tloc, 0.0, AL.subtract, AL.add),
            reads=["xg", "tloc"], writes=["xg"], tag=f"m_ln3{j}")
        k.op("DVE", lambda e: e.tensor_scalar(
            xnb, xg, gsc2[:, 1:2], 0.0, AL.mult, AL.add),
            reads=["xg", "sx"], writes=["xnb"], tag=f"m_ln4{j}")
        for mt in range(MT):
            pb = ["pbf0", "pbf1"][mt % 2]
            pp = PBF[mt % 2]
            k.op("PE", lambda e, mt=mt, pp=pp: e.transpose(
                pp[:, 0:128], xnb[:, mt * 128:(mt + 1) * 128], ident_bs),
                reads=["xnb", "identb"], writes=[pb], tag=f"m_tx{j}_{mt}")
            k.op("DVE", lambda e, mt=mt, pp=pp: e.tensor_copy(
                xnT[:, mt, :], pp[:, 0:128]), reads=[pb], writes=["xnT"])
        for fb in range(FB):
            def m1(e, fb=fb, j=j):
                last = None
                if fb == 0:
                    e.wait_ge(sem_w, dyn[("PE", f"ej1{j}")])
                for sub in range(SUBS):
                    fcol = (fb * SUBS + sub) * 128
                    for mt in range(MT):
                        last = e.matmul(
                            PS[2][:, sub * 128:(sub + 1) * 128],
                            w1e[:, mt, fcol:fcol + 128], xnT[:, mt, :],
                            start=(mt == 0), stop=(mt == MT - 1))
                if fb == FB - 1:
                    free_val(e, ("PE", f"ej1{j}"))
                return last
            k.op("PE", m1, reads=["xnT"], writes=["ps2"],
                 tag=f"m_m1{j}_{fb}")
            for sub in range(SUBS):
                fc = fb * SUBS + sub
                k.op("ACT", lambda e, sub=sub, fc=fc: e.activation(
                    gsc1[:, sub * 128:(sub + 1) * 128],
                    PS[2][:, sub * 128:(sub + 1) * 128], AF.Identity,
                    bias=b1c[:, fc:fc + 1]),
                    reads=["ps2"], writes=["gsc1"], tag=f"m_g0{j}{fb}{sub}")
            k.op("ACT", lambda e: e.activation(
                gsc2[:, 0:SUBS * 128], gsc1[:, 0:SUBS * 128], AF.Square),
                 reads=["gsc1"], writes=["gsc2"])
            k.op("DVE", lambda e: e.tensor_scalar(
                gsc2[:, 0:SUBS * 128], gsc2[:, 0:SUBS * 128], 0.044715, 1.0,
                AL.mult, AL.add),
                reads=["gsc2"], writes=["gsc2"])
            k.op("DVE", lambda e: e.tensor_mul(
                gsc3[:, 0:SUBS * 128], gsc1[:, 0:SUBS * 128],
                gsc2[:, 0:SUBS * 128]),
                 reads=["gsc1", "gsc2"], writes=["gsc3"])
            k.op("ACT", lambda e: e.activation(
                gsc2[:, 0:SUBS * 128], gsc3[:, 0:SUBS * 128], AF.Tanh,
                scale=0.7978845608028654),
                reads=["gsc3"], writes=["gsc2"])
            k.op("DVE", lambda e: e.scalar_tensor_tensor(
                gsc3[:, 0:SUBS * 128], gsc2[:, 0:SUBS * 128], 1.0,
                gsc1[:, 0:SUBS * 128], AL.add, AL.mult),
                reads=["gsc1", "gsc2"], writes=["gsc3"])
            k.op("DVE", lambda e, fb=fb: e.tensor_scalar(
                h1b[:, fb * SUBS:(fb + 1) * SUBS, :].rearrange(
                    "p a b -> p (a b)"),
                gsc3[:, 0:SUBS * 128], 0.5, 0.0, AL.mult, AL.add),
                reads=["gsc3"], writes=["h1b"], tag=f"m_hb{j}{fb}")
        k.dma("QACT", lambda e, j=j: e.dma_start(
            out=h1_d[j * 128:(j + 1) * 128, :],
            in_=h1b.rearrange("p a b -> p (a b)")),
            reads=["h1b"], writes=["h1_d"], tag=f"m_hs{j}")
        k.op("POOL", lambda e: (e.nop().then_inc(sem_ch, 1), None)[1],
             reads=["h1_d"], tag=f"m_cd{j}")

    k.op("SP", stash_ccum_sp("cc2", cci2, NCH, 2 * NCH), reads=["cci2"],
         tag="m_stcc2")
    for ee in range(E):
        def wl2(e, ee=ee):
            e.wait_ge(sem_ch, dyn[("SP", f"cc2{ee}")])
            return e.dma_start(
                out=w2e, in_=g["w2bf_d"][ee].rearrange(
                    "(fc p) m -> p fc m", p=128))
        k.dma("QSP", wl2, reads=["w2bf"], writes=["w2e"], tag=f"m_w2{ee}")
        k.dma("QSP", lambda e, ee=ee: e.dma_start(
            out=b2c, in_=g["b2col_d"][ee]),
            reads=["b2col"], writes=["b2c"], tag=f"m_b2{ee}")
        k.op("SP", lambda e: (e.nop().then_inc(sem_w, 1), None)[1],
             reads=["w2e", "b2c"], tag=f"m_we{ee}")

    for j in range(NCH):
        k.op("PE", stash_pe(j, "ej9", 2, E + 1, 2 * E), reads=["chi"],
             tag=f"m_st2p{j}")
        chunk_pre(j, 2)
        k.dma("QIND", lambda e: e.indirect_dma_start(
            g1col, None, g1c_d, IndirectOffsetOnAxis(ap=tok_i, axis=0),
            bounds_check=bnd(NTOK), oob_is_err=False),
            reads=["tok_i", "g1c_d"], writes=["g1col"], tag=f"m_gg{j}")
        k.dma("QIND", lambda e: e.indirect_dma_start(
            keptcol, None, keptc_d, IndirectOffsetOnAxis(ap=tok_i, axis=0),
            bounds_check=bnd(NTOK), oob_is_err=False),
            reads=["tok_i", "keptc_d"], writes=["keptcol"], tag=f"m_gk{j}")
        k.dma("QACT", lambda e, j=j: e.dma_start(
            out=h1b.rearrange("p a b -> p (a b)"),
            in_=h1_d[j * 128:(j + 1) * 128, :]),
            reads=["h1_d"], writes=["h1b"], tag=f"m_hl{j}")
        k.op("DVE", lambda e: e.tensor_copy(tokf, tok_i),
             reads=["tok_i"], writes=["tokf"], tag=f"m_tv{j}")
        k.op("DVE", lambda e: e.tensor_scalar(
            gsc2[:, 0:1], tokf, float(NTOK) - 0.5, 0.0, AL.is_lt, AL.add),
            reads=["tokf"], writes=["valid"], tag=f"m_vd{j}")
        k.op("DVE", lambda e: e.tensor_sub(tloc, tokf, tok0b),
             reads=["tokf", "tok0b"], writes=["tloc"])
        k.op("DVE", lambda e: e.scalar_tensor_tensor(
            gsc3[:, 0:1], gsc2[:, 0:1], -BIG, tloc, AL.mult, AL.add),
            reads=["valid", "tloc"], writes=["sc3"], tag=f"m_s1{j}")
        k.op("DVE", lambda e: e.tensor_scalar(
            gsc3[:, 0:1], gsc3[:, 0:1], 1.0, BIG, AL.mult, AL.add),
            reads=["sc3"], writes=["sc3"])
        k.op("DVE", lambda e: e.scalar_tensor_tensor(
            gsc1[:, 0:1], keptcol, -BIG, gsc3[:, 0:1], AL.mult, AL.add),
            reads=["keptcol", "sc3"], writes=["gsc1c"], tag=f"m_s2{j}")
        k.op("DVE", lambda e: e.tensor_scalar(
            gsc1[:, 0:1], gsc1[:, 0:1], 1.0, BIG, AL.mult, AL.add),
            reads=["gsc1c"], writes=["gsc1c"])
        k.op("DVE", lambda e: e.tensor_copy(scat_i, gsc1[:, 0:1]),
             reads=["gsc1c"], writes=["scat_i"], tag=f"m_s3{j}")
        for msub in range(MT):
            pp = PS[4] if msub < 4 else PS[5]
            pb = "ps4" if msub < 4 else "ps5"
            colo = (msub % 4) * 128
            def m2(e, msub=msub, pp=pp, colo=colo, j=j):
                last = None
                if msub == 0:
                    e.wait_ge(sem_w, dyn[("PE", f"ej9{j}")])
                for fc in range(FC):
                    last = e.matmul(
                        pp[:, colo:colo + 128],
                        w2e[:, fc, msub * 128:(msub + 1) * 128], h1b[:, fc, :],
                        start=(fc == 0), stop=(fc == FC - 1))
                if msub == MT - 1:
                    free_val(e, ("PE", f"ej9{j}"))
                return last
            k.op("PE", m2, reads=["h1b"], writes=[pb],
                 tag=f"m_m2{j}_{msub}")
            k.op("ACT", lambda e, msub=msub, pp=pp, colo=colo: e.activation(
                eo_sb[:, msub, :], pp[:, colo:colo + 128], AF.Identity,
                bias=b2c[:, msub:msub + 1]),
                reads=[pb], writes=["eo_sb"])
        for msub in range(MT):
            pb2 = ["ps0", "ps1"][msub % 2]
            pp2 = PS[msub % 2]
            k.op("PE", lambda e, msub=msub, pp2=pp2: e.transpose(
                pp2[:, 0:128], eo_sb[:, msub, :], ident_fs),
                reads=["eo_sb", "ident"], writes=[pb2], tag=f"m_te{j}_{msub}")
            k.op("ACT", lambda e, msub=msub, pp2=pp2: e.copy(
                eot[:, msub * 128:(msub + 1) * 128], pp2[:, 0:128]),
                reads=[pb2], writes=["eot"])
        k.op("DVE", lambda e: e.tensor_scalar(
            eot, eot, g1col, 0.0, AL.mult, AL.add),
            reads=["eot", "g1col"], writes=["eot"], tag=f"m_gt{j}")
        k.op("ACT", lambda e: e.activation(eot, eot, AF.Silu),
             reads=["eot"], writes=["eot"], tag=f"m_si{j}")
        k.dma("QIND", lambda e: e.indirect_dma_start(
            gbuf_d, IndirectOffsetOnAxis(ap=scat_i, axis=0),
            eot, None, bounds_check=bnd(TOK_OWN), oob_is_err=False),
            reads=["scat_i", "eot", "gbuf"], writes=["gbuf"], tag=f"m_sg{j}")
        k.op("POOL", lambda e: (e.nop().then_inc(sem_ch, 1), None)[1],
             reads=["gbuf"], tag=f"m_cd2{j}")

    if c.dbg and "dbg_g" in dbg:
        k.join()
        for rr in range(TOK_OWN // 128):
            k.dma("QACT", lambda e, rr=rr: e.dma_start(
                out=dbg["dbg_g"][rr * 128:(rr + 1) * 128, :],
                in_=gbuf_d[rr * 128:(rr + 1) * 128, :]),
                writes=["dbgg"], tag=f"m_dg{rr}")

    k.join()
    for i in range(NOWN):
        k.dma("QSP", lambda e, i=i: e.dma_start(
            out=fs_sb, in_=g["xs_d"][:, i * 128:(i + 1) * 128].rearrange(
                "(mt p) t -> p mt t", p=128)),
            reads=["xs_d"], writes=["fs_sb"], tag=f"f_s{i}")
        k.dma("QSP", lambda e, i=i: e.dma_start(
            out=g["fg"] if False else fsl2,
            in_=g["xs_d"][:, c.TOK_OWN + i * 128:
                          c.TOK_OWN + (i + 1) * 128].rearrange(
                "(mt p) t -> p mt t", p=128)),
            reads=["xs_d"], writes=["fsl2"], tag=f"f_s2{i}")
        k.op("DVE", lambda e: e.tensor_sub(
            fsl2.rearrange("p a b -> p (a b)"),
            fsl2.rearrange("p a b -> p (a b)"),
            fs_sb.rearrange("p a b -> p (a b)")),
            reads=["fs_sb", "fsl2"], writes=["fsl2"], tag=f"f_d{i}")
        k.op("DVE", lambda e: e.scalar_tensor_tensor(
            fs_sb.rearrange("p a b -> p (a b)"),
            fsl2.rearrange("p a b -> p (a b)"), jfcol[:, 0:1],
            fs_sb.rearrange("p a b -> p (a b)"), AL.mult, AL.add),
            reads=["fsl2", "jfcol", "fs_sb"], writes=["fs_sb"], tag=f"f_e{i}")
        k.dma("QIND", lambda e, i=i: e.indirect_dma_start(
            fx, None, x_full, IndirectOffsetOnAxis(ap=fidx_s[:, i:i + 1],
                                                   axis=0),
            bounds_check=bnd(NTOK), oob_is_err=False),
            reads=["x_full", "fidx"], writes=["fx"], tag=f"f_x{i}")
        k.dma("QSP", lambda e, i=i: e.dma_start(
            out=fg, in_=gbuf_d[i * 128:(i + 1) * 128, :]),
            reads=["gbuf"], writes=["fg"], tag=f"f_g{i}")
        for mt in range(MT):
            pb = ["ps0", "ps1"][mt % 2]
            pp = PS[mt % 2]
            k.op("PE", lambda e, mt=mt, pp=pp: e.transpose(
                pp[:, 0:128], fs_sb[:, mt, :], ident_fs),
                reads=["fs_sb", "ident"], writes=[pb], tag=f"f_t{i}_{mt}")
            k.op("ACT", lambda e, mt=mt, pp=pp: e.copy(
                fsT[:, mt * 128:(mt + 1) * 128], pp[:, 0:128]),
                reads=[pb], writes=["fsT"])
        k.op("DVE", lambda e: e.tensor_mul(fsT, fsT, fg),
             reads=["fsT", "fg"], writes=["fsT"], tag=f"f_m{i}")
        k.op("DVE", lambda e: e.tensor_add(fsT, fsT, fx),
             reads=["fx"], writes=["fsT"])
        k.dma("QSP", lambda e, i=i: e.dma_start(
            out=out_own[i * 128:(i + 1) * 128, :], in_=fsT),
            reads=["fsT"], writes=["out"], tag=f"f_o{i}")
    k.join()

"""Host-side input layout for the island kernel (pure slicing/constants)."""


def core_inputs(c: Cfg, fi, core_id, nonce):
    b = (core_id // 2) % 2
    j = core_id % 2
    Dh = c.Dh
    H = slice(j * Dh, (j + 1) * Dh)
    Hz = slice(c.D + j * Dh, c.D + (j + 1) * Dh)
    NTOK = c.NB * c.T
    NCH = c.NCHUNK
    f32 = np.float32

    Wi = fi["L0_attn_W_in"]
    Wi_sl = np.concatenate([Wi[:, H, :], Wi[:, Hz, :]], axis=1)
    own_tok0 = b * c.T + j * c.TOK_OWN
    metaf = np.zeros((1, 16), f32)
    metaf[0, 0] = own_tok0
    metai = np.zeros((1, 16), np.int32)
    metai[0, 0] = own_tok0
    metai[0, 1] = j
    metai[0, 2] = own_tok0 // c.TOK_OWN
    metai[0, 3] = b * c.T
    metai[0, 4] = 1 - j
    NM = c.T // c.TMAC
    NMSG = c.NL * NM * 2 + 8
    MT = c.M // 128

    selmat = np.zeros((16, c.S * 128), f32)
    for s in range(c.S):
        selmat[s, s * 128:(s + 1) * 128] = 1.0

    return {
        "x_full": np.ascontiguousarray(fi["L0_attn_x"].reshape(NTOK, c.M)).astype(f32),
        "Wi_sl": np.ascontiguousarray(Wi_sl).astype(f32),
        "convw": np.ascontiguousarray(fi["L0_attn_conv_w"][:, H, :]).astype(f32),
        "convb": np.ascontiguousarray(fi["L0_attn_conv_b"][:, H]).astype(f32),
        "Wx_sl": np.ascontiguousarray(fi["L0_attn_W_x"][:, :, H]).astype(f32),
        "Wdt_sl": np.ascontiguousarray(fi["L0_attn_W_dt"][:, H, :]).astype(f32),
        "bdt_sl": np.ascontiguousarray(fi["L0_attn_b_dt"][:, H]).astype(f32),
        "Alog_sl": np.ascontiguousarray(fi["L0_attn_A_log"][:, H, :]).astype(f32),
        "Dp_sl": np.ascontiguousarray(fi["L0_attn_Dp"][:, H]).astype(f32),
        "Wo_sl": np.ascontiguousarray(fi["L0_attn_W_out"][:, :, H]).astype(f32),
        "lnw_i": np.ascontiguousarray(fi["L0_attn_ln_w"]).astype(f32),
        "lnb_i": np.ascontiguousarray(fi["L0_attn_ln_b"]).astype(f32),
        "wg_i": np.ascontiguousarray(fi["L0_ffn_wg"]).astype(f32),
        "w1_i": np.ascontiguousarray(fi["L0_ffn_w1"]).astype(f32),
        "b1_i": np.ascontiguousarray(fi["L0_ffn_b1"]).astype(f32),
        "w2_i": np.ascontiguousarray(fi["L0_ffn_w2"]).astype(f32),
        "b2_i": np.ascontiguousarray(fi["L0_ffn_b2"]).astype(f32),
        "ident_f": np.eye(128, dtype=f32),
        "ident_b": np.eye(128).astype(ml_dtypes.bfloat16),
        "iotaNCH_i": np.arange(NCH, dtype=f32).reshape(NCH, 1),
        "selmat": selmat,
        "iota_r": np.arange(NTOK, dtype=f32).reshape(1, NTOK),
        "iota_c": np.arange(128, dtype=f32).reshape(128, 1),
        "iota8_r": np.arange(8, dtype=f32).reshape(1, 8),
        "k128_r": (np.arange(NCH + 1, dtype=f32) * 128).reshape(1, NCH + 1),
        "trie_i": (np.arange(8)[:, None] < np.arange(8)[None, :]).astype(f32),
        "ones8": np.ones((1, 8), f32),
        "metaf_i": metaf,
        "xsend_i": (np.arange(128)[:, None]
                    + (j * c.NL * NM + np.arange(c.NL * NM)[None, :]) * 128
                    ).astype(np.int32),
        "wsend_i": (np.arange(128)[:, None] + j * c.NL * NM * c.M
                    + (np.repeat(np.arange(c.NL * NM), MT) * c.M
                       + np.tile(np.arange(MT), c.NL * NM) * 128)[None, :]
                    ).astype(np.int32),
        "fsend_i": np.vstack([j * NMSG + np.arange(NMSG),
                              np.full(NMSG, 1 << 22)]).astype(np.int32),
        "fidx_i": (own_tok0 + np.arange(128)[:, None]
                   + 128 * np.arange(c.TOK_OWN // 128)[None, :]).astype(np.int32),
        "selmsk_i": _selmsk(b, j),
        "metai_i": metai,
        "nonce_i": _nonce_row(nonce, j),
    }


def _selmsk(b, j):
    row = np.zeros((1, 16), np.float32)
    row[0, 0] = b
    row[0, 1] = j
    return row


def _nonce_row(nonce, j):
    row = np.zeros((1, 16), np.uint32)
    row[0, 0] = nonce[j]
    row[0, 1] = nonce[1 - j]
    return row


def small_cfg(dbg=True):
    return Cfg(M=128, D=256, S=16, R=16, T=256, NB=2, NL=2, DCONV=4,
               TMAC=64, TCH=64, SOCT=4, E=8, F=256, MCH=128, MOE_SLACK=7,
               dbg=dbg)


def small_inputs(c: Cfg, seed=0):
    rng = np.random.default_rng(seed)
    f32 = np.float32
    NTOK = c.NB * c.T
    sc = 0.05
    fi = {
        "L0_attn_x": rng.standard_normal((c.NB, c.T, c.M)).astype(f32),
        "L0_attn_W_in": (rng.standard_normal((c.NL, 2 * c.D, c.M)) * sc).astype(f32),
        "L0_attn_conv_w": (rng.standard_normal((c.NL, c.D, c.DCONV)) * 0.3).astype(f32),
        "L0_attn_conv_b": (rng.standard_normal((c.NL, c.D)) * 0.2).astype(f32),
        "L0_attn_W_x": (rng.standard_normal((c.NL, c.R + 2 * c.S, c.D)) * sc).astype(f32),
        "L0_attn_W_dt": (rng.standard_normal((c.NL, c.D, c.R)) * sc).astype(f32),
        "L0_attn_b_dt": (rng.standard_normal((c.NL, c.D)) * 1.0 - 2.0).astype(f32),
        "L0_attn_A_log": (rng.standard_normal((c.NL, c.D, c.S)) * 0.5).astype(f32),
        "L0_attn_Dp": (1.0 + 0.2 * rng.standard_normal((c.NL, c.D))).astype(f32),
        "L0_attn_W_out": (rng.standard_normal((c.NL, c.M, c.D)) * sc).astype(f32),
        "L0_attn_ln_w": (1.0 + 0.2 * rng.standard_normal((c.NL, c.M))).astype(f32),
        "L0_attn_ln_b": (0.2 * rng.standard_normal((c.NL, c.M))).astype(f32),
        "L0_ffn_wg": (rng.standard_normal((c.M, c.E)) / np.sqrt(c.M)).astype(f32),
        "L0_ffn_w1": (rng.standard_normal((c.E, c.M, c.F)) / np.sqrt(c.M)).astype(f32),
        "L0_ffn_b1": (0.1 * rng.standard_normal((c.E, c.F))).astype(f32),
        "L0_ffn_w2": (rng.standard_normal((c.E, c.F, c.M)) / np.sqrt(c.F)).astype(f32),
        "L0_ffn_b2": (0.1 * rng.standard_normal((c.E, c.M))).astype(f32),
    }
    return fi

import os as _os
import time as _time
import hashlib as _hashlib
import shutil as _shutil
import pathlib as _pathlib
from concourse.bass_utils import run_bass_kernel_spmd
import concourse.bass_utils as _bu
from concourse import bass2jax as _b2j

_NEFF_CACHE = _pathlib.Path("/var/tmp/bass-neff-cache")
_orig_compile_bir = _bu.compile_bir_kernel


def _cached_compile_bir(bir_json, tmpdir, neff_name="file.neff"):
    key = None
    try:
        _NEFF_CACHE.mkdir(parents=True, exist_ok=True)
        raw = bir_json if isinstance(bir_json, bytes) else bir_json.encode()
        key = _hashlib.sha256(raw).hexdigest()[:32]
        hit = _NEFF_CACHE / (key + ".neff")
        if hit.exists():
            dst = _pathlib.Path(tmpdir) / neff_name
            _shutil.copyfile(hit, dst)
            return str(dst)
    except Exception:
        pass
    out = _orig_compile_bir(bir_json, tmpdir, neff_name=neff_name)
    if key is not None:
        try:
            tmp = _NEFF_CACHE / f".{key}.{_os.getpid()}.tmp"
            _shutil.copyfile(out, tmp)
            tmp.replace(_NEFF_CACHE / (key + ".neff"))
        except Exception:
            pass
    return out


_bu.compile_bir_kernel = _cached_compile_bir
_b2j.compile_bir_kernel = _cached_compile_bir

_BUILD_CACHE = {}


def _get_nc():
    if "nc" not in _BUILD_CACHE:
        c = Cfg()
        env = build_core(c, debug=False, detect_races=False)
        _BUILD_CACHE["nc"] = env["nc"]
        _BUILD_CACHE["c"] = c
    return _BUILD_CACHE["c"], _BUILD_CACHE["nc"]


def kernel(**inputs):
    c, nc = _get_nc()
    seed = (_os.getpid() << 16) ^ int(_time.time_ns() & 0xFFFFFFFF)
    rng = np.random.default_rng(seed)
    while True:
        nonce = rng.integers(1, 2**32 - 1, size=2, dtype=np.uint32)
        if nonce[0] != nonce[1]:
            break
    in_maps = [core_inputs(c, inputs, cid, nonce) for cid in range(8)]
    res = run_bass_kernel_spmd(nc, in_maps, core_ids=list(range(8)))
    NTOK = c.NB * c.T
    out = np.empty((NTOK, c.M), np.float32)
    for cid in range(4):
        b = (cid // 2) % 2
        j = cid % 2
        own0 = b * c.T + j * c.TOK_OWN
        out[own0:own0 + c.TOK_OWN] = res.results[cid]["out_own"]
    return out.reshape(c.NB, c.T, c.M)


# version 03
# speedup vs baseline: 1.1801x; 1.1801x over previous
import numpy as np
import concourse.bass as bass
import concourse.mybir as mybir
from dataclasses import dataclass, field

F32 = mybir.dt.float32
BF16 = mybir.dt.bfloat16
I32 = mybir.dt.int32
U32 = mybir.dt.uint32
AL = mybir.AluOpType
AF = mybir.ActivationFunctionType
AX = mybir.AxisListType


@dataclass
class Cfg:
    M: int = 1024
    D: int = 2048
    S: int = 16
    R: int = 64
    T: int = 8192
    NB: int = 2
    NL: int = 4
    DCONV: int = 4
    EPS: float = 1e-5
    TMAC: int = 256
    TCH: int = 512
    SOCT: int = 4
    E: int = 8
    F: int = 4096
    MCH: int = 128
    MOE_SLACK: int = 7
    dbg: bool = False

    @property
    def Dh(self):
        return self.D // 2

    @property
    def GH(self):
        return self.Dh // 128

    @property
    def MT(self):
        return self.M // 128

    @property
    def CAP(self):
        return max((self.NB * self.T) // self.E, 4)

    @property
    def NMAC(self):
        return self.T // self.TMAC

    @property
    def NTC(self):
        return self.T // self.TCH

    @property
    def NOCT(self):
        return self.S // self.SOCT

    @property
    def TOK_OWN(self):
        return self.T // 2

    @property
    def NCHUNK(self):
        return self.TOK_OWN // self.MCH + self.MOE_SLACK


ENGS = ("PE", "ACT", "DVE", "POOL", "SP")
QUEUES = {"QSP": "SP", "QACT": "ACT", "QPOOL": "POOL", "QIND": "POOL"}
NQSEM = 8


class Buf:
    __slots__ = ("name", "writer", "readers")

    def __init__(self, name):
        self.name = name
        self.writer = None
        self.readers = []


class K:
    def __init__(self, nc: bass.Bass, cfg: Cfg):
        self.nc = nc
        self.cfg = cfg
        self.ops = {e: [] for e in ENGS}
        self.sem_n = {e: 0 for e in ENGS}
        self.quse = {q: [0] * NQSEM for q in QUEUES}
        self.qrr = {q: 0 for q in QUEUES}
        self.bufs = {}
        self._sems = {}
        self.wm = {}
        self._anon = 0

    def buf(self, name):
        if name not in self.bufs:
            self.bufs[name] = Buf(name)
        return self.bufs[name]

    def _dep_waits(self, waiter_key, reads, writes):
        need = {}
        def add(dep):
            if dep is None:
                return
            sk, val = dep
            if need.get(sk, 0) < val:
                need[sk] = val
        for r in reads:
            b = self.buf(r)
            if b.writer:
                add(b.writer)
        for w in writes:
            b = self.buf(w)
            if b.writer:
                add(b.writer)
            for rd in b.readers:
                add(rd)
        out = []
        for sk, val in need.items():
            key = (waiter_key, sk)
            if self.wm.get(key, -1) >= val:
                continue
            out.append((sk, val))
            self.wm[key] = val
        return out

    def _record(self, stream, fn, reads, writes, inc, tag):
        if stream in QUEUES:
            eng_stream = QUEUES[stream]
            idx = self.qrr[stream]
            self.qrr[stream] = (idx + 1) % NQSEM
            sem_key = (stream, idx)
            prev_uses = self.quse[stream][idx]
            waits = self._dep_waits(eng_stream + "#dma", reads, writes)
            if prev_uses > 0:
                key = (eng_stream + "#dma", sem_key)
                if self.wm.get(key, -1) < 16 * prev_uses:
                    waits.append((sem_key, 16 * prev_uses))
                    self.wm[key] = 16 * prev_uses
            self.quse[stream][idx] = prev_uses + 1
            my = (sem_key, 16 * (prev_uses + 1))
            rec = dict(stream=stream, sem_key=sem_key, fn=fn, waits=waits,
                       inc=True, tag=tag)
            self.ops[eng_stream].append(rec)
        else:
            waits = self._dep_waits(stream, reads, writes)
            if inc:
                self.sem_n[stream] += 1
            seq = self.sem_n[stream]
            my = (stream, seq)
            rec = dict(stream=stream, sem_key=stream, fn=fn, waits=waits,
                       inc=inc, tag=tag)
            self.ops[stream].append(rec)
        for w in writes:
            b = self.buf(w)
            b.writer = my
            b.readers = []
        for r in reads:
            self.buf(r).readers.append(my)
        return rec

    def op(self, eng, fn, reads=(), writes=(), inc=True, tag=""):
        assert eng in ENGS
        return self._record(eng, fn, list(reads), list(writes), inc, tag)

    def dma(self, queue, out_ap_fn, reads=(), writes=(), tag=""):
        assert queue in QUEUES
        return self._record(queue, out_ap_fn, list(reads), list(writes), True, tag)

    def barrier_read(self, eng, reads, tag="bar"):
        self._record(eng, None, list(reads), [], False, tag)

    def build(self, block):
        nc = self.nc
        sems = {}
        for s in ENGS:
            sems[s] = nc.semaphore(f"sem_{s}").__enter__()
        for q in QUEUES:
            for i in range(NQSEM):
                sems[(q, i)] = nc.semaphore(f"sem_{q}{i}").__enter__()
        self._sems = sems
        eng_map = {
            "PE": ("tensor", nc.tensor),
            "ACT": ("scalar", nc.scalar),
            "DVE": ("vector", nc.vector),
            "POOL": ("gpsimd", nc.gpsimd),
            "SP": ("sync", nc.sync),
        }
        for ename, (attr, handle) in eng_map.items():
            oplist = self.ops[ename]
            if not oplist:
                continue
            def body(e, oplist=oplist, ename=ename):
                for rec in oplist:
                    for sk, val in rec["waits"]:
                        e.wait_ge(sems[sk], val)
                    if rec["fn"] is None:
                        continue
                    try:
                        ins = rec["fn"](e)
                    except Exception:
                        import traceback
                        print("FAILING OP tag:", rec.get("tag"))
                        traceback.print_exc()
                        raise
                    if rec["inc"]:
                        amt = 16 if rec["stream"] in QUEUES else 1
                        if ins is not None:
                            ins.then_inc(sems[rec["sem_key"]], amt)
                        else:
                            e.nop().then_inc(sems[rec["sem_key"]], amt)
            getattr(block, attr)(body)
        return sems


def join(self):
    targets = [(e, self.sem_n[e]) for e in ENGS if self.sem_n[e] > 0]
    for q in QUEUES:
        for i in range(NQSEM):
            u = self.quse[q][i]
            if u > 0:
                targets.append(((q, i), 16 * u))
    for e in ENGS:
        waits = []
        for sk, val in targets:
            if sk == e:
                continue
            key = (e, sk)
            if self.wm.get(key, -1) >= val:
                continue
            self.wm[key] = val
            waits.append((sk, val))
        key2 = e + "#dma"
        for sk, val in targets:
            if self.wm.get((key2, sk), -1) < val:
                self.wm[(key2, sk)] = val
        if waits:
            self.ops[e].append(dict(stream=e, sem_key=e, fn=None,
                                    waits=waits, inc=False, tag="join"))
    for b in self.bufs.values():
        b.writer = None
        b.readers = []

K.join = join


from concourse.bass_core_rust import IndirectOffsetOnAxis
import ml_dtypes
"""Full per-core SPMD program for mk_recurrent_d1 (island design).

One bass program runs on all 8 cores; per-core behavior differs only through
input values (weight slices, meta scalars, shared-slot id, nonce).
Cores {0,1}=batch0 island, {2,3}=batch1 island, {4..7} duplicates (ignored).
Within an island: core j owns d_inner half j for mamba (full sequence), and
token half j of the batch for MoE + final combine.
"""

BIG = float(1 << 22)


def build_core(c: Cfg, debug=False, detect_races=True):
    nc = bass.Bass("TRN2", target_bir_lowering=False, debug=debug,
                   detect_race_conditions=detect_races)
    NTOK = c.NB * c.T
    Dh, GH, MT, TM, NM = c.Dh, c.GH, c.MT, c.TMAC, c.NMAC
    TCH, SOCT, NOCT = c.TCH, c.SOCT, c.NOCT
    NTC = c.T // c.TCH
    XW = c.R + 2 * c.S
    S = c.S
    NTIL = NTOK // 128
    NMSG0 = c.NL * (c.T // c.TMAC) * 2 + 8
    NOWN = c.TOK_OWN // 128
    NCH = c.NCHUNK
    FB = c.F // 512
    FC = c.F // 128
    CAP = c.CAP
    RCH = min(2048, NTOK // 4)

    dram = lambda n, sh, dt, **kw: nc.dram_tensor(n, sh, dt, **kw).ap()
    x_full = dram("x_full", [NTOK, c.M], F32, kind="ExternalInput")
    Wi_sl = dram("Wi_sl", [c.NL, 2 * Dh, c.M], F32, kind="ExternalInput")
    convw = dram("convw", [c.NL, Dh, c.DCONV], F32, kind="ExternalInput")
    convb = dram("convb", [c.NL, Dh], F32, kind="ExternalInput")
    Wx_sl = dram("Wx_sl", [c.NL, XW, Dh], F32, kind="ExternalInput")
    Wdt_sl = dram("Wdt_sl", [c.NL, Dh, c.R], F32, kind="ExternalInput")
    bdt_sl = dram("bdt_sl", [c.NL, Dh], F32, kind="ExternalInput")
    Alog_sl = dram("Alog_sl", [c.NL, Dh, S], F32, kind="ExternalInput")
    Dp_sl = dram("Dp_sl", [c.NL, Dh], F32, kind="ExternalInput")
    Wo_sl = dram("Wo_sl", [c.NL, c.M, Dh], F32, kind="ExternalInput")
    lnw_i = dram("lnw_i", [c.NL, c.M], F32, kind="ExternalInput")
    lnb_i = dram("lnb_i", [c.NL, c.M], F32, kind="ExternalInput")
    wg_i = dram("wg_i", [c.M, c.E], F32, kind="ExternalInput")
    w1_i = dram("w1_i", [c.E, c.M, c.F], F32, kind="ExternalInput")
    b1_i = dram("b1_i", [c.E, c.F], F32, kind="ExternalInput")
    w2_i = dram("w2_i", [c.E, c.F, c.M], F32, kind="ExternalInput")
    b2_i = dram("b2_i", [c.E, c.M], F32, kind="ExternalInput")
    ident_f = dram("ident_f", [128, 128], F32, kind="ExternalInput")
    ident_b = dram("ident_b", [128, 128], BF16, kind="ExternalInput")
    iotaNCH_i = dram("iotaNCH_i", [NCH, 1], F32, kind="ExternalInput")
    selmat = dram("selmat", [16, c.S * 128], F32, kind="ExternalInput")
    iota_r = dram("iota_r", [1, NTOK], F32, kind="ExternalInput")
    iota_c = dram("iota_c", [128, 1], F32, kind="ExternalInput")
    iota8_r = dram("iota8_r", [1, 8], F32, kind="ExternalInput")
    k128_r = dram("k128_r", [1, NCH + 1], F32, kind="ExternalInput")
    trie_i = dram("trie_i", [8, 8], F32, kind="ExternalInput")
    ones8 = dram("ones8", [1, 8], F32, kind="ExternalInput")
    metaf_i = dram("metaf_i", [1, 16], F32, kind="ExternalInput")
    metai_i = dram("metai_i", [1, 16], I32, kind="ExternalInput")
    nonce_i = dram("nonce_i", [1, 16], U32, kind="ExternalInput")
    xsend_i = dram("xsend_i", [128, c.NL * NM], I32, kind="ExternalInput")
    wsend_i = dram("wsend_i", [128, c.NL * NM * MT], I32, kind="ExternalInput")
    fsend_i = dram("fsend_i", [2, NMSG0], I32, kind="ExternalInput")
    fidx_i = dram("fidx_i", [128, c.TOK_OWN // 128], I32, kind="ExternalInput")
    selmsk_i = dram("selmsk_i", [1, 16], F32, kind="ExternalInput")

    out_own = dram("out_own", [c.TOK_OWN, c.M], F32, kind="ExternalOutput")

    xT_d = dram("xT_d", [c.M, NTOK], F32, kind="Internal")
    xs_d = dram("xs_d", [c.M, c.T], F32, kind="Internal")
    u_d = dram("u_d", [Dh, c.T], F32, kind="Internal")
    bc_d = dram("bc_d", [2 * S, c.T], F32, kind="Internal")
    dt_d = dram("dt_d", [Dh, c.T], F32, kind="Internal")
    sz_d = dram("sz_d", [Dh, c.T], F32, kind="Internal")
    y_d = dram("y_d", [Dh, c.T], F32, kind="Internal")
    mu_d = dram("mu_d", [NTIL, 128], F32, kind="Internal")
    rstd_d = dram("rstd_d", [NTIL, 128], F32, kind="Internal")
    g1c_d = dram("g1c_d", [NTOK, 1], F32, kind="Internal")
    keptc_d = dram("keptc_d", [NTOK, 1], F32, kind="Internal")
    idxr_d = dram("idxr_d", [NTIL, 128], F32, kind="Internal")
    rankc_d = dram("rankc_d", [NTOK, 1], F32, kind="Internal")
    idxc_d = dram("idxc_d", [NTOK, 1], F32, kind="Internal")
    cnts4_d = dram("cnts4_d", [8, 4], F32, kind="Internal")
    toklist_d = dram("toklist_d", [c.E * c.TOK_OWN, 1], I32, kind="Internal")
    chtab_d = dram("chtab_d", [NCH, 2], F32, kind="Internal")
    gbuf_d = dram("gbuf_d", [c.TOK_OWN, c.M], F32, kind="Internal")
    w1bf_d = dram("w1bf_d", [c.E, c.M, c.F], BF16, kind="Internal")
    h1_d = dram("h1_d", [NCH * 128, c.F], BF16, kind="Internal")
    w2bf_d = dram("w2bf_d", [c.E, c.F, c.M], BF16, kind="Internal")
    b1col_d = dram("b1col_d", [c.E, 128, FC], F32, kind="Internal")
    b2col_d = dram("b2col_d", [c.E, 128, MT], F32, kind="Internal")

    dbg = {}
    if c.dbg:
        for n, sh in [("dbg_u", [Dh, c.T]), ("dbg_dt", [Dh, c.T]),
                      ("dbg_sz", [Dh, c.T]), ("dbg_y", [Dh, c.T]),
                      ("dbg_x1", [c.M, c.T]), ("dbg_BC", [2 * S, c.T]),
                      ("dbg_route", [8, NTOK]), ("dbg_g", [c.TOK_OWN, c.M]),
                      ("dbg_cht", [NCH, 2])]:
            dbg[n] = dram(n, sh, F32, kind="ExternalOutput")

    shp_x = dram("shp_x", [2 * c.NL * NM * 128, TM], F32, kind="Internal",
                 addr_space="Shared")
    shp_w = dram("shp_w", [2 * c.NL * NM * c.M, TM], F32, kind="Internal",
                 addr_space="Shared")
    NMSG = c.NL * NM * 2 + 8
    shf = dram("shf", [2 * NMSG, 16], U32, kind="Internal", addr_space="Shared")

    _backer = nc.alloc_sbuf_tensor("arena_backer", [128, 1], F32)
    SB_BASE = nc.lookup_mloc(_backer).addr
    SB_BUDGET = 192 * 1024 - SB_BASE - 4096
    nc.alloc_sbuf_tensor("arena_backer2", [128, (SB_BUDGET - 32) // 4], F32)
    sb_off = [0]

    def sbuf(name, sh, dt, at=None):
        per = int(np.prod(sh[1:])) * (2 if dt in (BF16,) else 4)
        if dt in (I32, U32, F32):
            per = int(np.prod(sh[1:])) * 4
        if at is None:
            at = sb_off[0]
            sb_off[0] = at + ((per + 31) // 32) * 32
        assert at + per <= SB_BUDGET, f"SBUF overflow at {name}"
        h = nc.alloc_sbuf_tensor_at(name, sh, dt, offset=SB_BASE + at)
        return h.ap()

    ident_fs = sbuf("ident_fs", [128, 128], F32)
    metaf_s = sbuf("metaf_s", [1, 16], F32)
    metai_s = sbuf("metai_s", [1, 16], I32)
    nonce_s = sbuf("nonce_s", [1, 16], U32)
    iota_cs = sbuf("iota_cs", [128, 1], F32)
    iota8_s = sbuf("iota8_s", [1, 8], F32)
    k128_s = sbuf("k128_s", [1, NCH + 1], F32)
    trie_s = sbuf("trie_s", [8, 8], F32)
    ones8_s = sbuf("ones8_s", [1, 8], F32)
    onescol = sbuf("onescol", [128, 1], F32)
    carry = sbuf("carry", [128, GH, S], F32)
    flagt = sbuf("flagt", [1, 16], U32)
    tok0b = sbuf("tok0b", [128, 1], F32)
    A_s = sbuf("A_s", [128, GH, S], F32)
    cb_s = sbuf("cb_s", [128, GH], F32)
    bdt_s = sbuf("bdt_s", [128, GH], F32)
    Dp_s = sbuf("Dp_s", [128, GH], F32)
    lnw_s = sbuf("lnw_s", [128, MT], F32)
    lnb_s = sbuf("lnb_s", [128, MT], F32)
    WxT = sbuf("WxT", [128, GH, XW], BF16)
    WdtT = sbuf("WdtT", [c.R, Dh], BF16)
    wgs = sbuf("wgs", [128, MT, 8], F32)
    csr = sbuf("csr", [8, 1], F32)
    ident_bs = sbuf("ident_bs", [128, 128], BF16)
    iotaNCH_s = sbuf("iotaNCH_s", [NCH, 1], F32)
    cntc = sbuf("cntc", [8, 1], F32)
    ccumc = sbuf("ccumc", [8, 1], F32)
    ccrow = sbuf("ccrow", [1, 8], F32)
    ccrep = sbuf("ccrep", [NCH, 8], F32)
    cmpc = sbuf("cmpc", [NCH, 8], F32)
    cci = sbuf("cci", [8, 1], I32)
    wld_f = sbuf("wld_f", [128, max(c.M, Dh, 512)], F32)
    wld_b = sbuf("wld_b", [128, max(c.M, Dh, 512)], BF16)
    xsend_s = sbuf("xsend_s", [128, c.NL * NM], I32)
    wsend_s = sbuf("wsend_s", [128, c.NL * NM * MT], I32)
    fsend_s = sbuf("fsend_s", [2, NMSG0], I32)
    nonce2 = sbuf("nonce2", [2, 16], U32)
    fidx_s = sbuf("fidx_s", [128, c.TOK_OWN // 128], I32)
    selmsk_s = sbuf("selmsk_s", [1, 16], F32)
    srtab = sbuf("srtab", [128, NCH], I32)
    cnt8 = sbuf("cnt8", [8, NCH + 1], F32)
    colsc = sbuf("colsc", [128, 16], F32)
    ch_sb = sbuf("ch_sb", [NCH, 4], F32)
    chi_sb = sbuf("chi_sb", [NCH, 4], I32)
    cci2 = sbuf("cci2", [8, 1], I32)
    PERSIST_END = sb_off[0] + 2048

    a32 = lambda x: (int(x) + 31) // 32 * 32
    f4 = lambda *d: a32(int(np.prod(d)) * 4)

    W_BASE = PERSIST_END
    o = 0
    WiT = sbuf("WiT", [128, MT, 2 * Dh], BF16, at=W_BASE + o)
    o += a32(MT * 2 * Dh * 2)
    WoT = sbuf("WoT", [128, GH, c.M], BF16, at=W_BASE + o)
    o += a32(GH * c.M * 2)
    sel_s = sbuf("sel_s", [16, c.S * 128], F32, at=W_BASE + o)
    o += f4(c.S * 128)
    dconv_s = sbuf("dconv_s", [128, GH * c.DCONV, 128], BF16, at=W_BASE + o)
    o += a32(GH * c.DCONV * 128 * 2)
    WM_END = o
    w1e = sbuf("w1e", [128, MT, c.F], BF16, at=W_BASE)
    w2e = sbuf("w2e", [128, FC, c.M], BF16, at=W_BASE)
    WMOE_END = max(a32(MT * c.F * 2), a32(FC * c.M * 2))

    o = 0
    P0 = W_BASE
    ptile = sbuf("ptile", [128, c.M], F32, at=P0 + o); o += f4(c.M)
    psq = sbuf("psq", [128, c.M], F32, at=P0 + o); o += f4(c.M)
    pxT = sbuf("pxT", [128, c.M], F32, at=P0 + o); o += f4(c.M)
    mu_all = sbuf("mu_all", [128, NTIL], F32, at=P0 + o); o += f4(NTIL)
    sx2_all = sbuf("sx2_all", [128, NTIL], F32, at=P0 + o); o += f4(NTIL)
    rstd_all = sbuf("rstd_all", [128, NTIL], F32, at=P0 + o); o += f4(NTIL)
    lgw = sbuf("lgw", [128, NTIL, 8], F32, at=P0 + o); o += f4(NTIL, 8)
    lg8 = sbuf("lg8", [8, 128], F32, at=P0 + o); o += f4(128)
    scr1 = sbuf("scr1", [128, NTIL, 8], F32, at=P0 + o); o += f4(NTIL, 8)
    scr2 = sbuf("scr2", [128, NTIL], F32, at=P0 + o); o += f4(NTIL)
    scr3 = sbuf("scr3", [128, NTIL], F32, at=P0 + o); o += f4(NTIL)
    tr_s = sbuf("tr_s", [128, 128], F32, at=P0 + o); o += f4(128)
    mask1T = sbuf("mask1T", [8, RCH], F32, at=P0 + o); o += f4(RCH)
    locT = sbuf("locT", [8, RCH], F32, at=P0 + o); o += f4(RCH)
    rowA = sbuf("rowA", [1, RCH], F32, at=P0 + o); o += f4(RCH)
    rowB = sbuf("rowB", [1, RCH], F32, at=P0 + o); o += f4(RCH)
    carry8 = sbuf("carry8", [8, 1], F32, at=P0 + o); o += 32
    rank8 = sbuf("rank8", [8, 1], F32, at=P0 + o); o += 32
    P_END = o

    if debug or True:
        print(f"[layout] SB_BUDGET={SB_BUDGET} PERSIST={PERSIST_END} "
              f"WM={WM_END} WMOE={WMOE_END} P={P_END}")
    AR = W_BASE + max(WM_END, P_END)

    def arena(name, sh, dt, off):
        return sbuf(name, sh, dt, at=AR + off)

    o = 0
    xmac = arena("xmac", [128, MT, TM], F32, o); o += f4(MT, TM)
    sq_a = arena("sq_a", [128, MT, TM], F32, o)
    sp1 = sq_a
    o += f4(MT, TM)
    xnmac = arena("xnmac", [128, MT, TM], BF16, o); o += a32(MT * TM * 2)
    xs_sb = []
    for p in range(2):
        xs_sb.append(arena(f"xs_sb{p}", [128, GH, c.DCONV - 1 + TM], BF16, o))
        o += a32(GH * (c.DCONV - 1 + TM) * 2)
    u_sb = arena("u_sb", [128, GH, TM], F32, o); o += f4(GH, TM)
    ub_sb = arena("ub_sb", [128, GH, TM], BF16, o); o += a32(GH * TM * 2)
    sz_sb = arena("sz_sb", [128, GH, TM], F32, o); o += f4(GH, TM)
    dtm_sb = arena("dtm_sb", [128, GH, TM], F32, o); o += f4(GH, TM)
    sp2 = arena("sp2", [128, GH, TM], F32, o); o += f4(GH, TM)
    xd_mine = arena("xd_mine", [128, 2, TM], F32, o); o += f4(2, TM)
    xd_sum = arena("xd_sum", [128, TM], F32, o); o += f4(TM)
    xd_two = arena("xd_two", [128, 2, TM], F32, o); o += f4(2, TM)
    mrow = arena("mrow", [1, TM], F32, o); o += f4(TM)
    rrow = arena("rrow", [1, TM], F32, o); o += f4(TM)
    vrow = arena("vrow", [1, TM], F32, o); o += f4(TM)
    A_END = o
    o = 0
    brep = arena("brep", [128, SOCT * TCH], F32, o); o += f4(SOCT * TCH)
    crep = arena("crep", [128, SOCT * TCH], F32, o); o += f4(SOCT * TCH)
    dAc = arena("dAc", [128, SOCT * TCH], F32, o); o += f4(SOCT * TCH)
    Xc = arena("Xc", [128, SOCT * TCH], F32, o); o += f4(SOCT * TCH)
    dtg = arena("dtg", [128, GH, TCH], F32, o); o += f4(GH, TCH)
    ug = arena("ug", [128, GH, TCH], F32, o); o += f4(GH, TCH)
    yacc = arena("yacc", [128, GH, TCH], F32, o); o += f4(GH, TCH)
    ytm = arena("ytm", [128, TCH], F32, o); o += f4(TCH)
    cfix = arena("cfix", [128, SOCT], F32, o); o += f4(SOCT)
    b_sb = arena("b_sb", [S, TCH], F32, o); o += f4(TCH)
    c_sb = arena("c_sb", [S, TCH], F32, o); o += f4(TCH)
    B_END = o
    o = 0
    ymac = arena("ymac", [128, GH, TM], F32, o); o += f4(GH, TM)
    uc_sb = arena("uc_sb", [128, GH, TM], F32, o); o += f4(GH, TM)
    szc_sb = arena("szc_sb", [128, GH, TM], F32, o); o += f4(GH, TM)
    ybf = arena("ybf", [128, GH, TM], BF16, o); o += a32(GH * TM * 2)
    womine = arena("womine", [128, MT, 2, TM], F32, o); o += f4(MT, 2, TM)
    wopart = arena("wopart", [128, MT, TM], F32, o); o += f4(MT, TM)
    xcur = arena("xcur", [128, MT, 2, TM], F32, o); o += f4(MT, 2, TM)
    wop2 = arena("wop2", [128, MT, TM], F32, o); o += f4(MT, TM)
    C_END = o

    AR2 = W_BASE + WMOE_END

    def arena2(name, sh, dt, off):
        return sbuf(name, sh, dt, at=AR2 + off)

    o = 0
    xg = arena2("xg", [128, c.M], F32, o); o += f4(c.M)
    xgsq = arena2("xgsq", [128, c.M], F32, o); o += f4(c.M)
    xnb = arena2("xnb", [128, c.M], BF16, o); o += a32(c.M * 2)
    xnT = arena2("xnT", [128, MT, 128], BF16, o); o += a32(MT * 128 * 2)
    h1b = arena2("h1b", [128, FC, 128], BF16, o); o += a32(FC * 128 * 2)
    g1col = arena2("g1col", [128, 1], F32, o); o += 32
    keptcol = arena2("keptcol", [128, 1], F32, o); o += 32
    tloc = arena2("tloc", [128, 1], F32, o); o += 32
    scat_i = arena2("scat_i", [128, 1], I32, o); o += 32
    tok_i = arena2("tok_i", [128, 1], I32, o); o += 32
    tokf = arena2("tokf", [128, 1], F32, o); o += 32
    eo_sb = arena2("eo_sb", [128, MT, 128], F32, o); o += f4(MT, 128)
    eot = arena2("eot", [128, c.M], F32, o); o += f4(c.M)
    gsc1 = arena2("gsc1", [128, 512], F32, o); o += f4(512)
    gsc2 = arena2("gsc2", [128, 512], F32, o); o += f4(512)
    gsc3 = arena2("gsc3", [128, 512], F32, o); o += f4(512)
    b1c = arena2("b1c", [128, FC], F32, o); o += f4(FC)
    b2c = arena2("b2c", [128, MT], F32, o); o += f4(MT)
    zero_t = arena2("zero_t", [128, 512], F32, o); o += f4(512)
    bigi = arena2("bigi", [128, 512], I32, o); o += f4(512)
    M_END = o
    o = 0
    fs_sb = arena2("fs_sb", [128, MT, 128], F32, o); o += f4(MT, 128)
    fsT = arena2("fsT", [128, c.M], F32, o); o += f4(c.M)
    fx = arena2("fx", [128, c.M], F32, o); o += f4(c.M)
    fg = arena2("fg", [128, c.M], F32, o); o += f4(c.M)
    fsl2 = arena2("fsl2", [128, MT, 128], F32, o); o += f4(MT, 128)
    F_END = o

    need = max(AR + max(A_END, B_END, C_END), AR2 + max(M_END, F_END),
               W_BASE + max(WM_END, WMOE_END, P_END))
    assert need < SB_BUDGET, (
        f"SBUF overflow: {need}B/part (persist {PERSIST_END}, WM {WM_END}, "
        f"WMOE {WMOE_END}, P {P_END}, A {A_END}, B {B_END}, C {C_END}, "
        f"M {M_END}, F {F_END})")
    sb_off[0] = PERSIST_END - 2048

    k = K(nc, c)
    PS = [nc.alloc_psum_tensor(f"ps{i}", [128, 512], F32).ap() for i in range(6)]
    PBF = [nc.alloc_psum_tensor(f"pbf{i}", [128, 1024], BF16).ap() for i in range(2)]
    psn = [f"ps{i}" for i in range(6)]
    pbn = [f"pbf{i}" for i in range(2)]

    sem_poll = nc.semaphore("sem_poll").__enter__()
    sem_w = nc.semaphore("sem_w").__enter__()
    sem_ch = nc.semaphore("sem_ch").__enter__()

    dyn = {}
    bndreg = {}

    def bnd(v):
        if v not in bndreg:
            bndreg[v] = nc.gpsimd.alloc_register(f"bnd{v}")
        return bndreg[v]

    def init_bnds(e):
        for v, r in bndreg.items():
            e.reg_mov(r, v)
        return None

    for _v in (NTOK, c.TOK_OWN, c.E * c.TOK_OWN, 2 * NMSG0,
               2 * c.NL * (c.T // c.TMAC) * 128,
               2 * c.NL * (c.T // c.TMAC) * c.M):
        bnd(_v)

    def stash(engine_name, eng_enum, names_aps):
        def fn(e):
            for nm, ap, lo, hi in names_aps:
                dyn[(engine_name, nm)] = nc.values_load(
                    ap, engines=[eng_enum],
                    min_val=lo, max_val=hi, skip_runtime_bounds_check=True)
            return None
        return fn

    ET = mybir.EngineType

    k.dma("QSP", lambda e: e.dma_start(out=ident_fs, in_=ident_f), writes=["ident"])
    k.dma("QSP", lambda e: e.dma_start(out=ident_bs, in_=ident_b), writes=["identb"])
    k.dma("QSP", lambda e: e.dma_start(out=metaf_s, in_=metaf_i), writes=["metaf"])
    k.dma("QSP", lambda e: e.dma_start(out=metai_s, in_=metai_i), writes=["metai"])
    k.dma("QSP", lambda e: e.dma_start(out=nonce_s, in_=nonce_i), writes=["nonce"])
    k.dma("QSP", lambda e: e.dma_start(out=g0 if False else nonce2[0:1, :],
                                       in_=nonce_i), writes=["nonce2"])
    k.dma("QSP", lambda e: e.dma_start(out=nonce2[1:2, :], in_=nonce_i),
          writes=["nonce2"])
    k.dma("QSP", lambda e: e.dma_start(out=iota_cs, in_=iota_c), writes=["iotac"])
    k.dma("QSP", lambda e: e.dma_start(out=iota8_s, in_=iota8_r), writes=["iota8"])
    k.dma("QSP", lambda e: e.dma_start(out=k128_s, in_=k128_r), writes=["k128"])
    k.dma("QSP", lambda e: e.dma_start(out=trie_s, in_=trie_i), writes=["trie"])
    k.dma("QSP", lambda e: e.dma_start(out=ones8_s, in_=ones8), writes=["ones8"])
    k.dma("QSP", lambda e: e.dma_start(
        out=wgs, in_=wg_i.rearrange("(mt p) e -> p mt e", p=128)), writes=["wgs"])
    k.dma("QSP", lambda e: e.dma_start(out=xsend_s, in_=xsend_i), writes=["xsend"])
    k.dma("QSP", lambda e: e.dma_start(out=wsend_s, in_=wsend_i), writes=["wsend"])
    k.dma("QSP", lambda e: e.dma_start(out=fsend_s, in_=fsend_i), writes=["fsend"])
    k.dma("QSP", lambda e: e.dma_start(out=fidx_s, in_=fidx_i), writes=["fidx"])
    k.dma("QSP", lambda e: e.dma_start(out=selmsk_s, in_=selmsk_i), writes=["selmsk"])
    k.op("POOL", init_bnds, writes=["bnds"], tag="initbnd")
    k.op("DVE", lambda e: e.memset(onescol, 1.0), writes=["onescol"])
    cstv = [float(c.EPS), 0.0, 1.0, -1.0, 0.5, -0.5, 0.044715,
            0.7978845608028654, -0.7978845608028654]
    csttile = sbuf("csttile", [128, len(cstv)], F32)
    for ci, cv in enumerate(cstv):
        k.op("DVE", lambda e, ci=ci, cv=cv: e.memset(
            csttile[:, ci:ci + 1], cv), writes=["cstt"])
        nc.const_aps.register(F32, cv, csttile[:, ci:ci + 1])
    k.join()
    k.op("DVE", lambda e: e.memset(zero_t, 0.0), writes=["zero_t"])
    k.op("DVE", lambda e: e.memset(carry, 0.0), writes=["carry"])

    k.op("SP", stash("SP", ET.SP, [
        ("quart", metai_s[0:1, 2:3], 0, 3),
    ]), reads=["metai"], tag="stash_sp")
    onesrow = sbuf("onesrow", [1, 128], F32)
    k.op("DVE", lambda e: e.memset(onesrow, 1.0), writes=["onesrow"])
    k.op("PE", lambda e: e.matmul(PS[0][:, 0:1], onesrow, metaf_s[0:1, 0:1],
                                  start=True, stop=True),
         reads=["onesrow", "metaf"], writes=["ps0"], tag="tok0b")
    k.op("ACT", lambda e: e.copy(tok0b, PS[0][:, 0:1]), reads=["ps0"],
         writes=["tok0b"])
    bfcol = sbuf("bfcol", [128, 1], F32)
    jfcol = sbuf("jfcol", [128, 1], F32)
    k.op("PE", lambda e: e.matmul(PS[0][:, 0:1], onesrow, selmsk_s[0:1, 0:1],
                                  start=True, stop=True),
         reads=["onesrow", "selmsk"], writes=["ps0"], tag="bfb")
    k.op("ACT", lambda e: e.copy(bfcol, PS[0][:, 0:1]), reads=["ps0"],
         writes=["bfcol"])
    k.op("PE", lambda e: e.matmul(PS[0][:, 0:1], onesrow, selmsk_s[0:1, 1:2],
                                  start=True, stop=True),
         reads=["onesrow", "selmsk"], writes=["ps0"], tag="jfb")
    k.op("ACT", lambda e: e.copy(jfcol, PS[0][:, 0:1]), reads=["ps0"],
         writes=["jfcol"])

    for tt in range(NTIL):
        pb = ["ps0", "ps1"][tt % 2]
        pst = PS[0] if tt % 2 == 0 else PS[1]
        k.dma("QSP", lambda e, tt=tt: e.dma_start(
            out=ptile, in_=x_full[tt * 128:(tt + 1) * 128, :]),
            writes=["ptile"], tag=f"xt{tt}")
        k.op("ACT", lambda e: e.activation(
            psq, ptile, AF.Square), reads=["ptile"], writes=["psq"])
        k.op("DVE", lambda e, tt=tt: e.tensor_reduce(
            sx2_all[:, tt:tt + 1].unsqueeze(2), psq.unsqueeze(1), AX.X, AL.add),
            reads=["psq"], writes=["sx2c"])
        k.op("DVE", lambda e, tt=tt: e.tensor_reduce(
            mu_all[:, tt:tt + 1].unsqueeze(2), ptile.unsqueeze(1), AX.X, AL.add),
            reads=["ptile"], writes=["muc"])
        for mt in range(MT):
            k.op("PE", lambda e, mt=mt, pst=pst: e.transpose(
                pst[:, 0:128], ptile[:, mt * 128:(mt + 1) * 128], ident_fs),
                reads=["ptile", "ident"], writes=[pb], tag=f"tr{tt}_{mt}")
            k.op("ACT", lambda e, mt=mt, pst=pst: e.copy(
                pxT[:, mt * 128:(mt + 1) * 128], pst[:, 0:128]),
                reads=[pb], writes=["pxT"])
        k.dma("QSP", lambda e, tt=tt: e.dma_start(
            out=xT_d[:, tt * 128:(tt + 1) * 128].rearrange(
                "(mt p) t -> p mt t", p=128),
            in_=pxT.rearrange("p (mt t) -> p mt t", t=128)),
            reads=["pxT"], writes=["xT_d"], tag=f"xts{tt}")
        def lgmm(e, tt=tt):
            last = None
            for mt in range(MT):
                last = e.matmul(PS[2][0:8, 0:128], wgs[:, mt, :],
                                pxT[:, mt * 128:(mt + 1) * 128],
                                start=(mt == 0), stop=(mt == MT - 1))
            return last
        k.op("PE", lgmm, reads=["pxT", "wgs"], writes=["ps2"], tag=f"lg{tt}")
        k.op("ACT", lambda e: e.copy(lg8, PS[2][0:8, 0:128]), reads=["ps2"],
             writes=["lg8"])
        k.op("PE", lambda e: e.transpose(PS[3][:, 0:8], lg8[0:8, :],
            ident_fs[0:8, 0:8]), reads=["lg8", "ident"],
            writes=["ps3"], tag=f"lgt{tt}")
        k.op("ACT", lambda e, tt=tt: e.copy(
            lgw[:, tt, :], PS[3][:, 0:8]), reads=["ps3"], writes=["lgw"])

    k.op("DVE", lambda e: e.tensor_scalar(mu_all, mu_all, 1.0 / c.M, 0.0,
                                          AL.mult, AL.add),
         reads=["muc"], writes=["mu_all"])
    k.op("DVE", lambda e: e.tensor_scalar(sx2_all, sx2_all, 1.0 / c.M, 0.0,
                                          AL.mult, AL.add),
         reads=["sx2c"], writes=["sx2_all"])
    k.op("DVE", lambda e: e.tensor_mul(scr2, mu_all, mu_all),
         reads=["mu_all"], writes=["scr2"])
    k.op("DVE", lambda e: e.tensor_sub(scr3, sx2_all, scr2),
         reads=["sx2_all", "scr2"], writes=["scr3"])
    k.op("ACT", lambda e: e.activation(scr2, scr3, AF.Ln, bias=float(c.EPS)),
         reads=["scr3"], writes=["scr2"])
    k.op("ACT", lambda e: e.activation(rstd_all, scr2, AF.Exp, scale=-0.5),
         reads=["scr2"], writes=["rstd_all"])
    for nm, src, dst in [("muT", mu_all, mu_d), ("rsT", rstd_all, rstd_d)]:
        k.op("PE", lambda e, src=src: e.transpose(
            PS[0][0:NTIL, 0:128], src, ident_fs),
             reads=["mu_all" if src is mu_all else "rstd_all", "ident"],
             writes=["ps0"], tag=nm)
        k.op("ACT", lambda e: e.copy(tr_s[0:NTIL, :], PS[0][0:NTIL, 0:128]),
             reads=["ps0"], writes=["tr_s"])
        k.dma("QSP", lambda e, dst=dst: e.dma_start(out=dst, in_=tr_s[0:NTIL, :]),
              reads=["tr_s"], writes=["mu_d" if dst is mu_d else "rstd_d"],
              tag=nm + "s")

    def csmm(e):
        last = None
        for mt in range(MT):
            last = e.matmul(PS[2][0:8, 0:1], wgs[:, mt, :], onescol,
                            start=(mt == 0), stop=(mt == MT - 1))
        return last
    k.op("PE", csmm, reads=["wgs", "onescol"], writes=["ps2"], tag="cs")
    k.op("ACT", lambda e: e.copy(csr, PS[2][0:8, 0:1]),
         reads=["ps2"], writes=["csr"])
    csrep = sbuf("csrep", [128, 8], F32)
    k.op("PE", lambda e: e.transpose(PS[3][0:1, 0:8], csr[0:8, 0:1], ident_fs[0:8, 0:8]),
         reads=["csr", "ident"], writes=["ps3"], tag="csT")
    csrow = sbuf("csrow", [1, 8], F32)
    k.op("ACT", lambda e: e.copy(csrow, PS[3][0:1, 0:8]), reads=["ps3"],
         writes=["csrow"])
    k.op("PE", lambda e: e.matmul(PS[2][:, 0:8], onesrow, csrow,
                                  start=True, stop=True),
         reads=["onesrow", "csrow"], writes=["ps2"], tag="csrep2")
    k.op("ACT", lambda e: e.copy(csrep, PS[2][:, 0:8]), reads=["ps2"],
         writes=["csrep"])

    mu_bc = mu_all.unsqueeze(2).broadcast_to([128, NTIL, 8])
    rs_bc = rstd_all.unsqueeze(2).broadcast_to([128, NTIL, 8])
    cs_bc = csrep.unsqueeze(1).broadcast_to([128, NTIL, 8])
    k.op("DVE", lambda e: e.tensor_mul(scr1, mu_bc, cs_bc),
         reads=["mu_all", "csrep"], writes=["scr1"])
    k.op("DVE", lambda e: e.tensor_sub(lgw, lgw, scr1),
         reads=["scr1"], writes=["lgw"])
    k.op("DVE", lambda e: e.tensor_mul(lgw, lgw, rs_bc),
         reads=["rstd_all"], writes=["lgw"])
    k.op("DVE", lambda e: e.tensor_reduce(scr2.unsqueeze(2),
                                          lgw.rearrange("p n e -> p n e"),
                                          AX.X, AL.max),
         reads=["lgw"], writes=["scr2"])
    mx_bc = scr2.unsqueeze(2).broadcast_to([128, NTIL, 8])
    k.op("DVE", lambda e: e.tensor_sub(scr1, lgw, mx_bc),
         reads=["lgw", "scr2"], writes=["scr1"])
    k.op("ACT", lambda e: e.activation(scr1, scr1, AF.Exp), writes=["scr1e"],
         reads=["scr1"])
    k.op("DVE", lambda e: e.tensor_reduce(scr3.unsqueeze(2), scr1, AX.X, AL.add),
         reads=["scr1e"], writes=["scr3"])
    k.op("DVE", lambda e: e.reciprocal(scr3, scr3), reads=["scr3"],
         writes=["g1a"])
    io8_bc = iota8_s
    io8rep = sbuf("io8rep", [128, 8], F32)
    k.op("PE", lambda e: e.matmul(PS[2][:, 0:8], onesrow, iota8_s,
                                  start=True, stop=True),
         reads=["onesrow", "iota8"], writes=["ps2"], tag="io8rep")
    k.op("ACT", lambda e: e.copy(io8rep, PS[2][:, 0:8]), reads=["ps2"],
         writes=["io8rep"])
    io8_bc3 = io8rep.unsqueeze(1).broadcast_to([128, NTIL, 8])
    k.op("DVE", lambda e: e.tensor_tensor(scr1, lgw, mx_bc, AL.is_ge),
         reads=["lgw", "scr2"], writes=["scr1"])
    k.op("DVE", lambda e: e.scalar_tensor_tensor(scr1, scr1, -BIG, io8_bc3,
                                                 AL.mult, AL.add),
         reads=["scr1", "io8rep"], writes=["scr1"])
    k.op("DVE", lambda e: e.tensor_reduce(scr2.unsqueeze(2), scr1, AX.X, AL.min),
         reads=["scr1"], writes=["scr2"])
    k.op("DVE", lambda e: e.tensor_scalar(scr2, scr2, 1.0, BIG, AL.mult, AL.add),
         reads=["scr2"], writes=["idxa"])

    for nm, srcbuf, src, dst, dshape in [
            ("idxT", "idxa", scr2, idxr_d, None),
            ("g1T", "g1a", scr3, g1c_d, None)]:
        k.op("PE", lambda e, src=src: e.transpose(
            PS[0][0:NTIL, 0:128], src, ident_fs),
             reads=[srcbuf, "ident"], writes=["ps0"], tag=nm)
        k.op("ACT", lambda e: e.copy(tr_s[0:NTIL, :], PS[0][0:NTIL, 0:128]),
             reads=["ps0"], writes=["tr_s"])
        if dst is idxr_d:
            k.dma("QSP", lambda e: e.dma_start(out=idxr_d, in_=tr_s[0:NTIL, :]),
                  reads=["tr_s"], writes=["idxr_d"], tag=nm + "s")
        else:
            k.dma("QSP", lambda e: e.dma_start(
                out=g1c_d.rearrange("(a b) o -> a (b o)", b=128),
                in_=tr_s[0:NTIL, :]),
                reads=["tr_s"], writes=["g1c_d"], tag=nm + "s")

    iotacol8 = sbuf("iotacol8", [8, 1], F32)
    k.dma("QSP", lambda e: e.dma_start(
        out=iotacol8, in_=iota8_r.rearrange("o e -> e o")),
        writes=["iotacol8"], tag="io8c")
    Q4 = NTOK // 4
    CW = min(512, RCH)
    idxflat = idxr_d.rearrange("a b -> (a b)").unsqueeze(0)
    for rc in range(NTOK // RCH):
        r0 = rc * RCH
        qstart = (r0 % Q4) == 0
        qend = ((r0 + RCH) % Q4) == 0
        k.dma("QSP", lambda e, r0=r0: e.dma_start(
            out=rowA, in_=idxflat[0:1, r0:r0 + RCH]),
            reads=["idxr_d"], writes=["rowA"], tag=f"idxrow{rc}")
        for ch in range(RCH // CW):
            pb = ["ps0", "ps1"][ch % 2]
            pst = PS[0] if ch % 2 == 0 else PS[1]
            k.op("PE", lambda e, ch=ch, pst=pst: e.matmul(
                pst[0:8, 0:CW], ones8_s.rearrange("o e -> o e"),
                rowA[0:1, ch * CW:(ch + 1) * CW], start=True, stop=True),
                reads=["rowA", "ones8"], writes=[pb], tag=f"mk{rc}_{ch}")
            k.op("DVE", lambda e, ch=ch, pst=pst: e.tensor_scalar(
                mask1T[:, ch * CW:(ch + 1) * CW], pst[0:8, 0:CW],
                iotacol8[0:8, 0:1], 0.0, AL.is_equal, AL.add),
                reads=[pb, "iotacol8"], writes=["mask1T"], tag=f"mke{rc}_{ch}")
        ones_bc = onescol[0:8, 0:1].broadcast_to([8, RCH])
        k.op("DVE", lambda e, rc=rc: e.tensor_tensor_scan(
            locT, ones_bc, mask1T,
            0.0 if rc == 0 else carry8, AL.mult, AL.add),
            reads=["mask1T", "onescol", "carry8"], writes=["locT"],
            tag=f"locscan{rc}")
        k.op("DVE", lambda e: e.tensor_copy(carry8, locT[:, RCH - 1:RCH]),
             reads=["locT"], writes=["carry8"], tag=f"cs{rc}")
        k.op("DVE", lambda e: e.tensor_mul(locT, mask1T, locT),
             reads=["mask1T"], writes=["locT"])
        for ch in range(RCH // CW):
            pb = ["ps0", "ps1"][ch % 2]
            pst = PS[0] if ch % 2 == 0 else PS[1]
            k.op("PE", lambda e, ch=ch, pst=pst: e.matmul(
                pst[0:1, 0:CW], onescol[0:8, 0:1],
                locT[:, ch * CW:(ch + 1) * CW], start=True, stop=True),
                reads=["locT", "onescol"], writes=[pb], tag=f"lr{rc}_{ch}")
            k.op("ACT", lambda e, ch=ch, pst=pst: e.copy(
                rowB[0:1, ch * CW:(ch + 1) * CW], pst[0:1, 0:CW]),
                reads=[pb], writes=["rowB"])
        k.op("DVE", lambda e: e.tensor_scalar(rowB, rowB, CAP + 0.5, 0.0,
                                              AL.is_lt, AL.add),
             reads=["rowB"], writes=["rowB"])
        k.dma("QSP", lambda e, r0=r0: e.dma_start(
            out=keptc_d[r0:r0 + RCH].rearrange("a b -> b a"), in_=rowB),
            reads=["rowB"], writes=["keptc_d"], tag=f"keptst{rc}")
        k.op("DVE", lambda e, qstart=qstart: e.tensor_tensor_scan(
            locT, ones_bc, mask1T,
            0.0 if qstart else rank8, AL.mult, AL.add),
            reads=["mask1T", "onescol", "rank8"], writes=["locT"],
            tag=f"rk{rc}")
        k.op("DVE", lambda e: e.tensor_copy(rank8, locT[:, RCH - 1:RCH]),
             reads=["locT"], writes=["rank8"], tag=f"rks{rc}")
        if qend:
            q = r0 // Q4
            k.op("DVE", lambda e, q=q: e.tensor_copy(cnt8[:, q:q + 1], rank8),
                 reads=["rank8"], writes=["cnt8q"], tag=f"cq{rc}")
        k.op("DVE", lambda e: e.tensor_mul(locT, mask1T, locT),
             reads=["mask1T"], writes=["locT"])
        for ch in range(RCH // CW):
            pb = ["ps0", "ps1"][ch % 2]
            pst = PS[0] if ch % 2 == 0 else PS[1]
            k.op("PE", lambda e, ch=ch, pst=pst: e.matmul(
                pst[0:1, 0:CW], onescol[0:8, 0:1],
                locT[:, ch * CW:(ch + 1) * CW], start=True, stop=True),
                reads=["locT", "onescol"], writes=[pb], tag=f"rr{rc}_{ch}")
            k.op("ACT", lambda e, ch=ch, pst=pst: e.copy(
                rowB[0:1, ch * CW:(ch + 1) * CW], pst[0:1, 0:CW]),
                reads=[pb], writes=["rowB"])
        k.dma("QSP", lambda e, r0=r0: e.dma_start(
            out=rankc_d[r0:r0 + RCH].rearrange("a b -> b a"), in_=rowB),
            reads=["rowB"], writes=["rankc_d"], tag=f"rankst{rc}")
        if c.dbg:
            k.dma("QSP", lambda e, r0=r0: e.dma_start(
                out=dbg["dbg_route"][0:8, r0:r0 + RCH], in_=mask1T),
                reads=["mask1T"], writes=["dbgroute"], tag=f"dbgr{rc}")
    k.dma("QSP", lambda e: e.dma_start(
        out=cnts4_d, in_=cnt8[0:8, 0:4]),
        reads=["cnt8q"], writes=["cnts4_d"], tag="cntst")
    k.dma("QSP", lambda e: e.dma_start(
        out=idxc_d.rearrange("(a b) o -> a (b o)", b=128),
        in_=idxr_d), reads=["idxr_d"], writes=["idxc_d"], tag="idxcst")

    k.join()

    k.op("SP", stash("SP", ET.SP, [
        ("brow", metai_s[0:1, 3:4], 0, NTOK - c.T),
    ]), reads=["metai"], tag="stash_brow")

    for mm in range(NM):
        k.dma("QSP", lambda e, mm=mm: e.dma_start(
            out=xmac,
            in_=xT_d[:, mm * TM:(mm + 1) * TM].rearrange(
                "(mt p) t -> p mt t", p=128)),
            reads=["xT_d"], writes=["xmac"], tag=f"ox{mm}")
        k.dma("QSP", lambda e, mm=mm: e.dma_start(
            out=sq_a,
            in_=xT_d[:, c.T + mm * TM:c.T + (mm + 1) * TM].rearrange(
                "(mt p) t -> p mt t", p=128)),
            reads=["xT_d"], writes=["sq_a"], tag=f"ox2{mm}")
        k.op("DVE", lambda e: e.tensor_sub(
            sq_a.rearrange("p a b -> p (a b)"),
            sq_a.rearrange("p a b -> p (a b)"),
            xmac.rearrange("p a b -> p (a b)")),
            reads=["xmac", "sq_a"], writes=["sq_a"], tag=f"osel{mm}")
        k.op("DVE", lambda e: e.scalar_tensor_tensor(
            xmac.rearrange("p a b -> p (a b)"),
            sq_a.rearrange("p a b -> p (a b)"), bfcol[:, 0:1],
            xmac.rearrange("p a b -> p (a b)"), AL.mult, AL.add),
            reads=["sq_a", "bfcol", "xmac"], writes=["xmac"], tag=f"osel2{mm}")
        for nmw, dst, srcd in [("mrow", mrow, mu_d), ("rrow", rrow, rstd_d)]:
            k.dma("QSP", lambda e, mm=mm, dst=dst, srcd=srcd: e.dma_start(
                out=dst, in_=srcd.rearrange("a b -> (a b)").unsqueeze(0)[
                    0:1, mm * TM:(mm + 1) * TM]),
                reads=["mu_d" if srcd is mu_d else "rstd_d"], writes=[nmw],
                tag=f"o{nmw}{mm}")
            k.dma("QSP", lambda e, mm=mm, srcd=srcd: e.dma_start(
                out=vrow, in_=srcd.rearrange("a b -> (a b)").unsqueeze(0)[
                    0:1, c.T + mm * TM:c.T + (mm + 1) * TM]),
                reads=["mu_d" if srcd is mu_d else "rstd_d"], writes=["vrow"],
                tag=f"o2{nmw}{mm}")
            k.op("DVE", lambda e, dst=dst: e.tensor_sub(vrow, vrow, dst),
                 reads=[nmw, "vrow"], writes=["vrow"], tag=f"o3{nmw}{mm}")
            k.op("DVE", lambda e, dst=dst: e.scalar_tensor_tensor(
                dst, vrow, selmsk_s[0:1, 0:1], dst, AL.mult, AL.add),
                reads=["vrow", "selmsk"], writes=[nmw], tag=f"o4{nmw}{mm}")
        k.op("PE", lambda e: e.matmul(PS[4][:, 0:TM], onesrow, mrow,
                                      start=True, stop=True),
             reads=["onesrow", "mrow"], writes=["ps4"], tag=f"obm{mm}")
        k.op("PE", lambda e: e.matmul(PS[5][:, 0:TM], onesrow, rrow,
                                      start=True, stop=True),
             reads=["onesrow", "rrow"], writes=["ps5"], tag=f"obr{mm}")
        for mt in range(MT):
            k.op("DVE", lambda e, mt=mt: e.tensor_sub(
                xmac[:, mt, :], xmac[:, mt, :], PS[4][:, 0:TM]),
                reads=["ps4", "xmac"], writes=["xmac"])
            k.op("DVE", lambda e, mt=mt: e.tensor_mul(
                xmac[:, mt, :], xmac[:, mt, :], PS[5][:, 0:TM]),
                reads=["ps5"], writes=["xmac"])
        k.dma("QSP", lambda e, mm=mm: e.dma_start(
            out=xs_d[:, mm * TM:(mm + 1) * TM].rearrange(
                "(mt p) t -> p mt t", p=128), in_=xmac),
            reads=["xmac"], writes=["xs_d"], tag=f"oxs{mm}")

    env = dict(locals())
    emit_rest(env)
    assert env["sb_off"][0] <= PERSIST_END, (
        f"late autos overflow pad: {env['sb_off'][0]} > {PERSIST_END}")
    with nc.Block() as block:
        k.build(block)
    return env


def emit_rest(env):
    g = env
    c = g["c"]; nc = g["nc"]; k = g["k"]; dyn = g["dyn"]
    PS = g["PS"]; PBF = g["PBF"]
    Dh, GH, MT, TM, NM = c.Dh, c.GH, c.MT, c.TMAC, c.NMAC
    TCH, SOCT, NOCT = c.TCH, c.SOCT, c.NOCT
    NTC = c.T // c.TCH
    S = c.S; XW = c.R + 2 * c.S
    NTOK = c.NB * c.T
    NCH = c.NCHUNK
    FC = c.F // 128
    SUBS = min(4, FC)
    FB = FC // SUBS
    NOWN = c.TOK_OWN // 128
    ET = mybir.EngineType
    ds = bass.ds
    stash = g["stash"]

    L = lambda n: g[n]
    (ident_fs, sel_s, metaf_s, metai_s, nonce_s, onesrow, onescol, flagt,
     carry) = (g["ident_fs"], g["sel_s"], g["metaf_s"],
        g["metai_s"], g["nonce_s"], g["onesrow"], g["onescol"], g["flagt"],
        g["carry"])
    nonce2 = g["nonce2"]
    bnd = g["bnd"]
    (xmac, sq_a, sp1, xnmac, xs_sb, u_sb, ub_sb, sz_sb, dtm_sb, sp2) = (
        g["xmac"], g["sq_a"], g["sp1"], g["xnmac"], g["xs_sb"], g["u_sb"],
        g["ub_sb"], g["sz_sb"], g["dtm_sb"], g["sp2"])
    (mrow, rrow, vrow, xd_mine, xd_sum) = (g["mrow"], g["rrow"], g["vrow"],
        g["xd_mine"], g["xd_sum"])
    xd_two, wop2, srtab = g["xd_two"], g["wop2"], g["srtab"]
    xsend_s, wsend_s, fsend_s, fidx_s, selmsk_s = (g["xsend_s"], g["wsend_s"],
        g["fsend_s"], g["fidx_s"], g["selmsk_s"])
    bfcol = g["bfcol"]
    (WiT, WoT, WxT, WdtT, A_s, cb_s, bdt_s, Dp_s, lnw_s, lnb_s, dconv_s) = (
        g["WiT"], g["WoT"], g["WxT"], g["WdtT"], g["A_s"], g["cb_s"],
        g["bdt_s"], g["Dp_s"], g["lnw_s"], g["lnb_s"], g["dconv_s"])
    (brep, crep, dAc, Xc, dtg, ug, yacc, ytm, cfix, b_sb, c_sb) = (
        g["brep"], g["crep"], g["dAc"], g["Xc"], g["dtg"], g["ug"],
        g["yacc"], g["ytm"], g["cfix"], g["b_sb"], g["c_sb"])
    (ymac, uc_sb, szc_sb, ybf, womine, wopart, xcur) = (g["ymac"], g["uc_sb"],
        g["szc_sb"], g["ybf"], g["womine"], g["wopart"], g["xcur"])
    (xs_d, u_d, dt_d, sz_d, y_d, shp_x, shp_w, shf, bc_d) = (g["xs_d"],
        g["u_d"], g["dt_d"], g["sz_d"], g["y_d"], g["shp_x"], g["shp_w"],
        g["shf"], g["bc_d"])
    (Wi_sl, convw, convb, Wx_sl, Wdt_sl, bdt_sl, Alog_sl, Dp_sl, Wo_sl,
     lnw_i, lnb_i) = (g["Wi_sl"], g["convw"], g["convb"], g["Wx_sl"],
        g["Wdt_sl"], g["bdt_sl"], g["Alog_sl"], g["Dp_sl"], g["Wo_sl"],
        g["lnw_i"], g["lnb_i"])
    ident_bs = g["ident_bs"]
    sem_poll, sem_w, sem_ch = g["sem_poll"], g["sem_w"], g["sem_ch"]
    NMSG = c.NL * NM * 2 + 8
    dbg = g["dbg"]

    flag2 = g["sbuf"]("flag2", [2, 16], U32)
    shf3 = shf.rearrange("(s m) w -> s m w", s=2)

    def poll_fn(msg):
        def fn(e):
            with e.register(f"kp{msg}") as kp, e.register(f"f0{msg}") as f0, \
                 e.register(f"f1{msg}") as f1, e.register(f"nv{msg}") as nv, \
                 e.register(f"c0{msg}") as c0:
                e.reg_load(nv, nonce_s[0:1, 1:2])
                e.reg_mov(kp, 1)
                with e.While(kp):
                    e.sem_clear(sem_poll)
                    e.dma_start(out=flag2,
                                in_=shf3[:, msg, :]).then_inc(sem_poll, 16)
                    e.wait_ge(sem_poll, 16)
                    e.reg_load(f0, flag2[0:1, 0:1])
                    e.reg_load(f1, flag2[1:2, 0:1])
                    e.reg_alu(c0, f0, nv, AL.not_equal)
                    e.reg_alu(kp, f1, nv, AL.not_equal)
                    e.reg_alu(kp, kp, c0, AL.mult)
                e.sem_clear(sem_poll)
            return e.nop()
        return fn

    def send_flag(msg):
        def fn(e):
            return e.indirect_dma_start(
                shf, IndirectOffsetOnAxis(ap=fsend_s[:, msg:msg + 1], axis=0),
                nonce2, None, bounds_check=bnd(2 * NMSG), oob_is_err=False)
        return fn

    def load_col(dst, src_ap, kk, tagp, bufname):
        k.dma("QSP", lambda e: e.dma_start(
            out=g["wld_f"][0:kk, 0:128],
            in_=src_ap.rearrange("(a p) -> a p", p=128)),
            writes=["stgf"], tag=tagp + "L")
        k.op("PE", lambda e: e.transpose(
            PS[0][:, 0:kk], g["wld_f"][0:kk, 0:128],
            ident_fs[0:kk, 0:kk]),
            reads=["stgf", "ident"], writes=["ps0"], tag=tagp + "T")
        k.op("ACT", lambda e: e.copy(dst, PS[0][:, 0:kk]),
             reads=["ps0"], writes=[bufname], tag=tagp + "C")

    def load_layer_weights(l):
        R2 = 2 * Dh // 128
        stg_f = g["wld_f"][:, 0:c.M]
        stg_b = g["wld_b"][:, 0:c.M]
        for rr in range(R2):
            k.dma("QSP", lambda e, l=l, rr=rr: e.dma_start(
                out=stg_f, in_=Wi_sl[l, rr * 128:(rr + 1) * 128, :]),
                writes=["stgf"], tag=f"wl{l}_{rr}")
            k.op("DVE", lambda e: e.tensor_copy(stg_b, stg_f),
                 reads=["stgf"], writes=["stgb"])
            for mt in range(MT):
                pb = ["pbf0", "pbf1"][mt % 2]
                pp = PBF[mt % 2]
                k.op("PE", lambda e, mt=mt, pp=pp: e.transpose(
                    pp[:, 0:128], stg_b[:, mt * 128:(mt + 1) * 128], ident_bs),
                    reads=["stgb", "identb"], writes=[pb], tag=f"wt{l}_{rr}_{mt}")
                k.op("DVE", lambda e, mt=mt, rr=rr, pp=pp: e.tensor_copy(
                    WiT[:, mt, rr * 128:(rr + 1) * 128], pp[:, 0:128]),
                    reads=[pb], writes=["WiT"])
        for mr in range(MT):
            k.dma("QSP", lambda e, l=l, mr=mr: e.dma_start(
                out=g["wld_f"][:, 0:Dh], in_=Wo_sl[l, mr * 128:(mr + 1) * 128, :]),
                writes=["stgf"], tag=f"wo{l}_{mr}")
            k.op("DVE", lambda e: e.tensor_copy(g["wld_b"][:, 0:Dh], g["wld_f"][:, 0:Dh]),
                 reads=["stgf"], writes=["stgb"])
            for gg in range(GH):
                pb = ["pbf0", "pbf1"][gg % 2]
                pp = PBF[gg % 2]
                k.op("PE", lambda e, gg=gg, pp=pp: e.transpose(
                    pp[:, 0:128], g["wld_b"][:, gg * 128:(gg + 1) * 128], ident_bs),
                    reads=["stgb", "identb"], writes=[pb], tag=f"wot{l}{mr}{gg}")
                k.op("DVE", lambda e, gg=gg, mr=mr, pp=pp: e.tensor_copy(
                    WoT[:, gg, mr * 128:(mr + 1) * 128], pp[:, 0:128]),
                    reads=[pb], writes=["WoT"])
        k.dma("QSP", lambda e, l=l: e.dma_start(
            out=g["wld_f"][0:XW, 0:Dh], in_=Wx_sl[l]), writes=["stgf"], tag=f"wx{l}")
        k.op("DVE", lambda e: e.tensor_copy(g["wld_b"][0:XW, 0:Dh], g["wld_f"][0:XW, 0:Dh]),
             reads=["stgf"], writes=["stgb"])
        for gg in range(GH):
            pb = ["pbf0", "pbf1"][gg % 2]
            pp = PBF[gg % 2]
            k.op("PE", lambda e, gg=gg, pp=pp: e.transpose(
                pp[:, 0:XW], g["wld_b"][0:XW, gg * 128:(gg + 1) * 128], ident_bs[0:XW, 0:XW]),
                reads=["stgb", "identb"], writes=[pb], tag=f"wxt{l}{gg}")
            k.op("DVE", lambda e, gg=gg, pp=pp: e.tensor_copy(
                WxT[:, gg, :], pp[:, 0:XW]), reads=[pb], writes=["WxT"])
        for gg in range(GH):
            k.dma("QSP", lambda e, l=l, gg=gg: e.dma_start(
                out=g["wld_f"][:, 0:c.R], in_=Wdt_sl[l, gg * 128:(gg + 1) * 128, :]),
                writes=["stgf"], tag=f"wd{l}{gg}")
            k.op("DVE", lambda e: e.tensor_copy(g["wld_b"][:, 0:c.R], g["wld_f"][:, 0:c.R]),
                 reads=["stgf"], writes=["stgb"])
            pb = ["pbf0", "pbf1"][gg % 2]
            pp = PBF[gg % 2]
            k.op("PE", lambda e, pp=pp: e.transpose(
                pp[0:c.R, 0:128], g["wld_b"][:, 0:c.R], ident_bs),
                reads=["stgb", "identb"], writes=[pb], tag=f"wdt{l}{gg}")
            k.op("DVE", lambda e, gg=gg, pp=pp: e.tensor_copy(
                WdtT[0:c.R, gg * 128:(gg + 1) * 128], pp[0:c.R, 0:128]),
                reads=[pb], writes=["WdtT"])
        k.dma("QSP", lambda e, l=l: e.dma_start(
            out=A_s, in_=Alog_sl[l].rearrange("(g p) s -> p g s", p=128)),
            writes=["Araw"], tag=f"va{l}")
        k.op("ACT", lambda e: e.activation(A_s, A_s, AF.Exp),
             reads=["Araw"], writes=["Aexp"])
        k.op("DVE", lambda e: e.tensor_scalar(A_s, A_s, -1.0, 0.0, AL.mult, AL.add),
             reads=["Aexp"], writes=["A_s"])
        for nm, dst, src in [("cb", cb_s, convb), ("bd", bdt_s, bdt_sl),
                             ("dp", Dp_s, Dp_sl)]:
            load_col(dst, src[l], GH, f"v{nm}{l}", nm)
        for nm, dst, src in [("lw", lnw_s, lnw_i), ("lb", lnb_s, lnb_i)]:
            load_col(dst, src[l], MT, f"v{nm}{l}", nm)
        k.dma("QSP", lambda e, l=l: e.dma_start(
            out=g["wld_f"][:, 0:GH * c.DCONV].rearrange(
                "p (g k) -> p g k", k=c.DCONV),
            in_=convw[l].rearrange("(g p) k -> p g k", p=128)),
            writes=["stgf"], tag=f"cw{l}")
        for gg in range(GH):
            for kk in range(c.DCONV):
                k.op("DVE", lambda e, gg=gg, kk=kk: e.tensor_scalar(
                    dconv_s[:, gg * c.DCONV + kk, :], ident_bs,
                    g["wld_f"][:, gg * c.DCONV + kk:gg * c.DCONV + kk + 1],
                    0.0, AL.mult, AL.add),
                    reads=["stgf", "identb"], writes=["dconv"],
                    tag=f"dc{l}{gg}{kk}")

    def phase_a(l):
        k.op("DVE", lambda e: e.memset(xd_mine, 0.0), writes=["xdm"],
             tag=f"a_z{l}")
        for mm in range(NM + 1):
            cur = mm % 2
            prv = 1 - cur
            if mm < NM:
                k.dma("QSP", lambda e, mm=mm: e.dma_start(
                    out=xmac, in_=xs_d[:, mm * TM:(mm + 1) * TM].rearrange(
                        "(mt p) t -> p mt t", p=128)),
                    writes=["xmac"], tag=f"a_x{mm}")
                def mu_mm(e, mm=mm):
                    last = None
                    for mt in range(MT):
                        last = e.matmul(PS[0][0:1, 0:TM], onescol,
                                        xmac[:, mt, :], start=(mt == 0),
                                        stop=(mt == MT - 1))
                    return last
                k.op("PE", mu_mm, reads=["xmac", "onescol"], writes=["ps0"],
                     tag=f"a_mu{mm}")
                k.op("ACT", lambda e: e.activation(sq_a, xmac, AF.Square),
                     reads=["xmac"], writes=["sq_a"])
                def sq_mm(e):
                    last = None
                    for mt in range(MT):
                        last = e.matmul(PS[1][0:1, 0:TM], onescol,
                                        sq_a[:, mt, :], start=(mt == 0),
                                        stop=(mt == MT - 1))
                    return last
                k.op("PE", sq_mm, reads=["sq_a", "onescol"], writes=["ps1"],
                     tag=f"a_sq{mm}")
                k.op("ACT", lambda e: e.copy(mrow, PS[0][0:1, 0:TM]),
                     reads=["ps0"], writes=["mrow"])
                k.op("ACT", lambda e: e.copy(vrow, PS[1][0:1, 0:TM]),
                     reads=["ps1"], writes=["vrow"])
                k.op("DVE", lambda e: e.tensor_scalar(mrow, mrow, 1.0 / c.M, 0.0,
                                                      AL.mult, AL.add),
                     reads=["mrow"], writes=["mrow"])
                k.op("DVE", lambda e: e.tensor_scalar(vrow, vrow, 1.0 / c.M, 0.0,
                                                      AL.mult, AL.add),
                     reads=["vrow"], writes=["vrow"])
                k.op("DVE", lambda e: e.scalar_tensor_tensor(
                    rrow, mrow, 1.0, mrow, AL.mult, AL.mult),
                    reads=["mrow"], writes=["rrow"])
                k.op("DVE", lambda e: e.tensor_sub(vrow, vrow, rrow),
                     reads=["rrow"], writes=["vrow"])
                k.op("ACT", lambda e: e.activation(vrow, vrow, AF.Ln,
                                                   bias=float(c.EPS)),
                     reads=["vrow"], writes=["vrow"])
                k.op("ACT", lambda e: e.activation(rrow, vrow, AF.Exp, scale=-0.5),
                     reads=["vrow"], writes=["rrow"])
                k.op("PE", lambda e: e.matmul(PS[4][:, 0:TM], onesrow, mrow,
                                              start=True, stop=True),
                     reads=["onesrow", "mrow"], writes=["ps4"], tag=f"a_bm{mm}")
                k.op("PE", lambda e: e.matmul(PS[5][:, 0:TM], onesrow, rrow,
                                              start=True, stop=True),
                     reads=["onesrow", "rrow"], writes=["ps5"], tag=f"a_br{mm}")
                for mt in range(MT):
                    k.op("DVE", lambda e, mt=mt: e.tensor_sub(
                        xmac[:, mt, :], xmac[:, mt, :], PS[4][:, 0:TM]),
                        reads=["ps4"], writes=["xmac"])
                    k.op("DVE", lambda e, mt=mt: e.tensor_mul(
                        xmac[:, mt, :], xmac[:, mt, :], PS[5][:, 0:TM]),
                        reads=["ps5"], writes=["xmac"])
                    k.op("ACT", lambda e, mt=mt: e.activation(
                        xnmac[:, mt, :], xmac[:, mt, :], AF.Identity,
                        bias=lnb_s[:, mt:mt + 1], scale=lnw_s[:, mt:mt + 1]),
                        reads=["xmac", "lw", "lb"], writes=["xnmac"])
                for io in range(2 * GH):
                    pb = ["ps2", "ps3"][io % 2]
                    pp = PS[2 + io % 2]
                    def xz_mm(e, io=io, pp=pp):
                        last = None
                        for mt in range(MT):
                            last = e.matmul(
                                pp[:, 0:TM], WiT[:, mt, io * 128:(io + 1) * 128],
                                xnmac[:, mt, :], start=(mt == 0),
                                stop=(mt == MT - 1))
                        return last
                    k.op("PE", xz_mm, reads=["WiT", "xnmac"], writes=[pb],
                         tag=f"a_xz{mm}_{io}")
                    if io < GH:
                        k.op("ACT", lambda e, io=io, cur=cur, pp=pp: e.activation(
                            xs_sb[cur][:, io, c.DCONV - 1:], pp[:, 0:TM], AF.Copy),
                            reads=[pb], writes=[f"xs{cur}"])
                    else:
                        k.op("ACT", lambda e, io=io, pp=pp: e.activation(
                            sz_sb[:, io - GH, :], pp[:, 0:TM], AF.Silu),
                            reads=[pb], writes=["sz_sb"])
                if mm == 0:
                    k.op("DVE", lambda e, cur=cur: e.memset(
                        xs_sb[cur][:, :, 0:c.DCONV - 1], 0.0),
                        writes=[f"xs{cur}"], tag=f"a_h0{mm}")
                else:
                    for gg in range(GH):
                        k.op("DVE", lambda e, gg=gg, cur=cur, prv=prv:
                             e.tensor_copy(
                                 xs_sb[cur][:, gg, 0:c.DCONV - 1],
                                 xs_sb[prv][:, gg, TM:TM + c.DCONV - 1]),
                             reads=[f"xs{prv}"], writes=[f"xs{cur}"],
                             tag=f"a_h{mm}_{gg}")
                for gg in range(GH):
                    pb = ["ps2", "ps3"][gg % 2]
                    pp = PS[2 + gg % 2]
                    def conv_mm(e, gg=gg, cur=cur, pp=pp):
                        last = None
                        for kk in range(c.DCONV):
                            last = e.matmul(
                                pp[:, 0:TM],
                                dconv_s[:, gg * c.DCONV + kk, :],
                                xs_sb[cur][:, gg, kk:kk + TM],
                                start=(kk == 0), stop=(kk == c.DCONV - 1))
                        return last
                    k.op("PE", conv_mm, reads=[f"xs{cur}", "dconv"], writes=[pb],
                         tag=f"a_cv{mm}_{gg}")
                    k.op("ACT", lambda e, gg=gg, pp=pp: e.activation(
                        u_sb[:, gg, :], pp[:, 0:TM], AF.Silu,
                        bias=cb_s[:, gg:gg + 1]),
                        reads=[pb, "cb"], writes=["u_sb"])
                k.op("DVE", lambda e: e.tensor_copy(ub_sb, u_sb),
                     reads=["u_sb"], writes=["ub_sb"])
                k.dma("QSP", lambda e, mm=mm: e.dma_start(
                    out=u_d[:, mm * TM:(mm + 1) * TM].rearrange(
                        "(g p) t -> p g t", p=128), in_=u_sb),
                    reads=["u_sb"], writes=["u_d"], tag=f"a_us{mm}")
                k.dma("QSP", lambda e, mm=mm: e.dma_start(
                    out=sz_d[:, mm * TM:(mm + 1) * TM].rearrange(
                        "(g p) t -> p g t", p=128), in_=sz_sb),
                    reads=["sz_sb"], writes=["sz_d"], tag=f"a_ss{mm}")
                for (nm, pidx, lo, n) in [("B", 0, c.R, S),
                                          ("C", 1, c.R + S, S),
                                          ("dt", 5, 0, c.R)]:
                    pb = f"ps{pidx}"
                    pp = PS[pidx]
                    def xd_mm(e, lo=lo, n=n, pp=pp):
                        last = None
                        for gg in range(GH):
                            last = e.matmul(
                                pp[0:n, 0:TM], WxT[:, gg, lo:lo + n],
                                ub_sb[:, gg, :], start=(gg == 0),
                                stop=(gg == GH - 1))
                        return last
                    k.op("PE", xd_mm, reads=["WxT", "ub_sb"], writes=[pb],
                         tag=f"a_xd{mm}_{nm}")
                base = {"B": 0, "C": 32, "dt": 64}
                k.op("ACT", lambda e, cur=cur: e.copy(
                    xd_mine[0:S, cur, :], PS[0][0:S, 0:TM]),
                    reads=["ps0"], writes=["xdm"])
                k.op("ACT", lambda e, cur=cur: e.copy(
                    xd_mine[32:32 + S, cur, :], PS[1][0:S, 0:TM]),
                    reads=["ps1"], writes=["xdm"])
                k.op("ACT", lambda e, cur=cur: e.copy(
                    xd_mine[64:64 + c.R, cur, :], PS[5][0:c.R, 0:TM]),
                    reads=["ps5"], writes=["xdm"])
                msg = l * NM + mm
                k.dma("QIND", lambda e, msg=msg, cur=cur: e.indirect_dma_start(
                    shp_x,
                    IndirectOffsetOnAxis(ap=xsend_s[:, msg:msg + 1], axis=0),
                    xd_mine[:, cur, :], None,
                    bounds_check=bnd(2 * c.NL * NM * 128), oob_is_err=False),
                    reads=["xdm", "xsend"], writes=["shsent"], tag=f"a_sx{mm}")
                k.dma("QIND", send_flag(msg), reads=["shsent", "nonce2", "fsend"],
                      writes=["flag_sent"], tag=f"a_sf{mm}")
            if mm >= 1:
                pm = mm - 1
                pcur = pm % 2
                msg = l * NM + pm
                k.op("SP", poll_fn(msg), reads=["flag_sent"], writes=["polled"],
                     tag=f"a_pl{pm}")
                k.dma("QSP", lambda e, msg=msg: e.dma_start(
                    out=xd_two,
                    in_=shp_x.rearrange("(s r) t -> s r t", s=2)[
                        :, msg * 128:(msg + 1) * 128, :].rearrange(
                        "s p t -> p s t")),
                    reads=["polled", "shsent"], writes=["xdp"], tag=f"a_lp{pm}")
                k.op("DVE", lambda e: e.tensor_add(
                    xd_sum, xd_two[:, 0, :], xd_two[:, 1, :]),
                    reads=["xdp"], writes=["xds"])
                k.dma("QACT", lambda e, pm=pm: e.dma_start(
                    out=bc_d[0:S, pm * TM:(pm + 1) * TM], in_=xd_sum[0:S, :]),
                    reads=["xds"], writes=["bc_d"], tag=f"a_bs{pm}")
                k.dma("QACT", lambda e, pm=pm: e.dma_start(
                    out=bc_d[S:2 * S, pm * TM:(pm + 1) * TM],
                    in_=xd_sum[32:32 + S, :]),
                    reads=["xds"], writes=["bc_d"], tag=f"a_cs{pm}")
                k.op("DVE", lambda e: e.tensor_copy(
                    xnmac.rearrange("p a b -> p (a b)")[0:c.R, 0:TM],
                    xd_sum[64:64 + c.R, :]),
                    reads=["xds"], writes=["xnmac"], tag=f"a_dc{pm}")
                for gg in range(GH):
                    pb = ["ps2", "ps3"][gg % 2]
                    pp = PS[2 + gg % 2]
                    k.op("PE", lambda e, gg=gg, pp=pp: e.matmul(
                        pp[:, 0:TM], WdtT[0:c.R, gg * 128:(gg + 1) * 128],
                        xnmac.rearrange("p a b -> p (a b)")[0:c.R, 0:TM],
                        start=True, stop=True),
                        reads=["WdtT", "xnmac"], writes=[pb], tag=f"a_dm{pm}_{gg}")
                    k.op("DVE", lambda e, gg=gg, pp=pp: e.tensor_scalar(
                        sp1[:, gg, :], pp[:, 0:TM], bdt_s[:, gg:gg + 1], 0.0,
                        AL.add, AL.add),
                        reads=[pb, "bd"], writes=["sq_a"], tag=f"a_sp{pm}_{gg}")
                for gg in range(GH):
                    k.op("DVE", lambda e, gg=gg: e.scalar_tensor_tensor(
                        sp2[:, gg, :], sp1[:, gg, :], -1.0, sp1[:, gg, :],
                        AL.mult, AL.max),
                        reads=["sq_a"], writes=["sp2"])
                k.op("ACT", lambda e: e.activation(dtm_sb, sp2, AF.Exp, scale=-1.0),
                     reads=["sp2"], writes=["dtm"])
                k.op("ACT", lambda e: e.activation(sp2, dtm_sb, AF.Ln, bias=1.0),
                     reads=["dtm"], writes=["sp2"])
                for gg in range(GH):
                    k.op("DVE", lambda e, gg=gg: e.scalar_tensor_tensor(
                        dtm_sb[:, gg, :], sp1[:, gg, :], 0.0, sp2[:, gg, :],
                        AL.max, AL.add),
                        reads=["sq_a", "sp2"], writes=["dtm"])
                k.dma("QSP", lambda e, pm=pm: e.dma_start(
                    out=dt_d[:, pm * TM:(pm + 1) * TM].rearrange(
                        "(g p) t -> p g t", p=128), in_=dtm_sb),
                    reads=["dtm"], writes=["dt_d"], tag=f"a_ds{pm}")

    def phase_b(l):
        k.op("DVE", lambda e: e.memset(carry, 0.0), writes=["carry"],
             tag=f"b_c0{l}")
        for tc in range(NTC):
            t0 = tc * TCH
            k.dma("QSP", lambda e, t0=t0: e.dma_start(
                out=dtg, in_=dt_d[:, t0:t0 + TCH].rearrange(
                    "(g p) t -> p g t", p=128)),
                reads=["dt_d"], writes=["dtg"], tag=f"b_ld{tc}")
            k.dma("QSP", lambda e, t0=t0: e.dma_start(
                out=ug, in_=u_d[:, t0:t0 + TCH].rearrange(
                    "(g p) t -> p g t", p=128)),
                reads=["u_d"], writes=["ug"], tag=f"b_lu{tc}")
            k.dma("QSP", lambda e, t0=t0: e.dma_start(
                out=b_sb, in_=bc_d[0:S, t0:t0 + TCH]),
                reads=["bc_d"], writes=["b_sb"], tag=f"b_lb{tc}")
            k.dma("QSP", lambda e, t0=t0: e.dma_start(
                out=c_sb, in_=bc_d[S:2 * S, t0:t0 + TCH]),
                reads=["bc_d"], writes=["c_sb"], tag=f"b_lc{tc}")
            for gg in range(GH):
                k.op("DVE", lambda e, gg=gg: e.tensor_mul(
                    ug[:, gg, :], dtg[:, gg, :], ug[:, gg, :]),
                    reads=["dtg", "ug"], writes=["ug"], tag=f"b_v{tc}_{gg}")
            CCW = min(512, TCH)
            for oc in range(NOCT):
                for sl in range(SOCT):
                    s_g = oc * SOCT + sl
                    for cc in range(TCH // CCW):
                        pb = ["ps0", "ps1"][cc % 2]
                        pp = PS[cc % 2]
                        k.op("PE", lambda e, s_g=s_g, cc=cc, pp=pp:
                             e.matmul(pp[0:128, 0:CCW],
                                      sel_s[:, s_g * 128:(s_g + 1) * 128],
                                      b_sb[:, cc * CCW:(cc + 1) * CCW],
                                      start=True, stop=True),
                             reads=["b_sb", "sel"], writes=[pb],
                             tag=f"b_rb{tc}{oc}{sl}{cc}")
                        k.op("ACT", lambda e, sl=sl, cc=cc, pp=pp: e.copy(
                            brep[:, sl * TCH + cc * CCW: sl * TCH + (cc + 1) * CCW],
                            pp[:, 0:CCW]), reads=[pb], writes=["brep"])
                        k.op("PE", lambda e, s_g=s_g, cc=cc, pp=pp:
                             e.matmul(pp[0:128, 0:CCW],
                                      sel_s[:, s_g * 128:(s_g + 1) * 128],
                                      c_sb[:, cc * CCW:(cc + 1) * CCW],
                                      start=True, stop=True),
                             reads=["c_sb", "sel"], writes=[pb],
                             tag=f"b_rc{tc}{oc}{sl}{cc}")
                        k.op("ACT", lambda e, sl=sl, cc=cc, pp=pp: e.copy(
                            crep[:, sl * TCH + cc * CCW: sl * TCH + (cc + 1) * CCW],
                            pp[:, 0:CCW]), reads=[pb], writes=["crep"])
                for gg in range(GH):
                    for sl in range(SOCT):
                        s_g = oc * SOCT + sl
                        k.op("ACT", lambda e, gg=gg, sl=sl, s_g=s_g: e.activation(
                            dAc[:, sl * TCH:(sl + 1) * TCH], dtg[:, gg, :],
                            AF.Exp, scale=A_s[:, gg, s_g:s_g + 1]),
                            reads=["dtg", "A_s"], writes=["dAc"],
                            tag=f"b_da{tc}{oc}{gg}{sl}")
                    vb = ug[:, gg, :].unsqueeze(1).broadcast_to([128, SOCT, TCH])
                    k.op("DVE", lambda e, vb=vb: e.tensor_mul(
                        Xc.rearrange("p (s t) -> p s t", s=SOCT), vb,
                        brep.rearrange("p (s t) -> p s t", s=SOCT)),
                        reads=["ug", "brep"], writes=["Xc"], tag=f"b_X{tc}{oc}{gg}")
                    dv = dAc.rearrange("p (s t) -> p s t", s=SOCT)[:, :, 0]
                    xv = Xc.rearrange("p (s t) -> p s t", s=SOCT)[:, :, 0]
                    cs_ = carry[:, gg, oc * SOCT:(oc + 1) * SOCT]
                    k.op("DVE", lambda e, dv=dv, cs_=cs_: e.tensor_mul(
                        cfix, dv, cs_), reads=["dAc", "carry"], writes=["cfix"],
                        tag=f"b_cf{tc}{oc}{gg}")
                    k.op("DVE", lambda e, xv=xv: e.tensor_add(xv, xv, cfix),
                         reads=["cfix"], writes=["Xc"])
                    k.op("DVE", lambda e, dv=dv: e.memset(dv, 0.0),
                         reads=["cfix"], writes=["dAc"], tag=f"b_z{tc}{oc}{gg}")
                    k.op("DVE", lambda e: e.tensor_tensor_scan(
                        Xc, dAc, Xc, 0.0, AL.mult, AL.add),
                        reads=["dAc"], writes=["Xc"], tag=f"b_sc{tc}{oc}{gg}")
                    hv = Xc.rearrange("p (s t) -> p s t", s=SOCT)[:, :, TCH - 1]
                    k.op("DVE", lambda e, hv=hv, cs_=cs_: e.tensor_copy(cs_, hv),
                         reads=["Xc"], writes=["carry"], tag=f"b_cs{tc}{oc}{gg}")
                    k.op("DVE", lambda e: e.tensor_mul(dAc, Xc, crep),
                         reads=["Xc", "crep"], writes=["dAc"],
                         tag=f"b_cm{tc}{oc}{gg}")
                    red_in = dAc.rearrange("p (s t) -> p t s", s=SOCT)
                    if oc == 0:
                        k.op("DVE", lambda e, gg=gg, red_in=red_in: e.tensor_reduce(
                            yacc[:, gg, :].unsqueeze(2), red_in, AX.X, AL.add),
                            reads=["dAc"], writes=["yacc"], tag=f"b_rd{tc}{oc}{gg}")
                    else:
                        k.op("DVE", lambda e, red_in=red_in: e.tensor_reduce(
                            ytm.unsqueeze(2), red_in, AX.X, AL.add),
                            reads=["dAc"], writes=["ytm"], tag=f"b_rd{tc}{oc}{gg}")
                        k.op("DVE", lambda e, gg=gg: e.tensor_add(
                            yacc[:, gg, :], yacc[:, gg, :], ytm),
                            reads=["ytm"], writes=["yacc"], tag=f"b_ra{tc}{oc}{gg}")
            k.dma("QSP", lambda e, t0=t0: e.dma_start(
                out=y_d[:, t0:t0 + TCH].rearrange("(g p) t -> p g t", p=128),
                in_=yacc), reads=["yacc"], writes=["y_d"], tag=f"b_ys{tc}")

    def phase_c(l, last):
        for mm in range(NM + 1):
            cur = mm % 2
            prv = 1 - cur
            if mm < NM:
                k.dma("QSP", lambda e, mm=mm: e.dma_start(
                    out=ymac, in_=y_d[:, mm * TM:(mm + 1) * TM].rearrange(
                        "(g p) t -> p g t", p=128)),
                    reads=["y_d"], writes=["ymac"], tag=f"c_y{mm}")
                k.dma("QSP", lambda e, mm=mm: e.dma_start(
                    out=uc_sb, in_=u_d[:, mm * TM:(mm + 1) * TM].rearrange(
                        "(g p) t -> p g t", p=128)),
                    reads=["u_d"], writes=["uc_sb"], tag=f"c_u{mm}")
                k.dma("QSP", lambda e, mm=mm: e.dma_start(
                    out=szc_sb, in_=sz_d[:, mm * TM:(mm + 1) * TM].rearrange(
                        "(g p) t -> p g t", p=128)),
                    reads=["sz_d"], writes=["szc_sb"], tag=f"c_s{mm}")
                k.dma("QSP", lambda e, mm=mm, cur=cur: e.dma_start(
                    out=xcur[:, :, cur, :],
                    in_=xs_d[:, mm * TM:(mm + 1) * TM].rearrange(
                        "(mt p) t -> p mt t", p=128)),
                    reads=["xs_d"], writes=["xcur"], tag=f"c_x{mm}")
                for gg in range(GH):
                    k.op("DVE", lambda e, gg=gg: e.scalar_tensor_tensor(
                        ymac[:, gg, :], uc_sb[:, gg, :], Dp_s[:, gg:gg + 1],
                        ymac[:, gg, :], AL.mult, AL.add),
                        reads=["uc_sb", "dp", "ymac"], writes=["ymac"],
                        tag=f"c_g{mm}_{gg}")
                k.op("DVE", lambda e: e.tensor_mul(ymac, ymac, szc_sb),
                     reads=["szc_sb"], writes=["ymac"])
                k.op("DVE", lambda e: e.tensor_copy(ybf, ymac),
                     reads=["ymac"], writes=["ybf"])
                for mt in range(MT):
                    pb = ["ps2", "ps3"][mt % 2]
                    pp = PS[2 + mt % 2]
                    def wo_mm(e, mt=mt, pp=pp):
                        last_i = None
                        for gg in range(GH):
                            last_i = e.matmul(
                                pp[:, 0:TM], WoT[:, gg, mt * 128:(mt + 1) * 128],
                                ybf[:, gg, :], start=(gg == 0),
                                stop=(gg == GH - 1))
                        return last_i
                    k.op("PE", wo_mm, reads=["WoT", "ybf"], writes=[pb],
                         tag=f"c_wo{mm}_{mt}")
                    k.op("ACT", lambda e, mt=mt, cur=cur, pp=pp: e.copy(
                        womine[:, mt, cur, :], pp[:, 0:TM]),
                        reads=[pb], writes=["wom"])
                msg = c.NL * NM + l * NM + mm
                for mt in range(MT):
                    k.dma("QIND", lambda e, mm=mm, cur=cur, l=l, mt=mt:
                          e.indirect_dma_start(
                              shp_w,
                              IndirectOffsetOnAxis(
                                  ap=wsend_s[:, (l * NM + mm) * MT + mt:
                                             (l * NM + mm) * MT + mt + 1],
                                  axis=0),
                              womine[:, mt, cur, :], None,
                              bounds_check=bnd(2 * c.NL * NM * c.M),
                              oob_is_err=False),
                          reads=["wom", "wsend"], writes=["wsent"],
                          tag=f"c_sw{mm}_{mt}")
                k.dma("QIND", send_flag(msg), reads=["wsent", "nonce2", "fsend"],
                      writes=["wflag"], tag=f"c_sf{mm}")
            if mm >= 1:
                pm = mm - 1
                pcur = pm % 2
                msg = c.NL * NM + l * NM + pm
                k.op("SP", poll_fn(msg), reads=["wflag"], writes=["wpolled"],
                     tag=f"c_pl{pm}")
                k.dma("QSP", lambda e, pm=pm, l=l: e.dma_start(
                    out=wopart,
                    in_=shp_w[(l * NM + pm) * c.M:(l * NM + pm + 1) * c.M,
                              :].rearrange("(mt p) t -> p mt t", p=128)),
                    reads=["wpolled", "wsent"], writes=["wop"], tag=f"c_lp{pm}")
                k.dma("QSP", lambda e, pm=pm, l=l: e.dma_start(
                    out=wop2,
                    in_=shp_w[c.NL * NM * c.M + (l * NM + pm) * c.M:
                              c.NL * NM * c.M + (l * NM + pm + 1) * c.M,
                              :].rearrange("(mt p) t -> p mt t", p=128)),
                    reads=["wpolled", "wsent"], writes=["wop2"], tag=f"c_l2{pm}")
                k.op("DVE", lambda e, pcur=pcur: e.tensor_add(
                    xcur[:, :, pcur, :], xcur[:, :, pcur, :], wopart),
                    reads=["wop", "xcur"], writes=["xcur"], tag=f"c_a1{pm}")
                k.op("DVE", lambda e, pcur=pcur: e.tensor_add(
                    xcur[:, :, pcur, :], xcur[:, :, pcur, :], wop2),
                    reads=["wop2"], writes=["xcur"], tag=f"c_a2{pm}")
                k.dma("QSP", lambda e, pm=pm, pcur=pcur: e.dma_start(
                    out=xs_d[:, pm * TM:(pm + 1) * TM].rearrange(
                        "(mt p) t -> p mt t", p=128),
                    in_=xcur[:, :, pcur, :]),
                    reads=["xcur"], writes=["xs_d"], tag=f"c_xs{pm}")

    w1bf_d, w2bf_d, b1col_d, b2col_d = (g["w1bf_d"], g["w2bf_d"],
                                        g["b1col_d"], g["b2col_d"])
    pw1_i, pb1_i, pw2_i, pb2_i = g["w1_i"], g["b1_i"], g["w2_i"], g["b2_i"]
    b1c_p, b2c_p = g["b1c"], g["b2c"]
    wld_f, wld_b = g["wld_f"], g["wld_b"]
    PSW = min(1024, max(c.M, c.Dh, 512))
    CW1 = min(PSW, c.F)
    CW2 = min(PSW, c.M)
    pst = []

    def _slab(src, dst, w, outname):
        def fn():
            k.dma("QACT", lambda e: e.dma_start(out=wld_f[:, 0:w], in_=src),
                  writes=["stgf"], tag="psI")
            k.op("DVE", lambda e: e.tensor_copy(wld_b[:, 0:w], wld_f[:, 0:w]),
                 reads=["stgf"], writes=["stgb"], tag="psC")
            k.dma("QPOOL", lambda e: e.dma_start(out=dst, in_=wld_b[:, 0:w]),
                  reads=["stgb"], writes=[outname], tag="psO")
        return fn

    for ee in range(c.E):
        for mt in range(MT):
            for fo in range(0, c.F, CW1):
                pst.append(_slab(pw1_i[ee, mt * 128:(mt + 1) * 128,
                                       fo:fo + CW1],
                                 w1bf_d[ee, mt * 128:(mt + 1) * 128,
                                        fo:fo + CW1], CW1, "w1bf"))
        for fc in range(FC):
            for mo in range(0, c.M, CW2):
                pst.append(_slab(pw2_i[ee, fc * 128:(fc + 1) * 128,
                                       mo:mo + CW2],
                                 w2bf_d[ee, fc * 128:(fc + 1) * 128,
                                        mo:mo + CW2], CW2, "w2bf"))

        def _bias(ee=ee):
            load_col(b1c_p, pb1_i[ee], FC, f"psb1{ee}", "b1c")
            k.dma("QPOOL", lambda e, ee=ee: e.dma_start(
                out=b1col_d[ee], in_=b1c_p),
                reads=["b1c"], writes=["b1col"], tag=f"psb1o{ee}")
            load_col(b2c_p, pb2_i[ee], MT, f"psb2{ee}", "b2c")
            k.dma("QPOOL", lambda e, ee=ee: e.dma_start(
                out=b2col_d[ee], in_=b2c_p),
                reads=["b2c"], writes=["b2col"], tag=f"psb2o{ee}")
        pst.append(_bias)
    for fn in pst:
        fn()

    k.dma("QSP", lambda e: e.dma_start(out=sel_s, in_=g["selmat"]),
          writes=["sel"], tag="selload")
    for l in range(c.NL):
        k.join()
        load_layer_weights(l)
        k.join()
        phase_a(l)
        k.join()
        phase_b(l)
        k.join()
        phase_c(l, last=(l == c.NL - 1))
        if g["c"].dbg and l == 0:
            k.join()
            for nm, src in [("dbg_u", u_d), ("dbg_dt", dt_d), ("dbg_sz", sz_d),
                            ("dbg_y", y_d)]:
                if nm in dbg:
                    for rr in range(Dh // 128):
                        k.dma("QACT", lambda e, nm=nm, src=src, rr=rr:
                              e.dma_start(out=dbg[nm][rr * 128:(rr + 1) * 128, :],
                                          in_=src[rr * 128:(rr + 1) * 128, :]),
                              writes=[f"d{nm}"], tag=f"dbg{nm}{rr}")
            bcv = dbg.get("dbg_BC")
            if bcv is not None:
                k.dma("QACT", lambda e: e.dma_start(out=bcv, in_=bc_d),
                      reads=["bc_d"], writes=["dbc1"], tag="dbgBC")
        if g["c"].dbg and l == 0 and "dbg_x1" in dbg:
            k.join()
            k.dma("QACT", lambda e: e.dma_start(out=dbg["dbg_x1"], in_=xs_d),
                  reads=["xs_d"], writes=["dx1"], tag="dbgx1")

    (w1e, w2e, xg, xgsq, xnb, xnT, h1b, g1col, keptcol, tloc, scat_i,
     tok_i, tokf, eo_sb, eot, gsc1, gsc2, gsc3, b1c, b2c, zero_t) = (
        g["w1e"], g["w2e"], g["xg"], g["xgsq"], g["xnb"],
        g["xnT"], g["h1b"], g["g1col"], g["keptcol"], g["tloc"], g["scat_i"],
        g["tok_i"], g["tokf"], g["eo_sb"], g["eot"], g["gsc1"], g["gsc2"],
        g["gsc3"], g["b1c"], g["b2c"], g["zero_t"])
    (x_full, w1_i, b1_i, w2_i, b2_i, g1c_d, keptc_d, rankc_d, idxc_d,
     cnts4_d, toklist_d, gbuf_d, iota_r, out_own, tok0b) = (
        g["x_full"], g["w1_i"], g["b1_i"], g["w2_i"], g["b2_i"], g["g1c_d"],
        g["keptc_d"], g["rankc_d"], g["idxc_d"], g["cnts4_d"], g["toklist_d"],
        g["gbuf_d"], g["iota_r"], g["out_own"], g["tok0b"])
    (k128_s, trie_s, ones8_s, iota_cs, chi_sb, ch_sb, cnt8, colsc) = (
        g["k128_s"], g["trie_s"], g["ones8_s"], g["iota_cs"], g["chi_sb"],
        g["ch_sb"], g["cnt8"], g["colsc"])
    iotaNCH_c = g["iotaNCH_i"]; iotaNCH_s = g["iotaNCH_s"]
    fs_sb, fsT, fx, fg = g["fs_sb"], g["fsT"], g["fx"], g["fg"]
    fsl2 = g["fsl2"]
    jfcol = g["jfcol"]
    E = c.E
    TOK_OWN = c.TOK_OWN

    k.join()
    k.op("DVE", lambda e: e.memset(zero_t, 0.0), writes=["zero_t"],
         tag="m_z0")
    k.dma("QSP", lambda e: e.dma_start(out=iotaNCH_s, in_=iotaNCH_c),
          writes=["ioN"], tag="m_ion")
    crow = g["csrow"]
    k.dma("QSP", lambda e: e.dma_start(
        out=crow, in_=cnts4_d[:, ds(dyn[("SP", "quart")], 1)].rearrange(
            "e o -> o e"), allow_slow_non_contiguous=True),
        reads=["cnts4_d"], writes=["csrow"], tag="m_cnt")
    k.op("PE", lambda e: e.matmul(PS[0][0:8, 0:1], crow, onesrow[0:1, 0:1],
                                  start=True, stop=True),
         reads=["csrow", "onesrow"], writes=["ps0"], tag="m_cc")
    cntcol = g["iotacol8"]
    cntc = g["cntc"]
    k.op("ACT", lambda e: e.copy(cntc, PS[0][0:8, 0:1]), reads=["ps0"],
         writes=["cntc"])
    k.op("PE", lambda e: e.matmul(PS[1][0:8, 0:NCH + 1], onesrow[0:1, 0:8],
                                  k128_s, start=True, stop=True),
         reads=["onesrow", "k128"], writes=["ps1"], tag="m_k1")
    k.op("DVE", lambda e: e.tensor_scalar(
        cnt8[0:8, :], PS[1][0:8, 0:NCH + 1], cntc[0:8, 0:1], 0.0,
        AL.is_lt, AL.add), reads=["ps1", "cntc"], writes=["cnt8"], tag="m_cmp")
    k.op("DVE", lambda e: e.tensor_reduce(
        cntc.unsqueeze(2), cnt8[0:8, :].unsqueeze(1), AX.X, AL.add),
        reads=["cnt8"], writes=["cntc"], tag="m_nch")
    k.op("PE", lambda e: e.matmul(PS[0][0:8, 0:1], trie_s, cntc[0:8, 0:1],
                                  start=True, stop=True),
         reads=["trie", "cntc"], writes=["ps0"], tag="m_ccum")
    ccumc = g["ccumc"]
    k.op("ACT", lambda e: e.copy(ccumc, PS[0][0:8, 0:1]), reads=["ps0"],
         writes=["ccumc"])
    k.op("PE", lambda e: e.transpose(PS[1][0:1, 0:8], ccumc, ident_fs[0:8, 0:8]),
         reads=["ccumc", "ident"], writes=["ps1"], tag="m_ccT")
    ccrow = g["ccrow"]
    k.op("ACT", lambda e: e.copy(ccrow, PS[1][0:1, 0:8]), reads=["ps1"],
         writes=["ccrow"])
    k.op("PE", lambda e: e.matmul(PS[0][0:NCH, 0:8], onesrow[0:1, 0:NCH], ccrow,
                                  start=True, stop=True),
         reads=["onesrow", "ccrow"], writes=["ps0"], tag="m_ccb")
    ccrep = g["ccrep"]
    k.op("ACT", lambda e: e.copy(ccrep, PS[0][0:NCH, 0:8]), reads=["ps0"],
         writes=["ccrep"])
    cmpc = g["cmpc"]
    k.op("DVE", lambda e: e.tensor_scalar(
        cmpc, ccrep, iotaNCH_s, 0.0, AL.is_le, AL.add),
        reads=["ccrep", "ioN"], writes=["cmpc"], tag="m_cmp2")
    k.op("DVE", lambda e: e.tensor_reduce(
        ch_sb[:, 0:1].unsqueeze(2), cmpc.unsqueeze(1), AX.X, AL.add),
        reads=["cmpc"], writes=["chsb"], tag="m_ej")
    k.op("DVE", lambda e: e.tensor_scalar(
        ch_sb[:, 0:1], ch_sb[:, 0:1], 1.0, -1.0, AL.mult, AL.add),
        reads=["chsb"], writes=["chsb"])
    k.op("DVE", lambda e: e.tensor_mul(cmpc, cmpc, ccrep),
         reads=["ccrep"], writes=["cmpc"])
    k.op("DVE", lambda e: e.tensor_reduce(
        ch_sb[:, 1:2].unsqueeze(2), cmpc.unsqueeze(1), AX.X, AL.max),
        reads=["cmpc"], writes=["chsb"], tag="m_cca")
    k.op("DVE", lambda e: e.tensor_sub(ch_sb[:, 1:2], iotaNCH_s,
        ch_sb[:, 1:2]), reads=["ioN"], writes=["chsb"])
    k.op("DVE", lambda e: e.tensor_scalar(
        ch_sb[:, 2:3], ch_sb[:, 1:2], 128.0, 0.0, AL.mult, AL.add),
        reads=["chsb"], writes=["chsb"], tag="m_sr1")
    k.op("DVE", lambda e: e.scalar_tensor_tensor(
        ch_sb[:, 1:2], ch_sb[:, 0:1], float(TOK_OWN), ch_sb[:, 2:3],
        AL.mult, AL.add), reads=["chsb"], writes=["chsb"], tag="m_sr2")
    k.op("DVE", lambda e: e.tensor_scalar(
        ch_sb[:, 3:4], ch_sb[:, 0:1], 1.0, 1.0, AL.mult, AL.add),
        reads=["chsb"], writes=["chsb"], tag="m_ej1")
    k.op("DVE", lambda e: e.tensor_copy(chi_sb[:, 0:1], ch_sb[:, 1:2]),
         reads=["chsb"], writes=["chi"], tag="m_chi0")
    k.op("DVE", lambda e: e.tensor_copy(chi_sb[:, 1:2], ch_sb[:, 3:4]),
         reads=["chsb"], writes=["chi"], tag="m_chi1")
    k.op("DVE", lambda e: e.tensor_scalar(
        ch_sb[:, 2:3], ch_sb[:, 3:4], 1.0, float(E), AL.mult, AL.add),
        reads=["chsb"], writes=["chsb"], tag="m_ej9")
    k.op("DVE", lambda e: e.tensor_copy(chi_sb[:, 2:3], ch_sb[:, 2:3]),
         reads=["chsb"], writes=["chi"], tag="m_chi2")
    k.op("PE", lambda e: e.transpose(PS[1][0:1, 0:NCH], ch_sb[:, 1:2],
                                     ident_fs[0:NCH, 0:NCH]),
         reads=["chsb", "ident"], writes=["ps1"], tag="m_srT")
    srrow = g["sbuf"]("srrow", [1, NCH], F32)
    k.op("ACT", lambda e: e.copy(srrow, PS[1][0:1, 0:NCH]),
         reads=["ps1"], writes=["srrow"], tag="m_srr")
    k.op("PE", lambda e: e.matmul(PS[0][:, 0:NCH], onesrow, srrow,
                                  start=True, stop=True),
         reads=["onesrow", "srrow"], writes=["ps0"], tag="m_srb")
    srf = g["sbuf"]("srf", [128, NCH], F32)
    k.op("DVE", lambda e: e.tensor_scalar(
        srf, PS[0][:, 0:NCH], iota_cs[:, 0:1], 0.0, AL.add, AL.add),
        reads=["ps0", "iotac"], writes=["srf"], tag="m_sradd")
    k.op("DVE", lambda e: e.tensor_copy(srtab, srf),
         reads=["srf"], writes=["srtab"], tag="m_srci")
    if c.dbg and "dbg_cht" in dbg:
        k.dma("QACT", lambda e: e.dma_start(out=dbg["dbg_cht"], in_=ch_sb[:, 0:2]),
              reads=["chsb"], writes=["dcht"], tag="m_dcht")
    cci = g["cci"]
    k.op("DVE", lambda e: e.tensor_copy(cci, ccumc), reads=["ccumc"],
         writes=["cci"], tag="m_cci")
    k.op("DVE", lambda e: e.tensor_scalar(
        colsc[0:8, 0:1], ccumc, 1.0, float(NCH), AL.mult, AL.add),
        reads=["ccumc"], writes=["cc2f"], tag="m_cc2f")
    k.op("DVE", lambda e: e.tensor_copy(g["cci2"], colsc[0:8, 0:1]),
         reads=["cc2f"], writes=["cci2"], tag="m_cci2")

    bigi = g["bigi"]
    k.op("DVE", lambda e: e.memset(bigi, 1 << 22), writes=["bigi"], tag="m_bi")
    nrow_tl = E * TOK_OWN
    assert nrow_tl % 512 == 0
    tl_prow = nrow_tl // 512
    for bb in range(0, tl_prow, 64):
        rows = min(64, tl_prow - bb)
        k.dma("QSP", lambda e, bb=bb, rows=rows: e.dma_start(
            out=toklist_d.rearrange("(a b) o -> a (b o)", b=512)[
                bb:bb + rows, :],
            in_=bigi[0:rows, :]), reads=["bigi"], writes=["tokl"],
            tag=f"m_ti{bb}")
    ZW = min(512, c.M)
    for bb in range(c.TOK_OWN // 128):
        for ms in range(c.M // ZW):
            k.dma("QSP", lambda e, bb=bb, ms=ms: e.dma_start(
                out=gbuf_d[bb * 128:(bb + 1) * 128, ms * ZW:(ms + 1) * ZW],
                in_=zero_t[:, 0:ZW]),
                reads=["zero_t"], writes=["gbuf"], tag=f"m_gz{bb}_{ms}")
    for i in range(NOWN):
        k.dma("QIND", lambda e, i=i: e.indirect_dma_start(
            g1col, None, g1c_d,
            IndirectOffsetOnAxis(ap=fidx_s[:, i:i + 1], axis=0),
            bounds_check=bnd(NTOK), oob_is_err=False),
            reads=["g1c_d", "fidx"], writes=["g1col"], tag=f"m_p1{i}")
        k.dma("QIND", lambda e, i=i: e.indirect_dma_start(
            keptcol, None, keptc_d,
            IndirectOffsetOnAxis(ap=fidx_s[:, i:i + 1], axis=0),
            bounds_check=bnd(NTOK), oob_is_err=False),
            reads=["keptc_d", "fidx"], writes=["keptcol"], tag=f"m_p2{i}")
        k.dma("QIND", lambda e, i=i: e.indirect_dma_start(
            tloc, None, rankc_d,
            IndirectOffsetOnAxis(ap=fidx_s[:, i:i + 1], axis=0),
            bounds_check=bnd(NTOK), oob_is_err=False),
            reads=["rankc_d", "fidx"], writes=["tloc"], tag=f"m_p3{i}")
        k.dma("QIND", lambda e, i=i: e.indirect_dma_start(
            tokf, None, idxc_d,
            IndirectOffsetOnAxis(ap=fidx_s[:, i:i + 1], axis=0),
            bounds_check=bnd(NTOK), oob_is_err=False),
            reads=["idxc_d", "fidx"], writes=["tokf"], tag=f"m_p4{i}")
        k.op("DVE", lambda e: e.scalar_tensor_tensor(
            g1col if False else tloc, tokf, float(TOK_OWN), tloc,
            AL.mult, AL.add), reads=["tokf", "tloc"], writes=["tloc"],
            tag=f"m_lp{i}")
        k.op("DVE", lambda e: e.tensor_scalar(
            tloc, tloc, 1.0, -1.0, AL.mult, AL.add),
            reads=["tloc"], writes=["tloc"])
        k.op("DVE", lambda e: e.scalar_tensor_tensor(
            gsc1[:, 0:1], keptcol, -BIG, tloc, AL.mult, AL.add),
            reads=["keptcol", "tloc"], writes=["gsc1c"], tag=f"m_ko{i}")
        k.op("DVE", lambda e: e.tensor_scalar(
            gsc1[:, 0:1], gsc1[:, 0:1], 1.0, BIG, AL.mult, AL.add),
            reads=["gsc1c"], writes=["gsc1c"])
        k.op("DVE", lambda e: e.tensor_copy(scat_i, gsc1[:, 0:1]),
             reads=["gsc1c"], writes=["scat_i"])
        k.op("DVE", lambda e, i=i: e.tensor_copy(tok_i, fidx_s[:, i:i + 1]),
             reads=["fidx"], writes=["tok_i"])
        k.dma("QIND", lambda e, i=i: e.indirect_dma_start(
            toklist_d, IndirectOffsetOnAxis(ap=scat_i, axis=0),
            tok_i, None, bounds_check=bnd(nrow_tl), oob_is_err=False),
            reads=["scat_i", "tok_i", "tokl"], writes=["tokl"],
            tag=f"m_sc{i}")

    h1_d = g["h1_d"]
    cci2 = g["cci2"]

    def stash_pe(j, nm, col, lo, hi):
        def fn(e):
            dyn[("PE", f"{nm}{j}")] = nc.values_load(
                chi_sb[j:j + 1, col:col + 1], engines=[ET.PE], min_val=lo,
                max_val=hi, skip_runtime_bounds_check=True)
            return None
        return fn

    def free_val(e, key):
        v = dyn.pop(key, None)
        if v is not None:
            try:
                for h in v.val.handles:
                    e.free_register(h)
            except Exception:
                pass

    def stash_ccum_sp(nm, src, lo, hi):
        def fn(e):
            for ee in range(E):
                if nm == "cc2":
                    free_val(e, ("SP", f"ccum{ee}"))
                dyn[("SP", f"{nm}{ee}")] = nc.values_load(
                    src[ee:ee + 1, 0:1], engines=[ET.SP], min_val=lo,
                    max_val=hi, skip_runtime_bounds_check=True)
            return None
        return fn

    def chunk_pre(j, sweep):
        k.op("DVE", lambda e: e.memset(tok_i, 1 << 22), writes=["tok_i"],
             tag=f"m{sweep}_tz{j}")
        k.dma("QIND", lambda e, j=j: e.indirect_dma_start(
            tok_i, None, toklist_d,
            IndirectOffsetOnAxis(ap=srtab[:, j:j + 1], axis=0),
            bounds_check=bnd(nrow_tl), oob_is_err=False),
            reads=["tokl", "srtab"], writes=["tok_i"], tag=f"m{sweep}_ix{j}")

    k.op("SP", stash_ccum_sp("ccum", cci, 0, NCH), reads=["cci"],
         tag="m_stcc")
    for ee in range(E):
        def wl1(e, ee=ee):
            e.wait_ge(sem_ch, dyn[("SP", f"ccum{ee}")])
            return e.dma_start(
                out=w1e, in_=g["w1bf_d"][ee].rearrange(
                    "(mt p) f -> p mt f", p=128))
        k.dma("QSP", wl1, reads=["w1bf"], writes=["w1e"], tag=f"m_w1{ee}")
        k.dma("QSP", lambda e, ee=ee: e.dma_start(
            out=b1c, in_=g["b1col_d"][ee]),
            reads=["b1col"], writes=["b1c"], tag=f"m_b1{ee}")
        k.op("SP", lambda e: (e.nop().then_inc(sem_w, 1), None)[1],
             reads=["w1e", "b1c"], tag=f"m_wd{ee}")

    for j in range(NCH):
        k.op("PE", stash_pe(j, "ej1", 1, 1, E), reads=["chi"],
             tag=f"m_stp{j}")
        chunk_pre(j, 1)
        k.dma("QIND", lambda e: e.indirect_dma_start(
            xg, None, x_full, IndirectOffsetOnAxis(ap=tok_i, axis=0),
            bounds_check=bnd(NTOK), oob_is_err=False),
            reads=["tok_i", "x_full"], writes=["xg"], tag=f"m_gx{j}")
        k.op("DVE", lambda e: e.tensor_reduce(
            tloc.unsqueeze(2), xg.unsqueeze(1), AX.X, AL.add),
            reads=["xg"], writes=["tloc"], tag=f"m_ln1{j}")
        k.op("ACT", lambda e: e.activation(xgsq, xg, AF.Square),
             reads=["xg"], writes=["xgsq"])
        k.op("DVE", lambda e: e.tensor_reduce(
            gsc2[:, 1:2].unsqueeze(2), xgsq.unsqueeze(1), AX.X, AL.add),
            reads=["xgsq"], writes=["sx"], tag=f"m_ln2{j}")
        k.op("DVE", lambda e: e.tensor_scalar(
            tloc, tloc, 1.0 / c.M, 0.0, AL.mult, AL.add),
            reads=["tloc"], writes=["tloc"])
        k.op("DVE", lambda e: e.tensor_scalar(
            gsc2[:, 1:2], gsc2[:, 1:2], 1.0 / c.M, 0.0, AL.mult, AL.add),
            reads=["sx"], writes=["sx"])
        k.op("DVE", lambda e: e.scalar_tensor_tensor(
            gsc3[:, 0:1], tloc, 1.0, tloc, AL.mult, AL.mult),
            reads=["tloc"], writes=["sc3"])
        k.op("DVE", lambda e: e.tensor_sub(gsc2[:, 1:2], gsc2[:, 1:2],
                                           gsc3[:, 0:1]),
             reads=["sc3"], writes=["sx"])
        k.op("ACT", lambda e: e.activation(gsc2[:, 1:2], gsc2[:, 1:2], AF.Ln,
                                           bias=float(c.EPS)),
             reads=["sx"], writes=["sx"])
        k.op("ACT", lambda e: e.activation(gsc2[:, 1:2], gsc2[:, 1:2], AF.Exp,
                                           scale=-0.5),
             reads=["sx"], writes=["sx"])
        k.op("DVE", lambda e: e.tensor_scalar(
            xg, xg, tloc, 0.0, AL.subtract, AL.add),
            reads=["xg", "tloc"], writes=["xg"], tag=f"m_ln3{j}")
        k.op("DVE", lambda e: e.tensor_scalar(
            xnb, xg, gsc2[:, 1:2], 0.0, AL.mult, AL.add),
            reads=["xg", "sx"], writes=["xnb"], tag=f"m_ln4{j}")
        for mt in range(MT):
            pb = ["pbf0", "pbf1"][mt % 2]
            pp = PBF[mt % 2]
            k.op("PE", lambda e, mt=mt, pp=pp: e.transpose(
                pp[:, 0:128], xnb[:, mt * 128:(mt + 1) * 128], ident_bs),
                reads=["xnb", "identb"], writes=[pb], tag=f"m_tx{j}_{mt}")
            k.op("DVE", lambda e, mt=mt, pp=pp: e.tensor_copy(
                xnT[:, mt, :], pp[:, 0:128]), reads=[pb], writes=["xnT"])
        for fb in range(FB):
            def m1(e, fb=fb, j=j):
                last = None
                if fb == 0:
                    e.wait_ge(sem_w, dyn[("PE", f"ej1{j}")])
                for sub in range(SUBS):
                    fcol = (fb * SUBS + sub) * 128
                    for mt in range(MT):
                        last = e.matmul(
                            PS[2][:, sub * 128:(sub + 1) * 128],
                            w1e[:, mt, fcol:fcol + 128], xnT[:, mt, :],
                            start=(mt == 0), stop=(mt == MT - 1))
                if fb == FB - 1:
                    free_val(e, ("PE", f"ej1{j}"))
                return last
            k.op("PE", m1, reads=["xnT"], writes=["ps2"],
                 tag=f"m_m1{j}_{fb}")
            for sub in range(SUBS):
                fc = fb * SUBS + sub
                k.op("ACT", lambda e, sub=sub, fc=fc: e.activation(
                    gsc1[:, sub * 128:(sub + 1) * 128],
                    PS[2][:, sub * 128:(sub + 1) * 128], AF.Identity,
                    bias=b1c[:, fc:fc + 1]),
                    reads=["ps2"], writes=["gsc1"], tag=f"m_g0{j}{fb}{sub}")
            k.op("ACT", lambda e: e.activation(
                gsc2[:, 0:SUBS * 128], gsc1[:, 0:SUBS * 128], AF.Square),
                 reads=["gsc1"], writes=["gsc2"])
            k.op("DVE", lambda e: e.tensor_scalar(
                gsc2[:, 0:SUBS * 128], gsc2[:, 0:SUBS * 128], 0.044715, 1.0,
                AL.mult, AL.add),
                reads=["gsc2"], writes=["gsc2"])
            k.op("DVE", lambda e: e.tensor_mul(
                gsc3[:, 0:SUBS * 128], gsc1[:, 0:SUBS * 128],
                gsc2[:, 0:SUBS * 128]),
                 reads=["gsc1", "gsc2"], writes=["gsc3"])
            k.op("ACT", lambda e: e.activation(
                gsc2[:, 0:SUBS * 128], gsc3[:, 0:SUBS * 128], AF.Tanh,
                scale=0.7978845608028654),
                reads=["gsc3"], writes=["gsc2"])
            k.op("DVE", lambda e: e.scalar_tensor_tensor(
                gsc3[:, 0:SUBS * 128], gsc2[:, 0:SUBS * 128], 1.0,
                gsc1[:, 0:SUBS * 128], AL.add, AL.mult),
                reads=["gsc1", "gsc2"], writes=["gsc3"])
            k.op("DVE", lambda e, fb=fb: e.tensor_scalar(
                h1b[:, fb * SUBS:(fb + 1) * SUBS, :].rearrange(
                    "p a b -> p (a b)"),
                gsc3[:, 0:SUBS * 128], 0.5, 0.0, AL.mult, AL.add),
                reads=["gsc3"], writes=["h1b"], tag=f"m_hb{j}{fb}")
        k.dma("QACT", lambda e, j=j: e.dma_start(
            out=h1_d[j * 128:(j + 1) * 128, :],
            in_=h1b.rearrange("p a b -> p (a b)")),
            reads=["h1b"], writes=["h1_d"], tag=f"m_hs{j}")
        k.op("POOL", lambda e: (e.nop().then_inc(sem_ch, 1), None)[1],
             reads=["h1_d"], tag=f"m_cd{j}")

    k.op("SP", stash_ccum_sp("cc2", cci2, NCH, 2 * NCH), reads=["cci2"],
         tag="m_stcc2")
    for ee in range(E):
        def wl2(e, ee=ee):
            e.wait_ge(sem_ch, dyn[("SP", f"cc2{ee}")])
            return e.dma_start(
                out=w2e, in_=g["w2bf_d"][ee].rearrange(
                    "(fc p) m -> p fc m", p=128))
        k.dma("QSP", wl2, reads=["w2bf"], writes=["w2e"], tag=f"m_w2{ee}")
        k.dma("QSP", lambda e, ee=ee: e.dma_start(
            out=b2c, in_=g["b2col_d"][ee]),
            reads=["b2col"], writes=["b2c"], tag=f"m_b2{ee}")
        k.op("SP", lambda e: (e.nop().then_inc(sem_w, 1), None)[1],
             reads=["w2e", "b2c"], tag=f"m_we{ee}")

    for j in range(NCH):
        k.op("PE", stash_pe(j, "ej9", 2, E + 1, 2 * E), reads=["chi"],
             tag=f"m_st2p{j}")
        chunk_pre(j, 2)
        k.dma("QIND", lambda e: e.indirect_dma_start(
            g1col, None, g1c_d, IndirectOffsetOnAxis(ap=tok_i, axis=0),
            bounds_check=bnd(NTOK), oob_is_err=False),
            reads=["tok_i", "g1c_d"], writes=["g1col"], tag=f"m_gg{j}")
        k.dma("QIND", lambda e: e.indirect_dma_start(
            keptcol, None, keptc_d, IndirectOffsetOnAxis(ap=tok_i, axis=0),
            bounds_check=bnd(NTOK), oob_is_err=False),
            reads=["tok_i", "keptc_d"], writes=["keptcol"], tag=f"m_gk{j}")
        k.dma("QACT", lambda e, j=j: e.dma_start(
            out=h1b.rearrange("p a b -> p (a b)"),
            in_=h1_d[j * 128:(j + 1) * 128, :]),
            reads=["h1_d"], writes=["h1b"], tag=f"m_hl{j}")
        k.op("DVE", lambda e: e.tensor_copy(tokf, tok_i),
             reads=["tok_i"], writes=["tokf"], tag=f"m_tv{j}")
        k.op("DVE", lambda e: e.tensor_scalar(
            gsc2[:, 0:1], tokf, float(NTOK) - 0.5, 0.0, AL.is_lt, AL.add),
            reads=["tokf"], writes=["valid"], tag=f"m_vd{j}")
        k.op("DVE", lambda e: e.tensor_sub(tloc, tokf, tok0b),
             reads=["tokf", "tok0b"], writes=["tloc"])
        k.op("DVE", lambda e: e.scalar_tensor_tensor(
            gsc3[:, 0:1], gsc2[:, 0:1], -BIG, tloc, AL.mult, AL.add),
            reads=["valid", "tloc"], writes=["sc3"], tag=f"m_s1{j}")
        k.op("DVE", lambda e: e.tensor_scalar(
            gsc3[:, 0:1], gsc3[:, 0:1], 1.0, BIG, AL.mult, AL.add),
            reads=["sc3"], writes=["sc3"])
        k.op("DVE", lambda e: e.scalar_tensor_tensor(
            gsc1[:, 0:1], keptcol, -BIG, gsc3[:, 0:1], AL.mult, AL.add),
            reads=["keptcol", "sc3"], writes=["gsc1c"], tag=f"m_s2{j}")
        k.op("DVE", lambda e: e.tensor_scalar(
            gsc1[:, 0:1], gsc1[:, 0:1], 1.0, BIG, AL.mult, AL.add),
            reads=["gsc1c"], writes=["gsc1c"])
        k.op("DVE", lambda e: e.tensor_copy(scat_i, gsc1[:, 0:1]),
             reads=["gsc1c"], writes=["scat_i"], tag=f"m_s3{j}")
        for msub in range(MT):
            pp = PS[4] if msub < 4 else PS[5]
            pb = "ps4" if msub < 4 else "ps5"
            colo = (msub % 4) * 128
            def m2(e, msub=msub, pp=pp, colo=colo, j=j):
                last = None
                if msub == 0:
                    e.wait_ge(sem_w, dyn[("PE", f"ej9{j}")])
                for fc in range(FC):
                    last = e.matmul(
                        pp[:, colo:colo + 128],
                        w2e[:, fc, msub * 128:(msub + 1) * 128], h1b[:, fc, :],
                        start=(fc == 0), stop=(fc == FC - 1))
                if msub == MT - 1:
                    free_val(e, ("PE", f"ej9{j}"))
                return last
            k.op("PE", m2, reads=["h1b"], writes=[pb],
                 tag=f"m_m2{j}_{msub}")
            k.op("ACT", lambda e, msub=msub, pp=pp, colo=colo: e.activation(
                eo_sb[:, msub, :], pp[:, colo:colo + 128], AF.Identity,
                bias=b2c[:, msub:msub + 1]),
                reads=[pb], writes=["eo_sb"])
        for msub in range(MT):
            pb2 = ["ps0", "ps1"][msub % 2]
            pp2 = PS[msub % 2]
            k.op("PE", lambda e, msub=msub, pp2=pp2: e.transpose(
                pp2[:, 0:128], eo_sb[:, msub, :], ident_fs),
                reads=["eo_sb", "ident"], writes=[pb2], tag=f"m_te{j}_{msub}")
            k.op("ACT", lambda e, msub=msub, pp2=pp2: e.copy(
                eot[:, msub * 128:(msub + 1) * 128], pp2[:, 0:128]),
                reads=[pb2], writes=["eot"])
        k.op("DVE", lambda e: e.tensor_scalar(
            eot, eot, g1col, 0.0, AL.mult, AL.add),
            reads=["eot", "g1col"], writes=["eot"], tag=f"m_gt{j}")
        k.op("ACT", lambda e: e.activation(eot, eot, AF.Silu),
             reads=["eot"], writes=["eot"], tag=f"m_si{j}")
        k.dma("QIND", lambda e: e.indirect_dma_start(
            gbuf_d, IndirectOffsetOnAxis(ap=scat_i, axis=0),
            eot, None, bounds_check=bnd(TOK_OWN), oob_is_err=False),
            reads=["scat_i", "eot", "gbuf"], writes=["gbuf"], tag=f"m_sg{j}")
        k.op("POOL", lambda e: (e.nop().then_inc(sem_ch, 1), None)[1],
             reads=["gbuf"], tag=f"m_cd2{j}")

    if c.dbg and "dbg_g" in dbg:
        k.join()
        for rr in range(TOK_OWN // 128):
            k.dma("QACT", lambda e, rr=rr: e.dma_start(
                out=dbg["dbg_g"][rr * 128:(rr + 1) * 128, :],
                in_=gbuf_d[rr * 128:(rr + 1) * 128, :]),
                writes=["dbgg"], tag=f"m_dg{rr}")

    k.join()
    for i in range(NOWN):
        k.dma("QSP", lambda e, i=i: e.dma_start(
            out=fs_sb, in_=g["xs_d"][:, i * 128:(i + 1) * 128].rearrange(
                "(mt p) t -> p mt t", p=128)),
            reads=["xs_d"], writes=["fs_sb"], tag=f"f_s{i}")
        k.dma("QSP", lambda e, i=i: e.dma_start(
            out=g["fg"] if False else fsl2,
            in_=g["xs_d"][:, c.TOK_OWN + i * 128:
                          c.TOK_OWN + (i + 1) * 128].rearrange(
                "(mt p) t -> p mt t", p=128)),
            reads=["xs_d"], writes=["fsl2"], tag=f"f_s2{i}")
        k.op("DVE", lambda e: e.tensor_sub(
            fsl2.rearrange("p a b -> p (a b)"),
            fsl2.rearrange("p a b -> p (a b)"),
            fs_sb.rearrange("p a b -> p (a b)")),
            reads=["fs_sb", "fsl2"], writes=["fsl2"], tag=f"f_d{i}")
        k.op("DVE", lambda e: e.scalar_tensor_tensor(
            fs_sb.rearrange("p a b -> p (a b)"),
            fsl2.rearrange("p a b -> p (a b)"), jfcol[:, 0:1],
            fs_sb.rearrange("p a b -> p (a b)"), AL.mult, AL.add),
            reads=["fsl2", "jfcol", "fs_sb"], writes=["fs_sb"], tag=f"f_e{i}")
        k.dma("QIND", lambda e, i=i: e.indirect_dma_start(
            fx, None, x_full, IndirectOffsetOnAxis(ap=fidx_s[:, i:i + 1],
                                                   axis=0),
            bounds_check=bnd(NTOK), oob_is_err=False),
            reads=["x_full", "fidx"], writes=["fx"], tag=f"f_x{i}")
        k.dma("QSP", lambda e, i=i: e.dma_start(
            out=fg, in_=gbuf_d[i * 128:(i + 1) * 128, :]),
            reads=["gbuf"], writes=["fg"], tag=f"f_g{i}")
        for mt in range(MT):
            pb = ["ps0", "ps1"][mt % 2]
            pp = PS[mt % 2]
            k.op("PE", lambda e, mt=mt, pp=pp: e.transpose(
                pp[:, 0:128], fs_sb[:, mt, :], ident_fs),
                reads=["fs_sb", "ident"], writes=[pb], tag=f"f_t{i}_{mt}")
            k.op("ACT", lambda e, mt=mt, pp=pp: e.copy(
                fsT[:, mt * 128:(mt + 1) * 128], pp[:, 0:128]),
                reads=[pb], writes=["fsT"])
        k.op("DVE", lambda e: e.tensor_mul(fsT, fsT, fg),
             reads=["fsT", "fg"], writes=["fsT"], tag=f"f_m{i}")
        k.op("DVE", lambda e: e.tensor_add(fsT, fsT, fx),
             reads=["fx"], writes=["fsT"])
        k.dma("QSP", lambda e, i=i: e.dma_start(
            out=out_own[i * 128:(i + 1) * 128, :], in_=fsT),
            reads=["fsT"], writes=["out"], tag=f"f_o{i}")
    k.join()

"""Host-side input layout for the island kernel (pure slicing/constants)."""


def core_inputs(c: Cfg, fi, core_id, nonce):
    b = (core_id // 2) % 2
    j = core_id % 2
    Dh = c.Dh
    H = slice(j * Dh, (j + 1) * Dh)
    Hz = slice(c.D + j * Dh, c.D + (j + 1) * Dh)
    NTOK = c.NB * c.T
    NCH = c.NCHUNK
    f32 = np.float32

    Wi = fi["L0_attn_W_in"]
    Wi_sl = np.concatenate([Wi[:, H, :], Wi[:, Hz, :]], axis=1)
    own_tok0 = b * c.T + j * c.TOK_OWN
    metaf = np.zeros((1, 16), f32)
    metaf[0, 0] = own_tok0
    metai = np.zeros((1, 16), np.int32)
    metai[0, 0] = own_tok0
    metai[0, 1] = j
    metai[0, 2] = own_tok0 // c.TOK_OWN
    metai[0, 3] = b * c.T
    metai[0, 4] = 1 - j
    NM = c.T // c.TMAC
    NMSG = c.NL * NM * 2 + 8
    MT = c.M // 128

    selmat = np.zeros((16, c.S * 128), f32)
    for s in range(c.S):
        selmat[s, s * 128:(s + 1) * 128] = 1.0

    return {
        "x_full": np.ascontiguousarray(fi["L0_attn_x"].reshape(NTOK, c.M)).astype(f32),
        "Wi_sl": np.ascontiguousarray(Wi_sl).astype(f32),
        "convw": np.ascontiguousarray(fi["L0_attn_conv_w"][:, H, :]).astype(f32),
        "convb": np.ascontiguousarray(fi["L0_attn_conv_b"][:, H]).astype(f32),
        "Wx_sl": np.ascontiguousarray(fi["L0_attn_W_x"][:, :, H]).astype(f32),
        "Wdt_sl": np.ascontiguousarray(fi["L0_attn_W_dt"][:, H, :]).astype(f32),
        "bdt_sl": np.ascontiguousarray(fi["L0_attn_b_dt"][:, H]).astype(f32),
        "Alog_sl": np.ascontiguousarray(fi["L0_attn_A_log"][:, H, :]).astype(f32),
        "Dp_sl": np.ascontiguousarray(fi["L0_attn_Dp"][:, H]).astype(f32),
        "Wo_sl": np.ascontiguousarray(fi["L0_attn_W_out"][:, :, H]).astype(f32),
        "lnw_i": np.ascontiguousarray(fi["L0_attn_ln_w"]).astype(f32),
        "lnb_i": np.ascontiguousarray(fi["L0_attn_ln_b"]).astype(f32),
        "wg_i": np.ascontiguousarray(fi["L0_ffn_wg"]).astype(f32),
        "w1_i": np.ascontiguousarray(fi["L0_ffn_w1"]).astype(f32),
        "b1_i": np.ascontiguousarray(fi["L0_ffn_b1"]).astype(f32),
        "w2_i": np.ascontiguousarray(fi["L0_ffn_w2"]).astype(f32),
        "b2_i": np.ascontiguousarray(fi["L0_ffn_b2"]).astype(f32),
        "ident_f": np.eye(128, dtype=f32),
        "ident_b": np.eye(128).astype(ml_dtypes.bfloat16),
        "iotaNCH_i": np.arange(NCH, dtype=f32).reshape(NCH, 1),
        "selmat": selmat,
        "iota_r": np.arange(NTOK, dtype=f32).reshape(1, NTOK),
        "iota_c": np.arange(128, dtype=f32).reshape(128, 1),
        "iota8_r": np.arange(8, dtype=f32).reshape(1, 8),
        "k128_r": (np.arange(NCH + 1, dtype=f32) * 128).reshape(1, NCH + 1),
        "trie_i": (np.arange(8)[:, None] < np.arange(8)[None, :]).astype(f32),
        "ones8": np.ones((1, 8), f32),
        "metaf_i": metaf,
        "xsend_i": (np.arange(128)[:, None]
                    + (j * c.NL * NM + np.arange(c.NL * NM)[None, :]) * 128
                    ).astype(np.int32),
        "wsend_i": (np.arange(128)[:, None] + j * c.NL * NM * c.M
                    + (np.repeat(np.arange(c.NL * NM), MT) * c.M
                       + np.tile(np.arange(MT), c.NL * NM) * 128)[None, :]
                    ).astype(np.int32),
        "fsend_i": np.vstack([j * NMSG + np.arange(NMSG),
                              np.full(NMSG, 1 << 22)]).astype(np.int32),
        "fidx_i": (own_tok0 + np.arange(128)[:, None]
                   + 128 * np.arange(c.TOK_OWN // 128)[None, :]).astype(np.int32),
        "selmsk_i": _selmsk(b, j),
        "metai_i": metai,
        "nonce_i": _nonce_row(nonce, j),
    }


def _selmsk(b, j):
    row = np.zeros((1, 16), np.float32)
    row[0, 0] = b
    row[0, 1] = j
    return row


def _nonce_row(nonce, j):
    row = np.zeros((1, 16), np.uint32)
    row[0, 0] = nonce[j]
    row[0, 1] = nonce[1 - j]
    return row


def small_cfg(dbg=True):
    return Cfg(M=128, D=256, S=16, R=16, T=256, NB=2, NL=2, DCONV=4,
               TMAC=64, TCH=64, SOCT=4, E=8, F=256, MCH=128, MOE_SLACK=7,
               dbg=dbg)


def small_inputs(c: Cfg, seed=0):
    rng = np.random.default_rng(seed)
    f32 = np.float32
    NTOK = c.NB * c.T
    sc = 0.05
    fi = {
        "L0_attn_x": rng.standard_normal((c.NB, c.T, c.M)).astype(f32),
        "L0_attn_W_in": (rng.standard_normal((c.NL, 2 * c.D, c.M)) * sc).astype(f32),
        "L0_attn_conv_w": (rng.standard_normal((c.NL, c.D, c.DCONV)) * 0.3).astype(f32),
        "L0_attn_conv_b": (rng.standard_normal((c.NL, c.D)) * 0.2).astype(f32),
        "L0_attn_W_x": (rng.standard_normal((c.NL, c.R + 2 * c.S, c.D)) * sc).astype(f32),
        "L0_attn_W_dt": (rng.standard_normal((c.NL, c.D, c.R)) * sc).astype(f32),
        "L0_attn_b_dt": (rng.standard_normal((c.NL, c.D)) * 1.0 - 2.0).astype(f32),
        "L0_attn_A_log": (rng.standard_normal((c.NL, c.D, c.S)) * 0.5).astype(f32),
        "L0_attn_Dp": (1.0 + 0.2 * rng.standard_normal((c.NL, c.D))).astype(f32),
        "L0_attn_W_out": (rng.standard_normal((c.NL, c.M, c.D)) * sc).astype(f32),
        "L0_attn_ln_w": (1.0 + 0.2 * rng.standard_normal((c.NL, c.M))).astype(f32),
        "L0_attn_ln_b": (0.2 * rng.standard_normal((c.NL, c.M))).astype(f32),
        "L0_ffn_wg": (rng.standard_normal((c.M, c.E)) / np.sqrt(c.M)).astype(f32),
        "L0_ffn_w1": (rng.standard_normal((c.E, c.M, c.F)) / np.sqrt(c.M)).astype(f32),
        "L0_ffn_b1": (0.1 * rng.standard_normal((c.E, c.F))).astype(f32),
        "L0_ffn_w2": (rng.standard_normal((c.E, c.F, c.M)) / np.sqrt(c.F)).astype(f32),
        "L0_ffn_b2": (0.1 * rng.standard_normal((c.E, c.M))).astype(f32),
    }
    return fi

import os as _os
import time as _time
import hashlib as _hashlib
import shutil as _shutil
import pathlib as _pathlib
from concourse.bass_utils import run_bass_kernel_spmd
import concourse.bass_utils as _bu
from concourse import bass2jax as _b2j

_NEFF_CACHE = _pathlib.Path("/var/tmp/bass-neff-cache")
_orig_compile_bir = _bu.compile_bir_kernel


def _cached_compile_bir(bir_json, tmpdir, neff_name="file.neff"):
    key = None
    try:
        _NEFF_CACHE.mkdir(parents=True, exist_ok=True)
        raw = bir_json if isinstance(bir_json, bytes) else bir_json.encode()
        key = _hashlib.sha256(raw).hexdigest()[:32]
        hit = _NEFF_CACHE / (key + ".neff")
        if hit.exists():
            dst = _pathlib.Path(tmpdir) / neff_name
            _shutil.copyfile(hit, dst)
            return str(dst)
    except Exception:
        pass
    out = _orig_compile_bir(bir_json, tmpdir, neff_name=neff_name)
    if key is not None:
        try:
            tmp = _NEFF_CACHE / f".{key}.{_os.getpid()}.tmp"
            _shutil.copyfile(out, tmp)
            tmp.replace(_NEFF_CACHE / (key + ".neff"))
        except Exception:
            pass
    return out


_bu.compile_bir_kernel = _cached_compile_bir
_b2j.compile_bir_kernel = _cached_compile_bir

_BUILD_CACHE = {}


def _get_nc():
    if "nc" not in _BUILD_CACHE:
        c = Cfg()
        env = build_core(c, debug=False, detect_races=False)
        _BUILD_CACHE["nc"] = env["nc"]
        _BUILD_CACHE["c"] = c
    return _BUILD_CACHE["c"], _BUILD_CACHE["nc"]


def kernel(**inputs):
    c, nc = _get_nc()
    seed = (_os.getpid() << 16) ^ int(_time.time_ns() & 0xFFFFFFFF)
    rng = np.random.default_rng(seed)
    while True:
        nonce = rng.integers(1, 2**32 - 1, size=2, dtype=np.uint32)
        if nonce[0] != nonce[1]:
            break
    in_maps = [core_inputs(c, inputs, cid, nonce) for cid in range(8)]
    res = run_bass_kernel_spmd(nc, in_maps, core_ids=list(range(8)))
    NTOK = c.NB * c.T
    out = np.empty((NTOK, c.M), np.float32)
    for cid in range(4):
        b = (cid // 2) % 2
        j = cid % 2
        own0 = b * c.T + j * c.TOK_OWN
        out[own0:own0 + c.TOK_OWN] = res.results[cid]["out_own"]
    return out.reshape(c.NB, c.T, c.M)
